# version 05
# baseline (speedup 1.0000x reference)
import contextlib
import numpy as np

import concourse.bass as bass
import concourse.mybir as mybir
from concourse.bass_utils import run_bass_kernel_spmd

F32 = mybir.dt.float32
BF16 = mybir.dt.bfloat16
AF = mybir.ActivationFunctionType
ALU = mybir.AluOpType
AX = mybir.AxisListType

NCORES = 8
SEQ = 2048
BLOC = 2
T = SEQ * BLOC
D = 1024
H = 4096
NST = 16
DCONV = 4
LN_EPS = 1e-5
G = D // 128
HT = H // 128
TC = 512
NCH = T // TC
NSUB = TC // 128
RCONST = float(3 << 22)
DEBUG_TAPS = False


SEMKEYS = ("sy", "gp", "te", "ve", "sc", "params", "x", "st",
           "gs0", "gs1", "os0", "os1", "w10", "w11", "w20", "w21",
           "wp0", "wp1", "scr0", "scr1")


class Ledger:
    def __init__(self, prev_marks):
        self.cnt = {k: 0 for k in SEMKEYS}
        self.marks = {}
        self.prev = prev_marks

    def tick(self, sems, key, inst, k=1):
        inst.then_inc(sems[key], k)
        self.cnt[key] += k

    def mark(self, name, key):
        assert name not in self.marks, name
        self.marks[name] = (key, self.cnt[key])

    def wait(self, sems, eng, name):
        if self.prev is None:
            return
        key, val = self.prev[name]
        if val > 0:
            eng.wait_ge(sems[key], val)


def emit(nc, led):
    L = led

    dx = nc.dram_tensor("x", [T, D], F32, kind="ExternalInput").ap()
    dconv_w = nc.dram_tensor("conv_w", [D, DCONV], F32, kind="ExternalInput").ap()
    dconv_b = nc.dram_tensor("conv_b", [D], F32, kind="ExternalInput").ap()
    dA = nc.dram_tensor("A", [D, NST], F32, kind="ExternalInput").ap()
    dBpw = nc.dram_tensor("Bp_w", [NST, D], F32, kind="ExternalInput").ap()
    dBpb = nc.dram_tensor("Bp_b", [NST], F32, kind="ExternalInput").ap()
    dCpw = nc.dram_tensor("Cp_w", [NST, D], F32, kind="ExternalInput").ap()
    dCpb = nc.dram_tensor("Cp_b", [NST], F32, kind="ExternalInput").ap()
    dD = nc.dram_tensor("Dv", [D], F32, kind="ExternalInput").ap()
    dgw = nc.dram_tensor("gate_w", [D, D], F32, kind="ExternalInput").ap()
    dgb = nc.dram_tensor("gate_b", [D], F32, kind="ExternalInput").ap()
    dow = nc.dram_tensor("out_w", [D, D], F32, kind="ExternalInput").ap()
    dob = nc.dram_tensor("out_b", [D], F32, kind="ExternalInput").ap()
    dlng = nc.dram_tensor("ln_g", [D], F32, kind="ExternalInput").ap()
    dlnb = nc.dram_tensor("ln_b", [D], F32, kind="ExternalInput").ap()
    dw1 = nc.dram_tensor("w1", [H, D], F32, kind="ExternalInput").ap()
    db1 = nc.dram_tensor("b1", [H], F32, kind="ExternalInput").ap()
    dw2 = nc.dram_tensor("w2", [D, H], F32, kind="ExternalInput").ap()
    db2 = nc.dram_tensor("b2", [D], F32, kind="ExternalInput").ap()
    dout = nc.dram_tensor("out", [T, D], F32, kind="ExternalOutput").ap()
    scr_w1 = nc.dram_tensor("scr_w1", [HT, 128, 1024], BF16, kind="Internal").ap()
    scr_w2 = nc.dram_tensor("scr_w2", [G, 128, 4096], BF16, kind="Internal").ap()
    scr_g = nc.dram_tensor("scr_g", [G, 128, 1024], BF16, kind="Internal").ap()
    scr_o = nc.dram_tensor("scr_o", [G, 128, 1024], BF16, kind="Internal").ap()
    dbg = {}
    if DEBUG_TAPS:
        for nm, shape, dt in (
                ("xn16", [128, G * (3 + TC)], BF16),
                ("u16d", [128, G * TC], BF16),
                ("btd", [NST, 2 * TC], BF16),
                ("BCd", [128, 2 * NST * TC], BF16),
                ("xqd", [128, G * TC], BF16),
                ("hqd", [128, HT * TC], BF16),
                ("gad", [128, G * TC], BF16),
                ("zresd", [128, G * TC], F32),
                ("carryd", [128, G * NST], F32),
                ("acold", [128, G * NST], F32),
                ("t1d", [128, 2 * TC], F32),
                ("statd", [128, NSUB * 8], F32),
                ("xtmd", [128, NSUB * D], F32),
                ("xntmd", [128, NSUB * D], F32),
                ("scrw1d", [128, 1024], BF16),
                ("scrw2d", [128, 4096], BF16)):
            dbg[nm] = nc.dram_tensor("dbg_" + nm, shape, dt,
                                     kind="ExternalOutput").ap()

    ctx = contextlib.ExitStack()
    sb = lambda name, shape, dt=F32: ctx.enter_context(
        nc.sbuf_tensor(name, shape, dt))
    ps = lambda name: ctx.enter_context(nc.psum_tensor(name, [128, 512], F32))

    with ctx:
        s_BpCpT = sb("BpCpT", [128, 2 * G * NST], BF16)
        s_acol = sb("acol", [128, G * NST])
        s_diagD = sb("diagD", [128, G * 128], BF16)
        s_convw = sb("convw", [128, G * DCONV])
        s_cols = sb("cols", [128, 5 * G])
        OF_CONVB, OF_GATEB, OF_OUTB, OF_LNG, OF_LNB2 = (
            0, G, 2 * G, 3 * G, 4 * G)
        s_b1qC = sb("b1qC", [128, HT])
        s_bpb = sb("bpb", [NST, 2])
        s_id32 = sb("id32", [128, 128])
        s_id16 = sb("id16", [128, 128], BF16)
        s_sel = sb("sel", [NST, NST * 128], BF16)
        s_ones1 = sb("ones1", [1, 128])
        s_negC = sb("negC", [128, 1])
        s_ones16 = sb("ones16c", [128, 1], BF16)
        s_carry = sb("carry", [128, G * NST])
        s_w1s = sb("w1s", [128, 2 * 1024], BF16)
        s_w2s = sb("w2s", [128, 2 * 4096], BF16)
        s_gs = sb("gs", [128, 2 * 1024], BF16)
        s_os = sb("osb", [128, 2 * 1024], BF16)
        s_xtm = sb("xtm", [128, NSUB * D])
        s_xntm = sb("xntm", [128, NSUB * D])
        s_stat = sb("stat", [128, NSUB * 8])
        s_xqtm = sb("xqtm", [128, D])
        s_xq16 = sb("xq16", [128, 2 * D], BF16)
        s_xn16 = sb("xn16", [128, G * (3 + TC)], BF16)
        s_xq = sb("xq", [128, G * TC], BF16)
        s_u16 = sb("u16", [128, G * TC], BF16)
        s_cv = sb("cv", [128, TC])
        s_bt = sb("bt", [NST, 2 * TC], BF16)
        s_BC = sb("BC", [128, 2 * NST * TC], BF16)
        s_w = sb("wbuf", [128, TC])
        s_s = sb("sbuf_s", [128, TC])
        s_z = sb("zbuf", [128, 2 * TC], BF16)
        s_ga = sb("ga", [128, G * TC], BF16)
        s_zres = sb("zres", [128, G * TC])
        s_z16 = sb("z16", [128, 4 * TC], BF16)
        s_hq = sb("hq", [128, HT * TC], BF16)
        s_t1 = sb("t1", [128, 2 * TC])
        pTA, pTB = ps("pTA"), ps("pTB")
        pMA, pMB = ps("pMA"), ps("pMB")
        pBCa, pBCb = ps("pBCa"), ps("pBCb")
        pYA, pYB = ps("pYA"), ps("pYB")

        sems = {k: ctx.enter_context(nc.semaphore("sem_" + k))
                for k in SEMKEYS}
        block = ctx.enter_context(nc.Block())

        def tick(key, inst, k=1):
            L.tick(sems, key, inst, k)

        def mark(name, key):
            L.mark(name, key)

        def wt(eng, name):
            L.wait(sems, eng, name)

        @block.gpsimd
        def _(gp):
            def ss(inst):
                tick("gp", inst)
                gp.wait_ge(sems["gp"], L.cnt["gp"])

            ss(gp.memset(s_id32[:], 0.0))
            ss(nc.gpsimd.affine_select(
                out=s_id32[:], in_=s_id32[:], compare_op=ALU.not_equal,
                fill=1.0, base=0, pattern=[[-1, 128]], channel_multiplier=1))
            ss(gp.memset(s_id16[:], 0.0))
            ss(nc.gpsimd.affine_select(
                out=s_id16[:], in_=s_id16[:], compare_op=ALU.not_equal,
                fill=1.0, base=0, pattern=[[-1, 128]], channel_multiplier=1))
            ss(gp.memset(s_sel[:], 0.0))
            sel3 = s_sel.rearrange("p (n f) -> p n f", f=128)
            ss(nc.gpsimd.affine_select(
                out=sel3, in_=sel3, compare_op=ALU.not_equal,
                fill=1.0, base=0, pattern=[[-1, NST], [0, 128]],
                channel_multiplier=1))
            gp.memset(s_ones1[:], 1.0)
            gp.memset(s_negC[:], -RCONST)
            gp.memset(s_ones16[:], 1.0)
            gp.memset(s_carry[:], 0.0)
            tick("gp", gp.memset(s_xn16[:], 0.0))
            mark("gp_const", "gp")

        @block.sync
        def _(sy):
            for g in range(G):
                tick("params", sy.dma_start(
                    out=s_convw[:, g * DCONV:(g + 1) * DCONV],
                    in_=dconv_w[g * 128:(g + 1) * 128, :]), 16)
                tick("params", sy.dma_start(
                    out=s_cols[:, OF_CONVB + g:OF_CONVB + g + 1],
                    in_=dconv_b[g * 128:(g + 1) * 128][:, None]), 16)
                tick("params", sy.dma_start(
                    out=s_cols[:, OF_GATEB + g:OF_GATEB + g + 1],
                    in_=dgb[g * 128:(g + 1) * 128][:, None]), 16)
                tick("params", sy.dma_start(
                    out=s_cols[:, OF_OUTB + g:OF_OUTB + g + 1],
                    in_=dob[g * 128:(g + 1) * 128][:, None]), 16)
                tick("params", sy.dma_start(
                    out=s_cols[:, OF_LNG + g:OF_LNG + g + 1],
                    in_=dlng[g * 128:(g + 1) * 128][:, None]), 16)
                tick("params", sy.dma_start(
                    out=s_acol[:, g * NST:(g + 1) * NST],
                    in_=dA[g * 128:(g + 1) * 128, :]), 16)
                tick("params", sy.dma_start(
                    out=s_stat[:, g:g + 1],
                    in_=dD[g * 128:(g + 1) * 128][:, None]), 16)
                tick("params", sy.dma_start(
                    out=s_stat[:, 8 + g:8 + g + 1],
                    in_=dlnb[g * 128:(g + 1) * 128][:, None]), 16)
            tick("params", sy.dma_start(out=s_bpb[:, 0:1], in_=dBpb[:, None]), 16)
            tick("params", sy.dma_start(out=s_bpb[:, 1:2], in_=dCpb[:, None]), 16)
            tick("params", sy.dma_start(
                out=s_zres[0:32, 0:128],
                in_=db1.rearrange("(a b) -> a b", b=128)), 16)
            tick("params", sy.dma_start(
                out=s_zres[0:8, 200:328],
                in_=db2.rearrange("(a b) -> a b", b=128)), 16)
            tick("params", sy.dma_start(
                out=s_zres[0:NST, 400:400 + D], in_=dBpw[:, :]), 16)
            tick("params", sy.dma_start(
                out=s_zres[0:NST, 1500:1500 + D], in_=dCpw[:, :]), 16)
            mark("sy_params", "params")

        @block.scalar
        def _(sc):
            wt(sc, "sy_params")
            tick("sc", sc.activation(s_acol[:], s_acol[:], AF.Exp))
            sc.wait_ge(sems["sc"], L.cnt["sc"])
            tick("sc", sc.activation(s_acol[:], s_acol[:], AF.Exp, scale=-1.0))
            mark("sc_acol", "sc")

        @block.tensor
        def _(te):
            wt(te, "sy_params")
            wt(te, "gp_const")
            te.transpose(pTA[:, 0:32], s_zres[0:32, 0:128],
                         s_id32[0:32, 0:32])
            te.transpose(pTA[:, 32:40], s_zres[0:8, 200:328],
                         s_id32[0:8, 0:8])
            for g in range(G):
                te.transpose(pTB[:, g * 16:(g + 1) * 16],
                             s_zres[0:NST, 400 + g * 128:400 + (g + 1) * 128],
                             s_id32[0:NST, 0:NST])
            for g in range(G):
                _last = te.transpose(pTB[:, 128 + g * 16:128 + (g + 1) * 16],
                             s_zres[0:NST, 1500 + g * 128:1500 + (g + 1) * 128],
                             s_id32[0:NST, 0:NST])
            tick("te", _last)
            mark("te_prep1", "te")

        @block.vector
        def _(ve):
            def vst(inst):
                tick("ve", inst)
                ve.wait_ge(sems["ve"], L.cnt["ve"])

            wt(ve, "te_prep1")
            wt(ve, "sy_params")
            wt(ve, "gp_const")
            for g in range(G):
                ve.tensor_scalar(s_diagD[:, g * 128:(g + 1) * 128],
                                 s_id16[:], s_stat[:, g:g + 1], None, ALU.mult)
            vst(ve.tensor_scalar(s_b1qC[:], pTA[:, 0:HT], 64.0, RCONST,
                                 ALU.mult, ALU.add))
            vst(ve.tensor_scalar(s_b1qC[:], s_b1qC[:], RCONST + 127.0,
                                 RCONST - 128.0, ALU.min, ALU.max))
            vst(ve.tensor_scalar(s_b1qC[:], s_b1qC[:], RCONST, None,
                                 ALU.subtract))
            vst(ve.tensor_scalar(s_b1qC[:], s_b1qC[:], 0.25, None, ALU.mult))
            vst(ve.tensor_scalar(s_cols[:, OF_LNB2:OF_LNB2 + 8],
                                 pTA[:, 32:40], 64.0, RCONST,
                                 ALU.mult, ALU.add))
            vst(ve.tensor_scalar(s_cols[:, OF_LNB2:OF_LNB2 + 8],
                                 s_cols[:, OF_LNB2:OF_LNB2 + 8],
                                 RCONST + 127.0, RCONST - 128.0,
                                 ALU.min, ALU.max))
            vst(ve.tensor_scalar(s_cols[:, OF_LNB2:OF_LNB2 + 8],
                                 s_cols[:, OF_LNB2:OF_LNB2 + 8],
                                 RCONST, None, ALU.subtract))
            vst(ve.scalar_tensor_tensor(
                s_cols[:, OF_LNB2:OF_LNB2 + 8],
                s_cols[:, OF_LNB2:OF_LNB2 + 8], 1.0 / 64.0,
                s_stat[:, 8:16], ALU.mult, ALU.add))
            ve.tensor_copy(s_BpCpT[:, 0:G * NST], pTB[:, 0:G * NST])
            tick("ve", ve.tensor_copy(s_BpCpT[:, G * NST:2 * G * NST],
                                      pTB[:, 128:128 + G * NST]))
            mark("ve_prep1", "ve")

        wjobs = []
        for (src, dst, nrt, rowlen, mode) in (
                (dw1, scr_w1, HT, D, "q64"),
                (dw2, scr_w2, G, H, "q64"),
                (dgw, scr_g, G, D, "cast"),
                (dow, scr_o, G, D, "cast")):
            for j in range(nrt):
                wjobs.append((src, dst, j, rowlen, mode))
        NJ = len(wjobs)

        def wp_in(i, rowlen):
            if rowlen == H:
                return s_zres[:, 0:H]
            return s_xtm[:, (i % 2) * 2048:(i % 2) * 2048 + rowlen]

        def wp_t1(i, rowlen):
            if rowlen == H:
                return s_xntm[:, 0:H]
            return s_xntm[:, (i % 2) * 2048:(i % 2) * 2048 + rowlen]

        @block.sync
        def _(sy):
            for i, (src, dst, j, rowlen, mode) in enumerate(wjobs):
                if rowlen == H:
                    wt(sy, "te_prep1")
                    if i >= 1:
                        wt(sy, f"wp_read_{i-1}")
                if i >= 2:
                    wt(sy, f"wp_read_{i-2}")
                tick(f"wp{i % 2}", sy.dma_start(
                    out=wp_in(i, rowlen),
                    in_=src[j * 128:(j + 1) * 128, :]), 16)
                mark(f"wp_dma_{i}", f"wp{i % 2}")

        @block.scalar
        def _(sc):
            for i, (src, dst, j, rowlen, mode) in enumerate(wjobs):
                par = i % 2
                wt(sc, f"wp_dma_{i}")
                src_sb = wp_in(i, rowlen)
                if mode == "q64":
                    if i >= 1:
                        wt(sc, f"wp_clip_{i-1}")
                    if i >= 2:
                        wt(sc, f"wp_clip_{i-2}")
                    tick("sc", sc.activation(
                        wp_t1(i, rowlen), src_sb,
                        AF.Copy, bias=RCONST, scale=64.0))
                else:
                    if i >= 2:
                        wt(sc, f"wp_te_{i-2}")
                    tick("sc", sc.activation(
                        s_hq[:, par * 4096:par * 4096 + rowlen], src_sb,
                        AF.Copy))
                mark(f"wp_read_{i}", "sc")



        @block.tensor
        def _(te):
            gidx = 0
            for i, (src, dst, j, rowlen, mode) in enumerate(wjobs):
                par = i % 2
                if mode == "q64":
                    wt(te, f"wp_clip_{i}")
                else:
                    wt(te, f"wp_read_{i}")
                if i >= 2:
                    wt(te, f"wp_stage_{i-2}")
                int_sb = s_hq[:, par * 4096:par * 4096 + rowlen]
                nk = rowlen // 128
                for k0 in range(0, nk, 4):
                    pk = pTA if gidx % 2 == 0 else pTB
                    pk16 = pk[:].bitcast(BF16)
                    if gidx < 2:
                        wt(te, "ve_prep1")
                    if gidx >= 2:
                        wt(te, f"wp_ev_{gidx-2}")
                    kn = min(4, nk - k0)
                    for kk in range(kn):
                        _last = te.transpose(pk16[:, kk * 128:(kk + 1) * 128],
                                     int_sb[:, (k0 + kk) * 128:
                                            (k0 + kk + 1) * 128],
                                     s_id16[:])
                    tick("te", _last)
                    mark(f"wp_tp_{gidx}", "te")
                    gidx += 1
                mark(f"wp_te_{i}", "te")

        item_gidx = []
        _g = 0
        for (_s, _d, _j, _rl, _m) in wjobs:
            ng = (_rl // 128 + 3) // 4
            item_gidx.append((_g, ng))
            _g += ng

        @block.vector
        def _(ve):
            def emit_clip(i):
                srcj, dstj, jj, rowlen, mode = wjobs[i]
                par = i % 2
                if mode != "q64":
                    mark(f"wp_clip_{i}", "ve")
                    return
                wt(ve, f"wp_read_{i}")
                if i >= 2:
                    wt(ve, f"wp_te_{i-2}")
                t1 = wp_t1(i, rowlen)
                ve.tensor_scalar(t1, t1, RCONST + 127.0, RCONST - 128.0,
                                 ALU.min, ALU.max)
                tick("ve", ve.tensor_scalar(
                    s_hq[:, par * 4096:par * 4096 + rowlen], t1,
                    RCONST, None, ALU.subtract))
                mark(f"wp_clip_{i}", "ve")

            def emit_evac(i):
                srcj, dstj, jj, rowlen, mode = wjobs[i]
                par = i % 2
                nk = rowlen // 128
                stg = s_w2s if rowlen == H else s_w1s
                stg_off = par * rowlen
                if i >= 2:
                    wt(ve, f"wp_scr_{i-2}")
                g0, ng = item_gidx[i]
                for gg in range(ng):
                    gidx = g0 + gg
                    k0 = gg * 4
                    wt(ve, f"wp_tp_{gidx}")
                    pk16 = (pTA if gidx % 2 == 0 else pTB)[:].bitcast(BF16)
                    kn = min(4, nk - k0)
                    _last = ve.tensor_copy(
                        stg[:, stg_off + k0 * 128:stg_off + (k0 + kn) * 128],
                        pk16[:, 0:kn * 128])
                    tick("ve", _last)
                    mark(f"wp_ev_{gidx}", "ve")
                mark(f"wp_stage_{i}", "ve")

            for i in range(NJ + 1):
                if i < NJ:
                    emit_clip(i)
                if i >= 1:
                    emit_evac(i - 1)

        @block.gpsimd
        def _(gp):
            for i, (src, dst, j, rowlen, mode) in enumerate(wjobs):
                par = i % 2
                wt(gp, f"wp_stage_{i}")
                stg = s_w2s if rowlen == H else s_w1s
                tick(f"scr{i % 2}", gp.dma_start(
                    out=dst[j, :, :],
                    in_=stg[:, par * rowlen:(par + 1) * rowlen]), 16)
                mark(f"wp_scr_{i}", f"scr{i % 2}")

        for k in SEMKEYS:
            mark(f"prep_end_{k}", k)

        for c in range(NCH):
            seq_start = (c % (NCH // BLOC) == 0)
            t0 = c * TC
            P = f"c{c}_"
            prev = f"c{c-1}_" if c > 0 else "prep_end_"

            def barrier(eng, names=("gp", "te", "ve", "sc", "st")):
                for k in names:
                    wt(eng, prev + k if c > 0 else f"prep_end_{k}")
                if c == 0:
                    wt(eng, "prep_end_scr0")
                    wt(eng, "prep_end_scr1")

            @block.sync
            def _(sy):
                barrier(sy)
                for q in range(NSUB):
                    tick("x", sy.dma_start(
                        out=s_xtm[:, q * D:(q + 1) * D],
                        in_=dx[t0 + q * 128:t0 + (q + 1) * 128, :]), 16)
                mark(P + "x", "x")
                for o in range(G):
                    if o >= 2:
                        wt(sy, P + f"gate_mm_{o-2}")
                    tick(f"gs{o % 2}", sy.dma_start(
                        out=s_gs[:, (o % 2) * 1024:(o % 2 + 1) * 1024],
                        in_=scr_g[o, :, :]), 16)
                    mark(P + f"gs_{o}", f"gs{o % 2}")
                for o in range(G):
                    if o >= 2:
                        wt(sy, P + f"out_mm_{o-2}")
                    tick(f"os{o % 2}", sy.dma_start(
                        out=s_os[:, (o % 2) * 1024:(o % 2 + 1) * 1024],
                        in_=scr_o[o, :, :]), 16)
                    mark(P + f"os_{o}", f"os{o % 2}")
                for j in range(HT):
                    if j >= 2:
                        wt(sy, P + f"ffn_mm1_{j-2}")
                    tick(f"w1{j % 2}", sy.dma_start(
                        out=s_w1s[:, (j % 2) * 1024:(j % 2 + 1) * 1024],
                        in_=scr_w1[j, :, :]), 16)
                    mark(P + f"w1_{j}", f"w1{j % 2}")
                for o in range(G):
                    if o >= 2:
                        wt(sy, P + f"ffn_mm2_{o-2}")
                    tick(f"w2{o % 2}", sy.dma_start(
                        out=s_w2s[:, (o % 2) * 4096:(o % 2 + 1) * 4096],
                        in_=scr_w2[o, :, :]), 16)
                    mark(P + f"w2_{o}", f"w2{o % 2}")

            @block.vector
            def _(ve):
                barrier(ve)
                wt(ve, P + "x")
                for q in range(NSUB):
                    _last = ve.reduce_sum(s_stat[:, q * 8:q * 8 + 1],
                                  s_xtm[:, q * D:(q + 1) * D], axis=AX.X)
                tick("ve", _last)
                mark(P + "ln1_sum", "ve")

            @block.scalar
            def _(sc):
                barrier(sc)
                wt(sc, P + "x")
                for q in range(NSUB):
                    _last = sc.activation(s_xntm[:, q * D:(q + 1) * D],
                                  s_xtm[:, q * D:(q + 1) * D], AF.Square,
                                  accum_out=s_stat[:, q * 8 + 1:q * 8 + 2])
                tick("sc", _last)
                mark(P + "ln1_sq", "sc")

            st3 = s_stat.rearrange("p (q k) -> p q k", k=8)

            @block.vector
            def _(ve):
                def vst(inst):
                    tick("ve", inst)
                    ve.wait_ge(sems["ve"], L.cnt["ve"])

                wt(ve, P + "ln1_sum")
                wt(ve, P + "ln1_sq")
                ve.tensor_scalar(st3[:, :, 2:3], st3[:, :, 0:1], 1.0 / D,
                                 None, ALU.mult)
                vst(ve.tensor_scalar(st3[:, :, 3:4], st3[:, :, 1:2], 1.0 / D,
                                     None, ALU.mult))
                vst(ve.tensor_tensor(st3[:, :, 4:5], st3[:, :, 2:3],
                                     st3[:, :, 2:3], ALU.mult))
                vst(ve.tensor_tensor(st3[:, :, 5:6], st3[:, :, 3:4],
                                     st3[:, :, 4:5], ALU.subtract))
                tick("ve", ve.tensor_scalar(st3[:, :, 5:6], st3[:, :, 5:6],
                                            LN_EPS, None, ALU.add))
                mark(P + "ln1_var", "ve")

            @block.scalar
            def _(sc):
                wt(sc, P + "ln1_var")
                tick("sc", sc.activation(st3[:, :, 5:6], st3[:, :, 5:6],
                                         AF.Sqrt))
                mark(P + "ln1_sd", "sc")

            @block.vector
            def _(ve):
                def vst(inst):
                    tick("ve", inst)
                    ve.wait_ge(sems["ve"], L.cnt["ve"])

                wt(ve, P + "ln1_sd")
                vst(ve.reciprocal(st3[:, :, 6:7], st3[:, :, 5:6]))
                vst(ve.tensor_tensor(st3[:, :, 7:8], st3[:, :, 2:3],
                                     st3[:, :, 6:7], ALU.mult))
                vst(ve.tensor_scalar(st3[:, :, 7:8], st3[:, :, 7:8],
                                     -1.0, None, ALU.mult))
                tick("ve", ve.nop())
                mark(P + "ln1_r", "ve")

            @block.vector
            def _(ve):
                wt(ve, P + "ln1_r")
                for q in range(NSUB):
                    _last = ve.tensor_scalar(s_xntm[:, q * D:(q + 1) * D],
                                     s_xtm[:, q * D:(q + 1) * D],
                                     st3[:, q, 6:7], st3[:, q, 7:8],
                                     ALU.mult, ALU.add)
                tick("ve", _last)
                mark(P + "ln1_ap", "ve")

            @block.scalar
            def _(sc):
                for q in range(NSUB):
                    if q == 0:
                        wt(sc, P + "ln1_ap")
                    if q >= 1:
                        wt(sc, P + f"xqclip_{q-1}")
                    if q >= 2:
                        wt(sc, P + f"tp_xq_{q-2}_1")
                    tick("sc", sc.activation(
                        s_xqtm[:], s_xntm[:, q * D:(q + 1) * D], AF.Copy,
                        bias=RCONST, scale=16.0))
                    mark(P + f"xq1_{q}", "sc")

            tgroups = []
            for q in range(NSUB):
                for src in ("xn", "xq"):
                    for half in range(2):
                        tgroups.append((q, src, half))

            @block.tensor
            def _(te):
                for gi, (q, src, half) in enumerate(tgroups):
                    pk = pTA if gi % 2 == 0 else pTB
                    if gi >= 2:
                        wt(te, P + f"tpev_{gi-2}")
                    if src == "xn":
                        wt(te, P + "ln1_ap")
                        for gg in range(4):
                            g = half * 4 + gg
                            _last = te.transpose(
                                pk[:, gg * 128:(gg + 1) * 128],
                                s_xntm[:, q * D + g * 128:
                                       q * D + (g + 1) * 128],
                                s_id32[:])
                    else:
                        wt(te, P + f"xqclip_{q}")
                        par = q % 2
                        pk16 = pk[:].bitcast(BF16)
                        for gg in range(4):
                            g = half * 4 + gg
                            _last = te.transpose(
                                pk16[:, gg * 128:(gg + 1) * 128],
                                s_xq16[:, par * D + g * 128:
                                       par * D + (g + 1) * 128],
                                s_id16[:])
                    tick("te", _last)
                    mark(P + f"tp_{src}_{q}_{half}", "te")

            @block.vector
            def _(ve):
                def emit_clip(q):
                    par = q % 2
                    wt(ve, P + f"xq1_{q}")
                    ve.tensor_scalar(s_xqtm[:], s_xqtm[:],
                                     RCONST + 127.0, RCONST - 128.0,
                                     ALU.min, ALU.max)
                    tick("ve", ve.tensor_scalar(
                        s_xq16[:, par * D:(par + 1) * D], s_xqtm[:],
                        RCONST, None, ALU.subtract))
                    mark(P + f"xqclip_{q}", "ve")

                emit_clip(0)
                emit_clip(1)
                for gi, (q, src, half) in enumerate(tgroups):
                    pk = pTA if gi % 2 == 0 else pTB
                    wt(ve, P + f"tp_{src}_{q}_{half}")
                    pk16 = pk[:].bitcast(BF16)
                    for gg in range(4):
                        g = half * 4 + gg
                        if src == "xn":
                            dst = s_xn16[:, g * (3 + TC) + 3 + q * 128:
                                         g * (3 + TC) + 3 + (q + 1) * 128]
                            _last = ve.tensor_copy(
                                dst, pk[:, gg * 128:(gg + 1) * 128])
                        else:
                            dst = s_xq[:, g * TC + q * 128:
                                       g * TC + (q + 1) * 128]
                            _last = ve.tensor_copy(
                                dst, pk16[:, gg * 128:(gg + 1) * 128])
                    tick("ve", _last)
                    mark(P + f"tpev_{gi}", "ve")
                    if src == "xq" and half == 1 and q + 2 < NSUB:
                        emit_clip(q + 2)
                mark(P + "xn_ready", "ve")
                mark(P + "xq_ready", "ve")

            @block.vector
            def _(ve):
                for g in range(G):
                    if g >= 1:
                        wt(ve, P + f"silu_{g-1}")
                    base = g * (3 + TC)
                    ve.tensor_scalar(
                        s_cv[:], s_xn16[:, base + 3:base + 3 + TC],
                        s_convw[:, g * DCONV + DCONV - 1:g * DCONV + DCONV],
                        None, ALU.mult)
                    for k in range(1, DCONV):
                        kk = DCONV - 1 - k
                        _last = ve.scalar_tensor_tensor(
                            s_cv[:], s_xn16[:, base + 3 - k:base + 3 - k + TC],
                            s_convw[:, g * DCONV + kk:g * DCONV + kk + 1],
                            s_cv[:], ALU.mult, ALU.add)
                    tick("ve", _last)
                    mark(P + f"cv_{g}", "ve")

            @block.scalar
            def _(sc):
                for g in range(G):
                    wt(sc, P + f"cv_{g}")
                    tick("sc", sc.activation(
                        s_u16[:, g * TC:(g + 1) * TC], s_cv[:], AF.Silu,
                        bias=s_cols[:, OF_CONVB + g:OF_CONVB + g + 1]))
                    mark(P + f"silu_{g}", "sc")

            @block.tensor
            def _(te):
                for g in range(G):
                    wt(te, P + f"silu_{g}")
                    te.matmul(pYA[0:NST, :],
                              s_BpCpT[:, g * NST:(g + 1) * NST],
                              s_u16[:, g * TC:(g + 1) * TC],
                              start=(g == 0), stop=(g == G - 1))
                for g in range(G):
                    _last = te.matmul(pYB[0:NST, :],
                              s_BpCpT[:, (G + g) * NST:(G + g + 1) * NST],
                              s_u16[:, g * TC:(g + 1) * TC],
                              start=(g == 0), stop=(g == G - 1))
                tick("te", _last)
                mark(P + "btct_mm", "te")

            @block.vector
            def _(ve):
                wt(ve, P + "btct_mm")
                ve.tensor_scalar(s_bt[0:NST, 0:TC], pYA[0:NST, :],
                                 s_bpb[:, 0:1], None, ALU.add)
                tick("ve", ve.tensor_scalar(s_bt[0:NST, TC:2 * TC],
                                            pYB[0:NST, :],
                                            s_bpb[:, 1:2], None, ALU.add))
                mark(P + "btct_ev", "ve")

            bcbank = [pBCa, pBCb, pYA, pYB]

            @block.tensor
            def _(te):
                wt(te, P + "btct_ev")
                for n in range(NST):
                    for which in range(2):
                        bi = (2 * n + which) % 4
                        gi = 2 * n + which
                        if gi >= 4:
                            wt(te, P + f"bcev_{gi-4}")
                        tick("te", te.matmul(
                            bcbank[bi][:],
                            s_sel[:, n * 128:(n + 1) * 128],
                            s_bt[0:NST, which * TC:(which + 1) * TC],
                            start=True, stop=True))
                        mark(P + f"bcmm_{gi}", "te")

            @block.scalar
            def _(sc):
                for n in range(NST):
                    for which in range(2):
                        gi = 2 * n + which
                        bi = gi % 4
                        wt(sc, P + f"bcmm_{gi}")
                        tick("sc", sc.copy(
                            s_BC[:, (which * NST + n) * TC:
                                 (which * NST + n + 1) * TC],
                            bcbank[bi][:]))
                        mark(P + f"bcev_{gi}", "sc")
                mark(P + "bc_ready", "sc")

            @block.tensor
            def _(te):
                for o in range(G):
                    wt(te, P + f"gs_{o}")
                    wt(te, P + "xn_ready")
                    if o >= 2:
                        wt(te, P + f"gate_ev_{o-2}")
                    pk = pMA if o % 2 == 0 else pMB
                    for k in range(G):
                        _last = te.matmul(pk[:],
                                  s_gs[:, (o % 2) * 1024 + k * 128:
                                       (o % 2) * 1024 + (k + 1) * 128],
                                  s_xn16[:, k * (3 + TC) + 3:
                                         k * (3 + TC) + 3 + TC],
                                  start=(k == 0), stop=(k == G - 1))
                    tick("te", _last)
                    mark(P + f"gate_mm_{o}", "te")

            @block.scalar
            def _(sc):
                for o in range(G):
                    pk = pMA if o % 2 == 0 else pMB
                    wt(sc, P + f"gate_mm_{o}")
                    tick("sc", sc.activation(
                        s_ga[:, o * TC:(o + 1) * TC], pk[:], AF.Silu,
                        bias=s_cols[:, OF_GATEB + o:OF_GATEB + o + 1]))
                    mark(P + f"gate_ev_{o}", "sc")

            @block.vector
            def _(ve):
                def emit_yg(g):
                    pk = pYA if g % 2 == 0 else pYB
                    wt(ve, P + f"ymm_{g * NST + NST - 1}")
                    wt(ve, P + f"gate_ev_{g}")
                    tick("ve", ve.tensor_tensor(
                        s_ga[:, g * TC:(g + 1) * TC], pk[:],
                        s_ga[:, g * TC:(g + 1) * TC], ALU.mult))
                    mark(P + f"yg_{g}", "ve")

                wt(ve, P + "bc_ready")
                for g in range(G):
                    for n in range(NST):
                        idx = g * NST + n
                        par = idx % 2
                        if idx >= 2:
                            wt(ve, P + f"ymm_{idx-2}")
                        ve.tensor_tensor(s_w[:],
                                         s_u16[:, g * TC:(g + 1) * TC],
                                         s_BC[:, n * TC:(n + 1) * TC],
                                         ALU.mult)
                        ci = g * NST + n
                        init = (0.0 if seq_start
                                else s_carry[:, ci:ci + 1])
                        ve.tensor_tensor_scan(
                            s_s[:],
                            s_acol[:, ci:ci + 1].broadcast_to([128, TC]),
                            s_w[:], init, ALU.mult, ALU.add)
                        tick("ve", ve.tensor_tensor(
                            s_z[:, par * TC:(par + 1) * TC], s_s[:],
                            s_BC[:, (NST + n) * TC:(NST + n + 1) * TC],
                            ALU.mult))
                        mark(P + f"z_{idx}", "ve")
                        ve.tensor_copy(s_carry[:, ci:ci + 1],
                                       s_s[:, TC - 1:TC])
                    if g >= 1:
                        emit_yg(g - 1)
                emit_yg(G - 1)

            @block.tensor
            def _(te):
                for g in range(G):
                    pk = pYA if g % 2 == 0 else pYB
                    if g < 2:
                        wt(te, P + f"bcev_{NST * 2 - 2 + g}")
                    else:
                        wt(te, P + f"yg_{g-2}")
                    te.matmul(pk[:], s_diagD[:, g * 128:(g + 1) * 128],
                              s_u16[:, g * TC:(g + 1) * TC],
                              start=True, stop=False)
                    for n in range(NST):
                        idx = g * NST + n
                        par = idx % 2
                        wt(te, P + f"z_{idx}")
                        tick("te", te.matmul(
                            pk[:], s_id16[:],
                            s_z[:, par * TC:(par + 1) * TC],
                            start=False, stop=(n == NST - 1)))
                        mark(P + f"ymm_{idx}", "te")


            @block.tensor
            def _(te):
                for o in range(G):
                    wt(te, P + f"os_{o}")
                    if o == 0:
                        for g in range(G):
                            wt(te, P + f"yg_{g}")
                    if o >= 2:
                        wt(te, P + f"res_{o-2}")
                    pk = pMA if o % 2 == 0 else pMB
                    for k in range(G):
                        _last = te.matmul(pk[:],
                                  s_os[:, (o % 2) * 1024 + k * 128:
                                       (o % 2) * 1024 + (k + 1) * 128],
                                  s_ga[:, k * TC:(k + 1) * TC],
                                  start=(k == 0), stop=(k == G - 1))
                    tick("te", _last)
                    mark(P + f"out_mm_{o}", "te")

            @block.vector
            def _(ve):
                for o in range(G):
                    pk = pMA if o % 2 == 0 else pMB
                    wt(ve, P + f"out_mm_{o}")
                    tick("ve", ve.scalar_tensor_tensor(
                        s_zres[:, o * TC:(o + 1) * TC], pk[:],
                        s_cols[:, OF_OUTB + o:OF_OUTB + o + 1],
                        s_xn16[:, o * (3 + TC) + 3:o * (3 + TC) + 3 + TC],
                        ALU.add, ALU.add))
                    mark(P + f"res_{o}", "ve")

            @block.scalar
            def _(sc):
                for g in range(G):
                    par = g % 2
                    wt(sc, P + f"res_{g}")
                    if g >= 2:
                        wt(sc, P + f"ln2mu_{g-2}")
                    _last = sc.copy(s_z16[:, par * TC:(par + 1) * TC],
                                   s_zres[:, g * TC:(g + 1) * TC])
                    tick("sc", _last)
                    mark(P + f"ln2cast_{g}", "sc")
                    if g >= 2:
                        wt(sc, P + f"ln2sq_{g-2}")
                    _last = sc.activation(s_z16[:, (2 + par) * TC:(3 + par) * TC],
                                  s_zres[:, g * TC:(g + 1) * TC], AF.Square)
                    tick("sc", _last)
                    mark(P + f"ln2sqc_{g}", "sc")

            @block.tensor
            def _(te):
                for g in range(G):
                    par = g % 2
                    wt(te, P + f"ln2cast_{g}")
                    tick("te", te.matmul(
                        pBCa[0:1, :], s_ones16[:],
                        s_z16[:, par * TC:(par + 1) * TC],
                        start=(g == 0), stop=(g == G - 1)))
                    mark(P + f"ln2mu_{g}", "te")
                    wt(te, P + f"ln2sqc_{g}")
                    tick("te", te.matmul(
                        pBCb[0:1, :], s_ones16[:],
                        s_z16[:, (2 + par) * TC:(3 + par) * TC],
                        start=(g == 0), stop=(g == G - 1)))
                    mark(P + f"ln2sq_{g}", "te")

            MU = s_cv[0:1, 0:TC]
            VCH = s_t1[0:1, 0:TC]
            MUR = s_t1[0:1, TC:2 * TC]

            @block.vector
            def _(ve):
                def vst(inst):
                    tick("ve", inst)
                    ve.wait_ge(sems["ve"], L.cnt["ve"])

                wt(ve, P + f"ln2sq_{G-1}")
                vst(ve.tensor_scalar(MU, pBCa[0:1, :], 1.0 / D, None,
                                     ALU.mult))
                vst(ve.tensor_scalar(VCH, pBCb[0:1, :], 1.0 / D, None,
                                     ALU.mult))
                vst(ve.tensor_tensor(MUR, MU, MU, ALU.mult))
                vst(ve.tensor_tensor(VCH, VCH, MUR, ALU.subtract))
                tick("ve", ve.tensor_scalar(VCH, VCH, LN_EPS, None, ALU.add))
                mark(P + "ln2_var", "ve")

            @block.scalar
            def _(sc):
                wt(sc, P + "ln2_var")
                tick("sc", sc.activation(VCH, VCH, AF.Sqrt))
                mark(P + "ln2_sd", "sc")

            @block.vector
            def _(ve):
                def vst(inst):
                    tick("ve", inst)
                    ve.wait_ge(sems["ve"], L.cnt["ve"])

                wt(ve, P + "ln2_sd")
                vst(ve.reciprocal(VCH, VCH))
                vst(ve.tensor_tensor(MUR, MU, VCH, ALU.mult))
                tick("ve", ve.nop())
                mark(P + "ln2_r", "ve")

            @block.tensor
            def _(te):
                wt(te, P + "ln2_r")
                te.matmul(pBCa[:], s_ones1[:], VCH, start=True, stop=True)
                tick("te", te.matmul(pBCb[:], s_ones1[:], MUR,
                                     start=True, stop=True))
                mark(P + "ln2_bc", "te")

            @block.vector
            def _(ve):
                wt(ve, P + "ln2_bc")
                for g in range(G):
                    ve.tensor_tensor(s_zres[:, g * TC:(g + 1) * TC],
                                     s_zres[:, g * TC:(g + 1) * TC],
                                     pBCa[:], ALU.mult)
                    _last = ve.tensor_tensor(s_zres[:, g * TC:(g + 1) * TC],
                                     s_zres[:, g * TC:(g + 1) * TC],
                                     pBCb[:], ALU.subtract)
                tick("ve", _last)
                mark(P + "ln2_ap", "ve")

            @block.vector
            def _(ve):
                wt(ve, P + "ln2_ap")
                for g in range(G):
                    _last = ve.tensor_scalar(s_zres[:, g * TC:(g + 1) * TC],
                                     s_zres[:, g * TC:(g + 1) * TC],
                                     s_cols[:, OF_LNG + g:OF_LNG + g + 1],
                                     s_cols[:, OF_LNB2 + g:OF_LNB2 + g + 1],
                                     ALU.mult, ALU.add)
                tick("ve", _last)
                mark(P + "attn_ready", "ve")

            @block.tensor
            def _(te):
                for j in range(HT):
                    wt(te, P + f"w1_{j}")
                    if j == 0:
                        wt(te, P + "xq_ready")
                        wt(te, P + f"res_{G-2}")
                        wt(te, P + f"res_{G-1}")
                    if j >= 2:
                        wt(te, P + f"ffn_h2_{j-2}")
                    pk = pMA if j % 2 == 0 else pMB
                    for k in range(G):
                        _last = te.matmul(pk[:],
                                  s_w1s[:, (j % 2) * 1024 + k * 128:
                                       (j % 2) * 1024 + (k + 1) * 128],
                                  s_xq[:, k * TC:(k + 1) * TC],
                                  start=(k == 0), stop=(k == G - 1))
                    tick("te", _last)
                    mark(P + f"ffn_mm1_{j}", "te")

            @block.vector
            def _(ve):
                for j in range(HT):
                    par = j % 2
                    wt(ve, P + f"ffn_mm1_{j}")
                    if j < 2:
                        wt(ve, P + "ln2_bc")
                    if j >= 2:
                        wt(ve, P + f"ffn_h3_{j-2}")
                    ve.scalar_tensor_tensor(
                        s_t1[:, par * TC:(par + 1) * TC], pMA[:] if par == 0
                        else pMB[:], 1.0 / 64.0,
                        s_b1qC[:, j:j + 1].broadcast_to([128, TC]),
                        ALU.mult, ALU.add)
                    mark(P + f"ffn_h1_{j}", "ve")
                    tick("ve", ve.tensor_scalar(
                        s_t1[:, par * TC:(par + 1) * TC],
                        s_t1[:, par * TC:(par + 1) * TC],
                        RCONST, RCONST + 127.0, ALU.add, ALU.min))
                    mark(P + f"ffn_h2_{j}", "ve")

            @block.scalar
            def _(sc):
                for j in range(HT):
                    par = j % 2
                    wt(sc, P + f"ffn_h2_{j}")
                    tick("sc", sc.activation(
                        s_hq[:, j * TC:(j + 1) * TC],
                        s_t1[:, par * TC:(par + 1) * TC], AF.Relu,
                        bias=s_negC[:]))
                    mark(P + f"ffn_h3_{j}", "sc")

            @block.tensor
            def _(te):
                for o in range(G):
                    wt(te, P + f"w2_{o}")
                    if o == 0:
                        wt(te, P + f"ffn_h2_{HT-2}")
                        wt(te, P + f"ffn_h2_{HT-1}")
                        wt(te, P + f"ffn_h3_{HT-1}")
                    if o >= 2:
                        wt(te, P + f"ffn_fadd_{o-2}")
                    pk = pMA if o % 2 == 0 else pMB
                    for k in range(HT):
                        _last = te.matmul(pk[:],
                                  s_w2s[:, (o % 2) * 4096 + k * 128:
                                       (o % 2) * 4096 + (k + 1) * 128],
                                  s_hq[:, k * TC:(k + 1) * TC],
                                  start=(k == 0), stop=(k == HT - 1))
                    tick("te", _last)
                    mark(P + f"ffn_mm2_{o}", "te")

            @block.vector
            def _(ve):
                for o in range(G):
                    pk = pMA if o % 2 == 0 else pMB
                    wt(ve, P + f"ffn_mm2_{o}")
                    if o == 0:
                        wt(ve, P + "attn_ready")
                    tick("ve", ve.scalar_tensor_tensor(
                        s_zres[:, o * TC:(o + 1) * TC], pk[:],
                        1.0 / 1024.0, s_zres[:, o * TC:(o + 1) * TC],
                        ALU.mult, ALU.add))
                    mark(P + f"ffn_fadd_{o}", "ve")

            @block.tensor
            def _(te):
                for q in range(NSUB):
                    for half in range(2):
                        gi = q * 2 + half
                        pk = pTA if gi % 2 == 0 else pTB
                        if gi >= 2:
                            wt(te, P + f"fin_ev_{gi-2}")
                        for gg in range(4):
                            g = half * 4 + gg
                            wt(te, P + f"ffn_fadd_{g}")
                            _last = te.transpose(
                                pk[:, gg * 128:(gg + 1) * 128],
                                s_zres[:, g * TC + q * 128:
                                       g * TC + (q + 1) * 128],
                                s_id32[:])
                        tick("te", _last)
                        mark(P + f"fin_tp_{gi}", "te")

            @block.vector
            def _(ve):
                for q in range(NSUB):
                    for half in range(2):
                        gi = q * 2 + half
                        pk = pTA if gi % 2 == 0 else pTB
                        wt(ve, P + f"fin_tp_{gi}")
                        _last = ve.tensor_tensor(
                            s_xntm[:, q * D + half * 512:
                                   q * D + (half + 1) * 512],
                            pk[:],
                            s_xtm[:, q * D + half * 512:
                                  q * D + (half + 1) * 512],
                            ALU.add)
                        tick("ve", _last)
                        mark(P + f"fin_ev_{gi}", "ve")

            @block.gpsimd
            def _(gp):
                barrier(gp, ("gp",))
                for q in range(NSUB):
                    wt(gp, P + f"fin_ev_{q * 2}")
                    wt(gp, P + f"fin_ev_{q * 2 + 1}")
                    tick("st", gp.dma_start(
                        out=dout[t0 + q * 128:t0 + (q + 1) * 128, :],
                        in_=s_xntm[:, q * D:(q + 1) * D]), 16)
                mark(P + "store", "st")

            @block.vector
            def _(ve):
                x3 = s_xn16.rearrange("p (g t) -> p g t", t=3 + TC)
                for g in range(G):
                    wt(ve, P + f"cv_{g}")
                if c + 1 < NCH and (c + 1) % (NCH // BLOC) != 0:
                    tick("ve", ve.tensor_copy(x3[:, :, 0:3],
                                              x3[:, :, TC:TC + 3]))
                else:
                    tick("ve", ve.memset(x3[:, :, 0:3], 0.0))
                mark(P + "halo", "ve")

            for k in ("gp", "te", "ve", "sc", "st"):
                mark(P + k, k)

        if DEBUG_TAPS:
            @block.gpsimd
            def _(gp):
                for k in ("gp", "te", "ve", "sc"):
                    wt(gp, f"c{NCH-1}_" + k)
                pairs = [("xn16", s_xn16), ("u16d", s_u16), ("btd", s_bt),
                         ("BCd", s_BC), ("xqd", s_xq), ("hqd", s_hq),
                         ("gad", s_ga), ("zresd", s_zres),
                         ("carryd", s_carry), ("acold", s_acol),
                         ("t1d", s_t1), ("statd", s_stat),
                         ("xtmd", s_xtm), ("xntmd", s_xntm)]
                for nm, t in pairs:
                    tick("st", gp.dma_start(out=dbg[nm], in_=t[:]), 16)
                tick("st", gp.dma_start(out=dbg["scrw1d"],
                                        in_=scr_w1[5, :, :]), 16)
                tick("st", gp.dma_start(out=dbg["scrw2d"],
                                        in_=scr_w2[3, :, :]), 16)

        @block.gpsimd
        def _(gp):
            if L.prev is not None:
                gp.wait_ge(sems["st"], L.cnt["st"])

    return nc


def build_program():
    led1 = Ledger(None)
    nc1 = bass.Bass("TRN2", target_bir_lowering=False,
                    detect_race_conditions=False)
    emit(nc1, led1)
    led2 = Ledger(led1.marks)
    nc2 = bass.Bass("TRN2", target_bir_lowering=False,
                    detect_race_conditions=False)
    emit(nc2, led2)
    assert led1.cnt == led2.cnt, (led1.cnt, led2.cnt)
    assert led1.marks == led2.marks
    return nc2


_NC_CACHE = None


def kernel(**inputs):
    global _NC_CACHE
    if _NC_CACHE is None:
        _NC_CACHE = build_program()
    nc = _NC_CACHE

    x = np.ascontiguousarray(inputs["L0_attn_x"], dtype=np.float32)
    base = {
        "conv_w": np.ascontiguousarray(
            inputs["L0_attn_conv_w"].reshape(D, DCONV), np.float32),
        "conv_b": np.ascontiguousarray(inputs["L0_attn_conv_b"], np.float32),
        "A": np.ascontiguousarray(inputs["L0_attn_A"], np.float32),
        "Bp_w": np.ascontiguousarray(inputs["L0_attn_Bp_w"], np.float32),
        "Bp_b": np.ascontiguousarray(inputs["L0_attn_Bp_b"], np.float32),
        "Cp_w": np.ascontiguousarray(inputs["L0_attn_Cp_w"], np.float32),
        "Cp_b": np.ascontiguousarray(inputs["L0_attn_Cp_b"], np.float32),
        "Dv": np.ascontiguousarray(inputs["L0_attn_D"], np.float32),
        "gate_w": np.ascontiguousarray(inputs["L0_attn_gate_w"], np.float32),
        "gate_b": np.ascontiguousarray(inputs["L0_attn_gate_b"], np.float32),
        "out_w": np.ascontiguousarray(inputs["L0_attn_out_w"], np.float32),
        "out_b": np.ascontiguousarray(inputs["L0_attn_out_b"], np.float32),
        "ln_g": np.ascontiguousarray(inputs["L0_attn_ln_g"], np.float32),
        "ln_b": np.ascontiguousarray(inputs["L0_attn_ln_b"], np.float32),
        "w1": np.ascontiguousarray(inputs["L0_ffn_w1"], np.float32),
        "b1": np.ascontiguousarray(inputs["L0_ffn_b1"], np.float32),
        "w2": np.ascontiguousarray(inputs["L0_ffn_w2"], np.float32),
        "b2": np.ascontiguousarray(inputs["L0_ffn_b2"], np.float32),
    }
    in_maps = []
    for ci in range(NCORES):
        m = dict(base)
        m["x"] = np.ascontiguousarray(
            x[ci * BLOC:(ci + 1) * BLOC].reshape(T, D))
        in_maps.append(m)

    res = run_bass_kernel_spmd(nc, in_maps, core_ids=list(range(NCORES)))
    out = np.empty((16, SEQ, D), np.float32)
    for ci in range(NCORES):
        out[ci * BLOC:(ci + 1) * BLOC] = res.results[ci]["out"].reshape(
            BLOC, SEQ, D)
    return out


# version 06
# speedup vs baseline: 1.1198x; 1.1198x over previous
import contextlib
import numpy as np

import concourse.bass as bass
import concourse.mybir as mybir
from concourse.bass_utils import run_bass_kernel_spmd

F32 = mybir.dt.float32
BF16 = mybir.dt.bfloat16
AF = mybir.ActivationFunctionType
ALU = mybir.AluOpType
AX = mybir.AxisListType

NCORES = 8
SEQ = 2048
BLOC = 2
T = SEQ * BLOC
D = 1024
H = 4096
NST = 16
DCONV = 4
LN_EPS = 1e-5
G = D // 128
HT = H // 128
TC = 512
NCH = T // TC
NSUB = TC // 128
RCONST = float(3 << 22)
DEBUG_TAPS = False


SEMKEYS = ("sy", "gp", "te", "ve", "sc", "params", "x", "st",
           "gs0", "gs1", "os0", "os1", "w10", "w11", "w20", "w21",
           "wp0", "wp1", "scr0", "scr1")


class Ledger:
    def __init__(self, prev_marks):
        self.cnt = {k: 0 for k in SEMKEYS}
        self.marks = {}
        self.prev = prev_marks

    def tick(self, sems, key, inst, k=1):
        inst.then_inc(sems[key], k)
        self.cnt[key] += k

    def mark(self, name, key):
        assert name not in self.marks, name
        self.marks[name] = (key, self.cnt[key])

    def wait(self, sems, eng, name):
        if self.prev is None:
            return
        key, val = self.prev[name]
        if val > 0:
            eng.wait_ge(sems[key], val)


def emit(nc, led):
    L = led

    dx = nc.dram_tensor("x", [T, D], F32, kind="ExternalInput").ap()
    dconv_w = nc.dram_tensor("conv_w", [D, DCONV], F32, kind="ExternalInput").ap()
    dconv_b = nc.dram_tensor("conv_b", [D], F32, kind="ExternalInput").ap()
    dA = nc.dram_tensor("A", [D, NST], F32, kind="ExternalInput").ap()
    dBpw = nc.dram_tensor("Bp_w", [NST, D], F32, kind="ExternalInput").ap()
    dBpb = nc.dram_tensor("Bp_b", [NST], F32, kind="ExternalInput").ap()
    dCpw = nc.dram_tensor("Cp_w", [NST, D], F32, kind="ExternalInput").ap()
    dCpb = nc.dram_tensor("Cp_b", [NST], F32, kind="ExternalInput").ap()
    dD = nc.dram_tensor("Dv", [D], F32, kind="ExternalInput").ap()
    dgw = nc.dram_tensor("gate_w", [D, D], F32, kind="ExternalInput").ap()
    dgb = nc.dram_tensor("gate_b", [D], F32, kind="ExternalInput").ap()
    dow = nc.dram_tensor("out_w", [D, D], F32, kind="ExternalInput").ap()
    dob = nc.dram_tensor("out_b", [D], F32, kind="ExternalInput").ap()
    dlng = nc.dram_tensor("ln_g", [D], F32, kind="ExternalInput").ap()
    dlnb = nc.dram_tensor("ln_b", [D], F32, kind="ExternalInput").ap()
    dw1 = nc.dram_tensor("w1", [H, D], F32, kind="ExternalInput").ap()
    db1 = nc.dram_tensor("b1", [H], F32, kind="ExternalInput").ap()
    dw2 = nc.dram_tensor("w2", [D, H], F32, kind="ExternalInput").ap()
    db2 = nc.dram_tensor("b2", [D], F32, kind="ExternalInput").ap()
    dout = nc.dram_tensor("out", [T, D], F32, kind="ExternalOutput").ap()
    scr_w1 = nc.dram_tensor("scr_w1", [HT, 128, 1024], BF16, kind="Internal").ap()
    scr_w2 = nc.dram_tensor("scr_w2", [G, 128, 4096], BF16, kind="Internal").ap()
    scr_g = nc.dram_tensor("scr_g", [G, 128, 1024], BF16, kind="Internal").ap()
    scr_o = nc.dram_tensor("scr_o", [G, 128, 1024], BF16, kind="Internal").ap()
    dbg = {}
    if DEBUG_TAPS:
        for nm, shape, dt in (
                ("xn16", [128, G * (3 + TC)], BF16),
                ("u16d", [128, G * TC], BF16),
                ("btd", [NST, 2 * TC], BF16),
                ("BCd", [128, 2 * NST * TC], BF16),
                ("xqd", [128, G * TC], BF16),
                ("hqd", [128, HT * TC], BF16),
                ("gad", [128, G * TC], BF16),
                ("zresd", [128, G * TC], F32),
                ("carryd", [128, G * NST], F32),
                ("acold", [128, G * NST], F32),
                ("t1d", [128, 2 * TC], F32),
                ("statd", [128, NSUB * 8], F32),
                ("xtmd", [128, NSUB * D], F32),
                ("xntmd", [128, NSUB * D], F32),
                ("scrw1d", [128, 1024], BF16),
                ("scrw2d", [128, 4096], BF16)):
            dbg[nm] = nc.dram_tensor("dbg_" + nm, shape, dt,
                                     kind="ExternalOutput").ap()

    ctx = contextlib.ExitStack()
    sb = lambda name, shape, dt=F32: ctx.enter_context(
        nc.sbuf_tensor(name, shape, dt))
    ps = lambda name: ctx.enter_context(nc.psum_tensor(name, [128, 512], F32))

    with ctx:
        s_BpCpT = sb("BpCpT", [128, 2 * G * NST], BF16)
        s_acol = sb("acol", [128, G * NST])
        s_diagD = sb("diagD", [128, G * 128], BF16)
        s_convw = sb("convw", [128, G * DCONV])
        s_dconv = sb("dconv", [128, G * DCONV * 128], BF16)
        s_cols = sb("cols", [128, 5 * G])
        OF_CONVB, OF_GATEB, OF_OUTB, OF_LNG, OF_LNB2 = (
            0, G, 2 * G, 3 * G, 4 * G)
        s_b1qC = sb("b1qC", [128, HT])
        s_bpb = sb("bpb", [NST, 2])
        s_id32 = sb("id32", [128, 128])
        s_id16 = sb("id16", [128, 128], BF16)
        s_sel = sb("sel", [NST, NST * 128], BF16)
        s_ones1 = sb("ones1", [1, 128])
        s_negC = sb("negC", [128, 1])
        s_ones16 = sb("ones16c", [128, 1], BF16)
        s_carry = sb("carry", [128, G * NST])
        s_w1s = sb("w1s", [128, 2 * 1024], BF16)
        s_w2s = sb("w2s", [128, 2 * 4096], BF16)
        s_gs = sb("gs", [128, 2 * 1024], BF16)
        s_os = sb("osb", [128, 2 * 1024], BF16)
        s_xtm = sb("xtm", [128, NSUB * D])
        s_xntm = sb("xntm", [128, NSUB * D])
        s_stat = sb("stat", [128, NSUB * 8])
        s_xq16 = sb("xq16", [128, 2 * D], BF16)
        s_xn16 = sb("xn16", [128, G * (3 + TC)], BF16)
        s_xq = sb("xq", [128, G * TC], BF16)
        s_u16 = sb("u16", [128, G * TC], BF16)
        s_bt = sb("bt", [NST, 2 * TC], BF16)
        s_BC = sb("BC", [128, 2 * NST * TC], BF16)
        s_w = sb("wbuf", [128, TC], BF16)
        s_s = sb("sbuf_s", [128, TC])
        s_z = sb("zbuf", [128, 2 * TC], BF16)
        s_ga = sb("ga", [128, G * TC], BF16)
        s_zres = sb("zres", [128, G * TC])
        s_z16 = sb("z16", [128, 2 * TC], BF16)
        s_hq = sb("hq", [128, HT * TC], BF16)
        s_t1 = sb("t1", [128, 2 * TC])
        pTA, pTB = ps("pTA"), ps("pTB")
        pMA, pMB = ps("pMA"), ps("pMB")
        pBCa, pBCb = ps("pBCa"), ps("pBCb")
        pYA, pYB = ps("pYA"), ps("pYB")

        sems = {k: ctx.enter_context(nc.semaphore("sem_" + k))
                for k in SEMKEYS}
        block = ctx.enter_context(nc.Block())

        def tick(key, inst, k=1):
            L.tick(sems, key, inst, k)

        def mark(name, key):
            L.mark(name, key)

        def wt(eng, name):
            L.wait(sems, eng, name)

        @block.gpsimd
        def _(gp):
            def ss(inst):
                tick("gp", inst)
                gp.wait_ge(sems["gp"], L.cnt["gp"])

            ss(gp.memset(s_id32[:], 0.0))
            ss(nc.gpsimd.affine_select(
                out=s_id32[:], in_=s_id32[:], compare_op=ALU.not_equal,
                fill=1.0, base=0, pattern=[[-1, 128]], channel_multiplier=1))
            ss(gp.memset(s_id16[:], 0.0))
            ss(nc.gpsimd.affine_select(
                out=s_id16[:], in_=s_id16[:], compare_op=ALU.not_equal,
                fill=1.0, base=0, pattern=[[-1, 128]], channel_multiplier=1))
            ss(gp.memset(s_sel[:], 0.0))
            sel3 = s_sel.rearrange("p (n f) -> p n f", f=128)
            ss(nc.gpsimd.affine_select(
                out=sel3, in_=sel3, compare_op=ALU.not_equal,
                fill=1.0, base=0, pattern=[[-1, NST], [0, 128]],
                channel_multiplier=1))
            gp.memset(s_ones1[:], 1.0)
            gp.memset(s_negC[:], -RCONST)
            gp.memset(s_ones16[:], 1.0)
            gp.memset(s_carry[:], 0.0)
            tick("gp", gp.memset(s_xn16[:], 0.0))
            mark("gp_const", "gp")

        @block.sync
        def _(sy):
            for g in range(G):
                tick("params", sy.dma_start(
                    out=s_convw[:, g * DCONV:(g + 1) * DCONV],
                    in_=dconv_w[g * 128:(g + 1) * 128, :]), 16)
                tick("params", sy.dma_start(
                    out=s_cols[:, OF_CONVB + g:OF_CONVB + g + 1],
                    in_=dconv_b[g * 128:(g + 1) * 128][:, None]), 16)
                tick("params", sy.dma_start(
                    out=s_cols[:, OF_GATEB + g:OF_GATEB + g + 1],
                    in_=dgb[g * 128:(g + 1) * 128][:, None]), 16)
                tick("params", sy.dma_start(
                    out=s_cols[:, OF_OUTB + g:OF_OUTB + g + 1],
                    in_=dob[g * 128:(g + 1) * 128][:, None]), 16)
                tick("params", sy.dma_start(
                    out=s_cols[:, OF_LNG + g:OF_LNG + g + 1],
                    in_=dlng[g * 128:(g + 1) * 128][:, None]), 16)
                tick("params", sy.dma_start(
                    out=s_acol[:, g * NST:(g + 1) * NST],
                    in_=dA[g * 128:(g + 1) * 128, :]), 16)
                tick("params", sy.dma_start(
                    out=s_stat[:, g:g + 1],
                    in_=dD[g * 128:(g + 1) * 128][:, None]), 16)
                tick("params", sy.dma_start(
                    out=s_stat[:, 8 + g:8 + g + 1],
                    in_=dlnb[g * 128:(g + 1) * 128][:, None]), 16)
            tick("params", sy.dma_start(out=s_bpb[:, 0:1], in_=dBpb[:, None]), 16)
            tick("params", sy.dma_start(out=s_bpb[:, 1:2], in_=dCpb[:, None]), 16)
            tick("params", sy.dma_start(
                out=s_zres[0:32, 0:128],
                in_=db1.rearrange("(a b) -> a b", b=128)), 16)
            tick("params", sy.dma_start(
                out=s_zres[0:8, 200:328],
                in_=db2.rearrange("(a b) -> a b", b=128)), 16)
            tick("params", sy.dma_start(
                out=s_zres[0:NST, 400:400 + D], in_=dBpw[:, :]), 16)
            tick("params", sy.dma_start(
                out=s_zres[0:NST, 1500:1500 + D], in_=dCpw[:, :]), 16)
            mark("sy_params", "params")

        @block.scalar
        def _(sc):
            wt(sc, "sy_params")
            tick("sc", sc.activation(s_acol[:], s_acol[:], AF.Exp))
            sc.wait_ge(sems["sc"], L.cnt["sc"])
            tick("sc", sc.activation(s_acol[:], s_acol[:], AF.Exp, scale=-1.0))
            mark("sc_acol", "sc")

        @block.tensor
        def _(te):
            wt(te, "sy_params")
            wt(te, "gp_const")
            te.transpose(pTA[:, 0:32], s_zres[0:32, 0:128],
                         s_id32[0:32, 0:32])
            te.transpose(pTA[:, 32:40], s_zres[0:8, 200:328],
                         s_id32[0:8, 0:8])
            for g in range(G):
                te.transpose(pTB[:, g * 16:(g + 1) * 16],
                             s_zres[0:NST, 400 + g * 128:400 + (g + 1) * 128],
                             s_id32[0:NST, 0:NST])
            for g in range(G):
                _last = te.transpose(pTB[:, 128 + g * 16:128 + (g + 1) * 16],
                             s_zres[0:NST, 1500 + g * 128:1500 + (g + 1) * 128],
                             s_id32[0:NST, 0:NST])
            tick("te", _last)
            mark("te_prep1", "te")

        @block.vector
        def _(ve):
            def vst(inst):
                tick("ve", inst)
                ve.wait_ge(sems["ve"], L.cnt["ve"])

            wt(ve, "te_prep1")
            wt(ve, "sy_params")
            wt(ve, "gp_const")
            for g in range(G):
                ve.tensor_scalar(s_diagD[:, g * 128:(g + 1) * 128],
                                 s_id16[:], s_stat[:, g:g + 1], None, ALU.mult)
            for g in range(G):
                for k in range(DCONV):
                    i = g * DCONV + k
                    ve.tensor_scalar(s_dconv[:, i * 128:(i + 1) * 128],
                                     s_id16[:], s_convw[:, i:i + 1],
                                     None, ALU.mult)
            vst(ve.tensor_scalar(s_b1qC[:], pTA[:, 0:HT], 64.0, RCONST,
                                 ALU.mult, ALU.add))
            vst(ve.tensor_scalar(s_b1qC[:], s_b1qC[:], RCONST + 127.0,
                                 RCONST - 128.0, ALU.min, ALU.max))
            vst(ve.tensor_scalar(s_b1qC[:], s_b1qC[:], RCONST, None,
                                 ALU.subtract))
            vst(ve.tensor_scalar(s_b1qC[:], s_b1qC[:], 0.25, None, ALU.mult))
            vst(ve.tensor_scalar(s_cols[:, OF_LNB2:OF_LNB2 + 8],
                                 pTA[:, 32:40], 64.0, RCONST,
                                 ALU.mult, ALU.add))
            vst(ve.tensor_scalar(s_cols[:, OF_LNB2:OF_LNB2 + 8],
                                 s_cols[:, OF_LNB2:OF_LNB2 + 8],
                                 RCONST + 127.0, RCONST - 128.0,
                                 ALU.min, ALU.max))
            vst(ve.tensor_scalar(s_cols[:, OF_LNB2:OF_LNB2 + 8],
                                 s_cols[:, OF_LNB2:OF_LNB2 + 8],
                                 RCONST, None, ALU.subtract))
            vst(ve.scalar_tensor_tensor(
                s_cols[:, OF_LNB2:OF_LNB2 + 8],
                s_cols[:, OF_LNB2:OF_LNB2 + 8], 1.0 / 64.0,
                s_stat[:, 8:16], ALU.mult, ALU.add))
            ve.tensor_copy(s_BpCpT[:, 0:G * NST], pTB[:, 0:G * NST])
            tick("ve", ve.tensor_copy(s_BpCpT[:, G * NST:2 * G * NST],
                                      pTB[:, 128:128 + G * NST]))
            mark("ve_prep1", "ve")

        wjobs = []
        for (src, dst, nrt, rowlen, mode) in (
                (dw1, scr_w1, HT, D, "q64"),
                (dw2, scr_w2, G, H, "q64"),
                (dgw, scr_g, G, D, "cast"),
                (dow, scr_o, G, D, "cast")):
            for j in range(nrt):
                wjobs.append((src, dst, j, rowlen, mode))
        NJ = len(wjobs)

        def wp_in(i, rowlen):
            if rowlen == H:
                return s_zres[:, 0:H]
            return s_xtm[:, (i % 2) * 2048:(i % 2) * 2048 + rowlen]

        def wp_t1(i, rowlen):
            if rowlen == H:
                return s_xntm[:, 0:H]
            return s_xntm[:, (i % 2) * 2048:(i % 2) * 2048 + rowlen]

        @block.sync
        def _(sy):
            for i, (src, dst, j, rowlen, mode) in enumerate(wjobs):
                if rowlen == H:
                    wt(sy, "te_prep1")
                    if i >= 1:
                        wt(sy, f"wp_read_{i-1}")
                if i >= 2:
                    wt(sy, f"wp_read_{i-2}")
                tick(f"wp{i % 2}", sy.dma_start(
                    out=wp_in(i, rowlen),
                    in_=src[j * 128:(j + 1) * 128, :]), 16)
                mark(f"wp_dma_{i}", f"wp{i % 2}")

        @block.scalar
        def _(sc):
            for i, (src, dst, j, rowlen, mode) in enumerate(wjobs):
                par = i % 2
                wt(sc, f"wp_dma_{i}")
                src_sb = wp_in(i, rowlen)
                if mode == "q64":
                    if i >= 1:
                        wt(sc, f"wp_clip_{i-1}")
                    if i >= 2:
                        wt(sc, f"wp_clip_{i-2}")
                    tick("sc", sc.activation(
                        wp_t1(i, rowlen), src_sb,
                        AF.Copy, bias=RCONST, scale=64.0))
                else:
                    if i >= 2:
                        wt(sc, f"wp_te_{i-2}")
                    tick("sc", sc.activation(
                        s_hq[:, par * 4096:par * 4096 + rowlen], src_sb,
                        AF.Copy))
                mark(f"wp_read_{i}", "sc")



        @block.tensor
        def _(te):
            gidx = 0
            for i, (src, dst, j, rowlen, mode) in enumerate(wjobs):
                par = i % 2
                if mode == "q64":
                    wt(te, f"wp_clip_{i}")
                else:
                    wt(te, f"wp_read_{i}")
                if i >= 2:
                    wt(te, f"wp_stage_{i-2}")
                int_sb = s_hq[:, par * 4096:par * 4096 + rowlen]
                nk = rowlen // 128
                for k0 in range(0, nk, 4):
                    pk = pTA if gidx % 2 == 0 else pTB
                    pk16 = pk[:].bitcast(BF16)
                    if gidx < 2:
                        wt(te, "ve_prep1")
                    if gidx >= 2:
                        wt(te, f"wp_ev_{gidx-2}")
                    kn = min(4, nk - k0)
                    for kk in range(kn):
                        _last = te.transpose(pk16[:, kk * 128:(kk + 1) * 128],
                                     int_sb[:, (k0 + kk) * 128:
                                            (k0 + kk + 1) * 128],
                                     s_id16[:])
                    tick("te", _last)
                    mark(f"wp_tp_{gidx}", "te")
                    gidx += 1
                mark(f"wp_te_{i}", "te")

        item_gidx = []
        _g = 0
        for (_s, _d, _j, _rl, _m) in wjobs:
            ng = (_rl // 128 + 3) // 4
            item_gidx.append((_g, ng))
            _g += ng

        @block.vector
        def _(ve):
            def emit_clip(i):
                srcj, dstj, jj, rowlen, mode = wjobs[i]
                par = i % 2
                if mode != "q64":
                    mark(f"wp_clip_{i}", "ve")
                    return
                wt(ve, f"wp_read_{i}")
                if i >= 2:
                    wt(ve, f"wp_te_{i-2}")
                t1 = wp_t1(i, rowlen)
                ve.tensor_scalar(t1, t1, RCONST + 127.0, RCONST - 128.0,
                                 ALU.min, ALU.max)
                tick("ve", ve.tensor_scalar(
                    s_hq[:, par * 4096:par * 4096 + rowlen], t1,
                    RCONST, None, ALU.subtract))
                mark(f"wp_clip_{i}", "ve")

            def emit_evac(i):
                srcj, dstj, jj, rowlen, mode = wjobs[i]
                par = i % 2
                nk = rowlen // 128
                stg = s_w2s if rowlen == H else s_w1s
                stg_off = par * rowlen
                if i >= 2:
                    wt(ve, f"wp_scr_{i-2}")
                g0, ng = item_gidx[i]
                for gg in range(ng):
                    gidx = g0 + gg
                    k0 = gg * 4
                    wt(ve, f"wp_tp_{gidx}")
                    pk16 = (pTA if gidx % 2 == 0 else pTB)[:].bitcast(BF16)
                    kn = min(4, nk - k0)
                    _last = ve.tensor_copy(
                        stg[:, stg_off + k0 * 128:stg_off + (k0 + kn) * 128],
                        pk16[:, 0:kn * 128])
                    tick("ve", _last)
                    mark(f"wp_ev_{gidx}", "ve")
                mark(f"wp_stage_{i}", "ve")

            for i in range(NJ + 1):
                if i < NJ:
                    emit_clip(i)
                if i >= 1:
                    emit_evac(i - 1)

        @block.gpsimd
        def _(gp):
            for i, (src, dst, j, rowlen, mode) in enumerate(wjobs):
                par = i % 2
                wt(gp, f"wp_stage_{i}")
                stg = s_w2s if rowlen == H else s_w1s
                tick(f"scr{i % 2}", gp.dma_start(
                    out=dst[j, :, :],
                    in_=stg[:, par * rowlen:(par + 1) * rowlen]), 16)
                mark(f"wp_scr_{i}", f"scr{i % 2}")

        for k in SEMKEYS:
            mark(f"prep_end_{k}", k)

        for c in range(NCH):
            seq_start = (c % (NCH // BLOC) == 0)
            t0 = c * TC
            P = f"c{c}_"
            prev = f"c{c-1}_" if c > 0 else "prep_end_"

            def barrier(eng, names=("gp", "te", "ve", "sc", "st")):
                for k in names:
                    wt(eng, prev + k if c > 0 else f"prep_end_{k}")
                if c == 0:
                    wt(eng, "prep_end_scr0")
                    wt(eng, "prep_end_scr1")

            @block.sync
            def _(sy):
                barrier(sy)
                for q in range(NSUB):
                    tick("x", sy.dma_start(
                        out=s_xtm[:, q * D:(q + 1) * D],
                        in_=dx[t0 + q * 128:t0 + (q + 1) * 128, :]), 16)
                mark(P + "x", "x")
                for o in range(G):
                    if o >= 2:
                        wt(sy, P + f"gate_mm_{o-2}")
                    tick(f"gs{o % 2}", sy.dma_start(
                        out=s_gs[:, (o % 2) * 1024:(o % 2 + 1) * 1024],
                        in_=scr_g[o, :, :]), 16)
                    mark(P + f"gs_{o}", f"gs{o % 2}")
                for j in range(HT):
                    if j >= 2:
                        wt(sy, P + f"ffn_mm1_{j-2}")
                    tick(f"w1{j % 2}", sy.dma_start(
                        out=s_w1s[:, (j % 2) * 1024:(j % 2 + 1) * 1024],
                        in_=scr_w1[j, :, :]), 16)
                    mark(P + f"w1_{j}", f"w1{j % 2}")
                for o in range(G):
                    if o >= 2:
                        wt(sy, P + f"out_mm_{o-2}")
                    tick(f"os{o % 2}", sy.dma_start(
                        out=s_os[:, (o % 2) * 1024:(o % 2 + 1) * 1024],
                        in_=scr_o[o, :, :]), 16)
                    mark(P + f"os_{o}", f"os{o % 2}")
                for o in range(G):
                    if o >= 2:
                        wt(sy, P + f"ffn_mm2_{o-2}")
                    tick(f"w2{o % 2}", sy.dma_start(
                        out=s_w2s[:, (o % 2) * 4096:(o % 2 + 1) * 4096],
                        in_=scr_w2[o, :, :]), 16)
                    mark(P + f"w2_{o}", f"w2{o % 2}")

            @block.vector
            def _(ve):
                barrier(ve)
                wt(ve, P + "x")
                for q in range(NSUB):
                    _last = ve.reduce_sum(s_stat[:, q * 8:q * 8 + 1],
                                          s_xtm[:, q * D:(q + 1) * D],
                                          axis=AX.X)
                tick("ve", _last)
                mark(P + "ln1_sum", "ve")

            @block.scalar
            def _(sc):
                barrier(sc)
                wt(sc, P + "x")
                for q in range(NSUB):
                    _last = sc.activation(
                        s_xntm[:, q * D:(q + 1) * D],
                        s_xtm[:, q * D:(q + 1) * D], AF.Square,
                        accum_out=s_stat[:, q * 8 + 1:q * 8 + 2])
                tick("sc", _last)
                mark(P + "ln1_sq", "sc")

            st3 = s_stat.rearrange("p (q k) -> p q k", k=8)

            @block.vector
            def _(ve):
                def vst(inst):
                    tick("ve", inst)
                    ve.wait_ge(sems["ve"], L.cnt["ve"])

                wt(ve, P + "ln1_sum")
                wt(ve, P + "ln1_sq")
                ve.tensor_scalar(st3[:, :, 2:3], st3[:, :, 0:1], 1.0 / D,
                                 None, ALU.mult)
                vst(ve.tensor_scalar(st3[:, :, 3:4], st3[:, :, 1:2], 1.0 / D,
                                     None, ALU.mult))
                vst(ve.tensor_tensor(st3[:, :, 4:5], st3[:, :, 2:3],
                                     st3[:, :, 2:3], ALU.mult))
                vst(ve.tensor_tensor(st3[:, :, 5:6], st3[:, :, 3:4],
                                     st3[:, :, 4:5], ALU.subtract))
                tick("ve", ve.tensor_scalar(st3[:, :, 5:6], st3[:, :, 5:6],
                                            LN_EPS, None, ALU.add))
                mark(P + "ln1_var", "ve")

            @block.scalar
            def _(sc):
                wt(sc, P + "ln1_var")
                tick("sc", sc.activation(st3[:, :, 5:6], st3[:, :, 5:6],
                                         AF.Sqrt))
                mark(P + "ln1_sd", "sc")

            @block.vector
            def _(ve):
                def vst(inst):
                    tick("ve", inst)
                    ve.wait_ge(sems["ve"], L.cnt["ve"])

                wt(ve, P + "ln1_sd")
                vst(ve.reciprocal(st3[:, :, 6:7], st3[:, :, 5:6]))
                vst(ve.tensor_tensor(st3[:, :, 7:8], st3[:, :, 2:3],
                                     st3[:, :, 6:7], ALU.mult))
                vst(ve.tensor_scalar(st3[:, :, 7:8], st3[:, :, 7:8],
                                     -1.0, None, ALU.mult))
                for q in range(NSUB):
                    _last = ve.tensor_scalar(s_xntm[:, q * D:(q + 1) * D],
                                             s_xtm[:, q * D:(q + 1) * D],
                                             st3[:, q, 6:7], st3[:, q, 7:8],
                                             ALU.mult, ALU.add)
                tick("ve", _last)
                mark(P + "ln1_ap", "ve")

            tgroups = []
            for q in range(NSUB):
                for srcn in ("xn", "xq"):
                    for half in range(2):
                        tgroups.append((q, srcn, half))

            @block.tensor
            def _(te):
                for gi, (q, srcn, half) in enumerate(tgroups):
                    pk = pTA if gi % 2 == 0 else pTB
                    if gi >= 2:
                        wt(te, P + f"tpev_{gi-2}")
                    if srcn == "xn":
                        wt(te, P + "ln1_ap")
                        for gg in range(4):
                            g = half * 4 + gg
                            _last = te.transpose(
                                pk[:, gg * 128:(gg + 1) * 128],
                                s_xntm[:, q * D + g * 128:
                                       q * D + (g + 1) * 128],
                                s_id32[:])
                    else:
                        wt(te, P + f"xqclip_{q}")
                        par = q % 2
                        pk16 = pk[:].bitcast(BF16)
                        for gg in range(4):
                            g = half * 4 + gg
                            _last = te.transpose(
                                pk16[:, gg * 128:(gg + 1) * 128],
                                s_xq16[:, par * D + g * 128:
                                       par * D + (g + 1) * 128],
                                s_id16[:])
                    tick("te", _last)
                    mark(P + f"tp_{srcn}_{q}_{half}", "te")

            @block.scalar
            def _(sc):
                def xq1(q):
                    if q >= 1:
                        wt(sc, P + f"xqclip_{q-1}")
                    if q >= 2:
                        wt(sc, P + f"tp_xq_{q-2}_1")
                    tick("sc", sc.activation(
                        s_t1[:, 0:D], s_xntm[:, q * D:(q + 1) * D], AF.Copy,
                        bias=RCONST, scale=16.0))
                    mark(P + f"xq1_{q}", "sc")

                xq1(0)
                xq1(1)
                for gi, (q, srcn, half) in enumerate(tgroups):
                    pk = pTA if gi % 2 == 0 else pTB
                    wt(sc, P + f"tp_{srcn}_{q}_{half}")
                    pk16 = pk[:].bitcast(BF16)
                    for gg in range(4):
                        g = half * 4 + gg
                        if srcn == "xn":
                            _last = sc.copy(
                                s_xn16[:, g * (3 + TC) + 3 + q * 128:
                                       g * (3 + TC) + 3 + (q + 1) * 128],
                                pk[:, gg * 128:(gg + 1) * 128])
                        else:
                            _last = sc.copy(
                                s_xq[:, g * TC + q * 128:
                                     g * TC + (q + 1) * 128],
                                pk16[:, gg * 128:(gg + 1) * 128])
                    tick("sc", _last)
                    mark(P + f"tpev_{gi}", "sc")
                    if srcn == "xq" and half == 1 and q + 2 < NSUB:
                        xq1(q + 2)
                mark(P + "xn_ready", "sc")
                mark(P + "xq_ready", "sc")

            @block.vector
            def _(ve):
                for q in range(NSUB):
                    par = q % 2
                    wt(ve, P + f"xq1_{q}")
                    ve.tensor_scalar(s_t1[:, 0:D], s_t1[:, 0:D],
                                     RCONST + 127.0, RCONST - 128.0,
                                     ALU.min, ALU.max)
                    tick("ve", ve.tensor_scalar(
                        s_xq16[:, par * D:(par + 1) * D], s_t1[:, 0:D],
                        RCONST, None, ALU.subtract))
                    mark(P + f"xqclip_{q}", "ve")

            @block.tensor
            def _(te):
                for g in range(G):
                    pk = pTA if g % 2 == 0 else pTB
                    if g == 0:
                        wt(te, P + "tpev_14")
                        wt(te, P + "xn_ready")
                    elif g == 1:
                        wt(te, P + "tpev_15")
                    else:
                        wt(te, P + f"silu_{g-2}")
                    base = g * (3 + TC)
                    for k in range(DCONV):
                        kk = DCONV - 1 - k
                        _last = te.matmul(
                            pk[:],
                            s_dconv[:, (g * DCONV + kk) * 128:
                                    (g * DCONV + kk + 1) * 128],
                            s_xn16[:, base + 3 - k:base + 3 - k + TC],
                            start=(k == 0), stop=(k == DCONV - 1))
                    tick("te", _last)
                    mark(P + f"cv_{g}", "te")

            @block.scalar
            def _(sc):
                for g in range(G):
                    pk = pTA if g % 2 == 0 else pTB
                    wt(sc, P + f"cv_{g}")
                    tick("sc", sc.activation(
                        s_u16[:, g * TC:(g + 1) * TC], pk[:], AF.Silu,
                        bias=s_cols[:, OF_CONVB + g:OF_CONVB + g + 1]))
                    mark(P + f"silu_{g}", "sc")

            @block.tensor
            def _(te):
                for g in range(G):
                    wt(te, P + f"silu_{g}")
                    te.matmul(pYA[0:NST, :],
                              s_BpCpT[:, g * NST:(g + 1) * NST],
                              s_u16[:, g * TC:(g + 1) * TC],
                              start=(g == 0), stop=(g == G - 1))
                for g in range(G):
                    _last = te.matmul(pYB[0:NST, :],
                                      s_BpCpT[:, (G + g) * NST:
                                              (G + g + 1) * NST],
                                      s_u16[:, g * TC:(g + 1) * TC],
                                      start=(g == 0), stop=(g == G - 1))
                tick("te", _last)
                mark(P + "btct_mm", "te")

            @block.vector
            def _(ve):
                wt(ve, P + "btct_mm")
                ve.tensor_scalar(s_bt[0:NST, 0:TC], pYA[0:NST, :],
                                 s_bpb[:, 0:1], None, ALU.add)
                tick("ve", ve.tensor_scalar(s_bt[0:NST, TC:2 * TC],
                                            pYB[0:NST, :],
                                            s_bpb[:, 1:2], None, ALU.add))
                mark(P + "btct_ev", "ve")

            bcbank = [pBCa, pBCb, pYA, pYB]

            @block.tensor
            def _(te):
                wt(te, P + "btct_ev")
                for n in range(NST):
                    for which in range(2):
                        bi = (2 * n + which) % 4
                        gi = 2 * n + which
                        if gi >= 4:
                            wt(te, P + f"bcev_{gi-4}")
                        tick("te", te.matmul(
                            bcbank[bi][:],
                            s_sel[:, n * 128:(n + 1) * 128],
                            s_bt[0:NST, which * TC:(which + 1) * TC],
                            start=True, stop=True))
                        mark(P + f"bcmm_{gi}", "te")

            @block.scalar
            def _(sc):
                for n in range(NST):
                    for which in range(2):
                        gi = 2 * n + which
                        bi = gi % 4
                        wt(sc, P + f"bcmm_{gi}")
                        tick("sc", sc.copy(
                            s_BC[:, (which * NST + n) * TC:
                                 (which * NST + n + 1) * TC],
                            bcbank[bi][:]))
                        mark(P + f"bcev_{gi}", "sc")
                mark(P + "bc_ready", "sc")

            @block.tensor
            def _(te):
                for o in range(G):
                    wt(te, P + f"gs_{o}")
                    wt(te, P + "xn_ready")
                    if o >= 2:
                        wt(te, P + f"gate_ev_{o-2}")
                    pk = pMA if o % 2 == 0 else pMB
                    for k in range(G):
                        _last = te.matmul(
                            pk[:],
                            s_gs[:, (o % 2) * 1024 + k * 128:
                                 (o % 2) * 1024 + (k + 1) * 128],
                            s_xn16[:, k * (3 + TC) + 3:
                                   k * (3 + TC) + 3 + TC],
                            start=(k == 0), stop=(k == G - 1))
                    tick("te", _last)
                    mark(P + f"gate_mm_{o}", "te")

            @block.scalar
            def _(sc):
                for o in range(G):
                    pk = pMA if o % 2 == 0 else pMB
                    wt(sc, P + f"gate_mm_{o}")
                    tick("sc", sc.activation(
                        s_ga[:, o * TC:(o + 1) * TC], pk[:], AF.Silu,
                        bias=s_cols[:, OF_GATEB + o:OF_GATEB + o + 1]))
                    mark(P + f"gate_ev_{o}", "sc")

            fbank = [pMA, pMB, pTA, pTB]

            @block.vector
            def _(ve):
                def emit_yg(g):
                    pk = pYA if g % 2 == 0 else pYB
                    wt(ve, P + f"ymm_{g * NST + NST - 1}")
                    wt(ve, P + f"gate_ev_{g}")
                    tick("ve", ve.tensor_tensor(
                        s_ga[:, g * TC:(g + 1) * TC], pk[:],
                        s_ga[:, g * TC:(g + 1) * TC], ALU.mult))
                    mark(P + f"yg_{g}", "ve")

                def emit_h(j):
                    par = j % 2
                    bk = fbank[j % 4]
                    wt(ve, P + f"ffn_mm1_{j}")
                    if j >= 2:
                        wt(ve, P + f"ffn_h3_{j-2}")
                    ve.scalar_tensor_tensor(
                        s_t1[:, par * TC:(par + 1) * TC], bk[:], 1.0 / 64.0,
                        s_b1qC[:, j:j + 1].broadcast_to([128, TC]),
                        ALU.mult, ALU.add)
                    tick("ve", ve.tensor_scalar(
                        s_t1[:, par * TC:(par + 1) * TC],
                        s_t1[:, par * TC:(par + 1) * TC],
                        RCONST, RCONST + 127.0, ALU.add, ALU.min))
                    mark(P + f"ffn_h2_{j}", "ve")

                wt(ve, P + "bc_ready")
                for g in range(G):
                    for n in range(NST):
                        idx = g * NST + n
                        par = idx % 2
                        if idx >= 2:
                            wt(ve, P + f"ymm_{idx-2}")
                        ve.tensor_tensor(s_w[:],
                                         s_u16[:, g * TC:(g + 1) * TC],
                                         s_BC[:, n * TC:(n + 1) * TC],
                                         ALU.mult)
                        ci = g * NST + n
                        init = (0.0 if seq_start
                                else s_carry[:, ci:ci + 1])
                        ve.tensor_tensor_scan(
                            s_s[:],
                            s_acol[:, ci:ci + 1].broadcast_to([128, TC]),
                            s_w[:], init, ALU.mult, ALU.add)
                        tick("ve", ve.tensor_tensor(
                            s_z[:, par * TC:(par + 1) * TC], s_s[:],
                            s_BC[:, (NST + n) * TC:(NST + n + 1) * TC],
                            ALU.mult))
                        mark(P + f"z_{idx}", "ve")
                        ve.tensor_copy(s_carry[:, ci:ci + 1],
                                       s_s[:, TC - 1:TC])
                    if g >= 1:
                        emit_yg(g - 1)
                        for j in range(4 * (g - 1), 4 * g):
                            emit_h(j)
                emit_yg(G - 1)
                for j in range(4 * (G - 1), 4 * G):
                    emit_h(j)

            @block.tensor
            def _(te):
                for g in range(G):
                    pk = pYA if g % 2 == 0 else pYB
                    if g < 2:
                        wt(te, P + f"bcev_{NST * 2 - 2 + g}")
                    else:
                        wt(te, P + f"yg_{g-2}")
                    te.matmul(pk[:], s_diagD[:, g * 128:(g + 1) * 128],
                              s_u16[:, g * TC:(g + 1) * TC],
                              start=True, stop=False)
                    for n in range(NST):
                        idx = g * NST + n
                        par = idx % 2
                        wt(te, P + f"z_{idx}")
                        tick("te", te.matmul(
                            pk[:], s_id16[:],
                            s_z[:, par * TC:(par + 1) * TC],
                            start=False, stop=(n == NST - 1)))
                        mark(P + f"ymm_{idx}", "te")
                    for j in range(4 * g, 4 * g + 4):
                        wt(te, P + f"w1_{j}")
                        if j == 0:
                            wt(te, P + "xq_ready")
                            wt(te, P + f"gate_ev_{G-2}")
                        elif j == 1:
                            wt(te, P + f"gate_ev_{G-1}")
                        elif j == 2:
                            wt(te, P + f"silu_{G-2}")
                        elif j == 3:
                            wt(te, P + f"silu_{G-1}")
                        else:
                            wt(te, P + f"ffn_h2_{j-4}")
                        bk = fbank[j % 4]
                        for k in range(G):
                            _last = te.matmul(
                                bk[:],
                                s_w1s[:, (j % 2) * 1024 + k * 128:
                                      (j % 2) * 1024 + (k + 1) * 128],
                                s_xq[:, k * TC:(k + 1) * TC],
                                start=(k == 0), stop=(k == G - 1))
                        tick("te", _last)
                        mark(P + f"ffn_mm1_{j}", "te")

            @block.scalar
            def _(sc):
                for j in range(HT):
                    par = j % 2
                    wt(sc, P + f"ffn_h2_{j}")
                    tick("sc", sc.activation(
                        s_hq[:, j * TC:(j + 1) * TC],
                        s_t1[:, par * TC:(par + 1) * TC], AF.Relu,
                        bias=s_negC[:]))
                    mark(P + f"ffn_h3_{j}", "sc")

            @block.tensor
            def _(te):
                for o in range(G):
                    wt(te, P + f"os_{o}")
                    if o == 0:
                        for g in range(G):
                            wt(te, P + f"yg_{g}")
                        wt(te, P + f"ffn_h2_{HT-4}")
                    elif o == 1:
                        wt(te, P + f"ffn_h2_{HT-3}")
                    else:
                        wt(te, P + f"res_{o-2}")
                    pk = pMA if o % 2 == 0 else pMB
                    for k in range(G):
                        _last = te.matmul(
                            pk[:],
                            s_os[:, (o % 2) * 1024 + k * 128:
                                 (o % 2) * 1024 + (k + 1) * 128],
                            s_ga[:, k * TC:(k + 1) * TC],
                            start=(k == 0), stop=(k == G - 1))
                    tick("te", _last)
                    mark(P + f"out_mm_{o}", "te")

            @block.vector
            def _(ve):
                for o in range(G):
                    pk = pMA if o % 2 == 0 else pMB
                    wt(ve, P + f"out_mm_{o}")
                    tick("ve", ve.scalar_tensor_tensor(
                        s_zres[:, o * TC:(o + 1) * TC], pk[:],
                        s_cols[:, OF_OUTB + o:OF_OUTB + o + 1],
                        s_xn16[:, o * (3 + TC) + 3:o * (3 + TC) + 3 + TC],
                        ALU.add, ALU.add))
                    mark(P + f"res_{o}", "ve")

            @block.scalar
            def _(sc):
                for g in range(G):
                    par = g % 2
                    wt(sc, P + f"res_{g}")
                    if g >= 2:
                        wt(sc, P + f"ln2sq_{g-2}")
                    tick("sc", sc.copy(s_z16[:, par * TC:(par + 1) * TC],
                                       s_zres[:, g * TC:(g + 1) * TC]))
                    mark(P + f"ln2cast_{g}", "sc")
                    wt(sc, P + f"ln2mu_{g}")
                    tick("sc", sc.activation(
                        s_z16[:, par * TC:(par + 1) * TC],
                        s_z16[:, par * TC:(par + 1) * TC], AF.Square))
                    mark(P + f"ln2sqc_{g}", "sc")

            @block.tensor
            def _(te):
                for g in range(G):
                    par = g % 2
                    wt(te, P + f"ln2cast_{g}")
                    if g == 0:
                        wt(te, P + f"bcev_{NST * 2 - 4}")
                    tick("te", te.matmul(
                        pBCa[0:1, :], s_ones16[:],
                        s_z16[:, par * TC:(par + 1) * TC],
                        start=(g == 0), stop=(g == G - 1)))
                    mark(P + f"ln2mu_{g}", "te")
                    wt(te, P + f"ln2sqc_{g}")
                    if g == 0:
                        wt(te, P + f"bcev_{NST * 2 - 3}")
                    tick("te", te.matmul(
                        pBCb[0:1, :], s_ones16[:],
                        s_z16[:, par * TC:(par + 1) * TC],
                        start=(g == 0), stop=(g == G - 1)))
                    mark(P + f"ln2sq_{g}", "te")

            MU = s_s[0:1, 0:TC]
            VCH = s_t1[0:1, 0:TC]
            MUR = s_t1[0:1, TC:2 * TC]

            @block.vector
            def _(ve):
                def vst(inst):
                    tick("ve", inst)
                    ve.wait_ge(sems["ve"], L.cnt["ve"])

                wt(ve, P + f"ln2sq_{G-1}")
                wt(ve, P + f"ffn_h3_{HT-2}")
                wt(ve, P + f"ffn_h3_{HT-1}")
                vst(ve.tensor_scalar(MU, pBCa[0:1, :], 1.0 / D, None,
                                     ALU.mult))
                vst(ve.tensor_scalar(VCH, pBCb[0:1, :], 1.0 / D, None,
                                     ALU.mult))
                vst(ve.tensor_tensor(MUR, MU, MU, ALU.mult))
                vst(ve.tensor_tensor(VCH, VCH, MUR, ALU.subtract))
                tick("ve", ve.tensor_scalar(VCH, VCH, LN_EPS, None, ALU.add))
                mark(P + "ln2_var", "ve")

            @block.scalar
            def _(sc):
                wt(sc, P + "ln2_var")
                tick("sc", sc.activation(VCH, VCH, AF.Sqrt))
                mark(P + "ln2_sd", "sc")

            @block.vector
            def _(ve):
                def vst(inst):
                    tick("ve", inst)
                    ve.wait_ge(sems["ve"], L.cnt["ve"])

                wt(ve, P + "ln2_sd")
                vst(ve.reciprocal(VCH, VCH))
                vst(ve.tensor_tensor(MUR, MU, VCH, ALU.mult))
                tick("ve", ve.nop())
                mark(P + "ln2_r", "ve")

            @block.tensor
            def _(te):
                wt(te, P + "ln2_r")
                te.matmul(pBCa[:], s_ones1[:], VCH, start=True, stop=True)
                tick("te", te.matmul(pBCb[:], s_ones1[:], MUR,
                                     start=True, stop=True))
                mark(P + "ln2_bc", "te")

            @block.vector
            def _(ve):
                wt(ve, P + "ln2_bc")
                for g in range(G):
                    ve.tensor_tensor(s_zres[:, g * TC:(g + 1) * TC],
                                     s_zres[:, g * TC:(g + 1) * TC],
                                     pBCa[:], ALU.mult)
                    _last = ve.tensor_tensor(s_zres[:, g * TC:(g + 1) * TC],
                                             s_zres[:, g * TC:(g + 1) * TC],
                                             pBCb[:], ALU.subtract)
                for g in range(G):
                    _last = ve.tensor_scalar(
                        s_zres[:, g * TC:(g + 1) * TC],
                        s_zres[:, g * TC:(g + 1) * TC],
                        s_cols[:, OF_LNG + g:OF_LNG + g + 1],
                        s_cols[:, OF_LNB2 + g:OF_LNB2 + g + 1],
                        ALU.mult, ALU.add)
                tick("ve", _last)
                mark(P + "attn_ready", "ve")

            @block.tensor
            def _(te):
                for o in range(G):
                    wt(te, P + f"w2_{o}")
                    if o == 0:
                        wt(te, P + f"ffn_h3_{HT-1}")
                        wt(te, P + f"res_{G-2}")
                    elif o == 1:
                        wt(te, P + f"res_{G-1}")
                    else:
                        wt(te, P + f"ffn_fadd_{o-2}")
                    pk = pMA if o % 2 == 0 else pMB
                    for k in range(HT):
                        _last = te.matmul(
                            pk[:],
                            s_w2s[:, (o % 2) * 4096 + k * 128:
                                  (o % 2) * 4096 + (k + 1) * 128],
                            s_hq[:, k * TC:(k + 1) * TC],
                            start=(k == 0), stop=(k == HT - 1))
                    tick("te", _last)
                    mark(P + f"ffn_mm2_{o}", "te")

            @block.vector
            def _(ve):
                for o in range(G):
                    pk = pMA if o % 2 == 0 else pMB
                    wt(ve, P + f"ffn_mm2_{o}")
                    if o == 0:
                        wt(ve, P + "attn_ready")
                    tick("ve", ve.scalar_tensor_tensor(
                        s_zres[:, o * TC:(o + 1) * TC], pk[:],
                        1.0 / 1024.0, s_zres[:, o * TC:(o + 1) * TC],
                        ALU.mult, ALU.add))
                    mark(P + f"ffn_fadd_{o}", "ve")

            @block.tensor
            def _(te):
                for q in range(NSUB):
                    for half in range(2):
                        gi = q * 2 + half
                        pk = pTA if gi % 2 == 0 else pTB
                        if gi >= 2:
                            wt(te, P + f"fin_ev_{gi-2}")
                        for gg in range(4):
                            g = half * 4 + gg
                            wt(te, P + f"ffn_fadd_{g}")
                            _last = te.transpose(
                                pk[:, gg * 128:(gg + 1) * 128],
                                s_zres[:, g * TC + q * 128:
                                       g * TC + (q + 1) * 128],
                                s_id32[:])
                        tick("te", _last)
                        mark(P + f"fin_tp_{gi}", "te")

            @block.vector
            def _(ve):
                for q in range(NSUB):
                    for half in range(2):
                        gi = q * 2 + half
                        pk = pTA if gi % 2 == 0 else pTB
                        wt(ve, P + f"fin_tp_{gi}")
                        tick("ve", ve.tensor_tensor(
                            s_xntm[:, q * D + half * 512:
                                   q * D + (half + 1) * 512],
                            pk[:],
                            s_xtm[:, q * D + half * 512:
                                  q * D + (half + 1) * 512],
                            ALU.add))
                        mark(P + f"fin_ev_{gi}", "ve")

            @block.gpsimd
            def _(gp):
                barrier(gp, ("gp",))
                for q in range(NSUB):
                    wt(gp, P + f"fin_ev_{q * 2}")
                    wt(gp, P + f"fin_ev_{q * 2 + 1}")
                    tick("st", gp.dma_start(
                        out=dout[t0 + q * 128:t0 + (q + 1) * 128, :],
                        in_=s_xntm[:, q * D:(q + 1) * D]), 16)
                mark(P + "store", "st")

            @block.vector
            def _(ve):
                x3 = s_xn16.rearrange("p (g t) -> p g t", t=3 + TC)
                for g in range(G):
                    wt(ve, P + f"cv_{g}")
                if c + 1 < NCH and (c + 1) % (NCH // BLOC) != 0:
                    tick("ve", ve.tensor_copy(x3[:, :, 0:3],
                                              x3[:, :, TC:TC + 3]))
                else:
                    tick("ve", ve.memset(x3[:, :, 0:3], 0.0))
                mark(P + "halo", "ve")

            for k in ("gp", "te", "ve", "sc", "st"):
                mark(P + k, k)

        if DEBUG_TAPS:
            @block.gpsimd
            def _(gp):
                for k in ("gp", "te", "ve", "sc"):
                    wt(gp, f"c{NCH-1}_" + k)
                pairs = [("xn16", s_xn16), ("u16d", s_u16), ("btd", s_bt),
                         ("BCd", s_BC), ("xqd", s_xq), ("hqd", s_hq),
                         ("gad", s_ga), ("zresd", s_zres),
                         ("carryd", s_carry), ("acold", s_acol),
                         ("t1d", s_t1), ("statd", s_stat),
                         ("xtmd", s_xtm), ("xntmd", s_xntm)]
                for nm, t in pairs:
                    tick("st", gp.dma_start(out=dbg[nm], in_=t[:]), 16)
                tick("st", gp.dma_start(out=dbg["scrw1d"],
                                        in_=scr_w1[5, :, :]), 16)
                tick("st", gp.dma_start(out=dbg["scrw2d"],
                                        in_=scr_w2[3, :, :]), 16)

        @block.gpsimd
        def _(gp):
            if L.prev is not None:
                gp.wait_ge(sems["st"], L.cnt["st"])

    return nc


def build_program():
    led1 = Ledger(None)
    nc1 = bass.Bass("TRN2", target_bir_lowering=False,
                    detect_race_conditions=False)
    emit(nc1, led1)
    led2 = Ledger(led1.marks)
    nc2 = bass.Bass("TRN2", target_bir_lowering=False,
                    detect_race_conditions=False)
    emit(nc2, led2)
    assert led1.cnt == led2.cnt, (led1.cnt, led2.cnt)
    assert led1.marks == led2.marks
    return nc2


_NC_CACHE = None


def kernel(**inputs):
    global _NC_CACHE
    if _NC_CACHE is None:
        _NC_CACHE = build_program()
    nc = _NC_CACHE

    x = np.ascontiguousarray(inputs["L0_attn_x"], dtype=np.float32)
    base = {
        "conv_w": np.ascontiguousarray(
            inputs["L0_attn_conv_w"].reshape(D, DCONV), np.float32),
        "conv_b": np.ascontiguousarray(inputs["L0_attn_conv_b"], np.float32),
        "A": np.ascontiguousarray(inputs["L0_attn_A"], np.float32),
        "Bp_w": np.ascontiguousarray(inputs["L0_attn_Bp_w"], np.float32),
        "Bp_b": np.ascontiguousarray(inputs["L0_attn_Bp_b"], np.float32),
        "Cp_w": np.ascontiguousarray(inputs["L0_attn_Cp_w"], np.float32),
        "Cp_b": np.ascontiguousarray(inputs["L0_attn_Cp_b"], np.float32),
        "Dv": np.ascontiguousarray(inputs["L0_attn_D"], np.float32),
        "gate_w": np.ascontiguousarray(inputs["L0_attn_gate_w"], np.float32),
        "gate_b": np.ascontiguousarray(inputs["L0_attn_gate_b"], np.float32),
        "out_w": np.ascontiguousarray(inputs["L0_attn_out_w"], np.float32),
        "out_b": np.ascontiguousarray(inputs["L0_attn_out_b"], np.float32),
        "ln_g": np.ascontiguousarray(inputs["L0_attn_ln_g"], np.float32),
        "ln_b": np.ascontiguousarray(inputs["L0_attn_ln_b"], np.float32),
        "w1": np.ascontiguousarray(inputs["L0_ffn_w1"], np.float32),
        "b1": np.ascontiguousarray(inputs["L0_ffn_b1"], np.float32),
        "w2": np.ascontiguousarray(inputs["L0_ffn_w2"], np.float32),
        "b2": np.ascontiguousarray(inputs["L0_ffn_b2"], np.float32),
    }
    in_maps = []
    for ci in range(NCORES):
        m = dict(base)
        m["x"] = np.ascontiguousarray(
            x[ci * BLOC:(ci + 1) * BLOC].reshape(T, D))
        in_maps.append(m)

    res = run_bass_kernel_spmd(nc, in_maps, core_ids=list(range(NCORES)))
    out = np.empty((16, SEQ, D), np.float32)
    for ci in range(NCORES):
        out[ci * BLOC:(ci + 1) * BLOC] = res.results[ci]["out"].reshape(
            BLOC, SEQ, D)
    return out


# version 08
# speedup vs baseline: 1.1881x; 1.0610x over previous
import contextlib
import numpy as np

import concourse.bass as bass
import concourse.mybir as mybir
from concourse.bass_utils import run_bass_kernel_spmd

F32 = mybir.dt.float32
BF16 = mybir.dt.bfloat16
AF = mybir.ActivationFunctionType
ALU = mybir.AluOpType
AX = mybir.AxisListType

NCORES = 8
SEQ = 2048
BLOC = 2
T = SEQ * BLOC
D = 1024
H = 4096
NST = 16
DCONV = 4
LN_EPS = 1e-5
G = D // 128
HT = H // 128
TC = 512
NCH = T // TC
NSUB = TC // 128
RCONST = float(3 << 22)
DEBUG_TAPS = False


SEMKEYS = ("sy", "gp", "te", "ve", "sc", "params", "x", "st",
           "gs0", "gs1", "os0", "os1", "w10", "w11", "w20", "w21",
           "wp0", "wp1", "scr0", "scr1")


class Ledger:
    def __init__(self, prev_marks):
        self.cnt = {k: 0 for k in SEMKEYS}
        self.marks = {}
        self.prev = prev_marks

    def tick(self, sems, key, inst, k=1):
        inst.then_inc(sems[key], k)
        self.cnt[key] += k

    def mark(self, name, key):
        assert name not in self.marks, name
        self.marks[name] = (key, self.cnt[key])

    def wait(self, sems, eng, name):
        if self.prev is None:
            return
        key, val = self.prev[name]
        if val > 0:
            eng.wait_ge(sems[key], val)


def emit(nc, led):
    L = led

    dx = nc.dram_tensor("x", [T, D], F32, kind="ExternalInput").ap()
    dconv_w = nc.dram_tensor("conv_w", [D, DCONV], F32, kind="ExternalInput").ap()
    dconv_b = nc.dram_tensor("conv_b", [D], F32, kind="ExternalInput").ap()
    dA = nc.dram_tensor("A", [D, NST], F32, kind="ExternalInput").ap()
    dBpw = nc.dram_tensor("Bp_w", [NST, D], F32, kind="ExternalInput").ap()
    dBpb = nc.dram_tensor("Bp_b", [NST], F32, kind="ExternalInput").ap()
    dCpw = nc.dram_tensor("Cp_w", [NST, D], F32, kind="ExternalInput").ap()
    dCpb = nc.dram_tensor("Cp_b", [NST], F32, kind="ExternalInput").ap()
    dD = nc.dram_tensor("Dv", [D], F32, kind="ExternalInput").ap()
    dgw = nc.dram_tensor("gate_w", [D, D], F32, kind="ExternalInput").ap()
    dgb = nc.dram_tensor("gate_b", [D], F32, kind="ExternalInput").ap()
    dow = nc.dram_tensor("out_w", [D, D], F32, kind="ExternalInput").ap()
    dob = nc.dram_tensor("out_b", [D], F32, kind="ExternalInput").ap()
    dlng = nc.dram_tensor("ln_g", [D], F32, kind="ExternalInput").ap()
    dlnb = nc.dram_tensor("ln_b", [D], F32, kind="ExternalInput").ap()
    dw1 = nc.dram_tensor("w1", [H, D], F32, kind="ExternalInput").ap()
    db1 = nc.dram_tensor("b1", [H], F32, kind="ExternalInput").ap()
    dw2 = nc.dram_tensor("w2", [D, H], F32, kind="ExternalInput").ap()
    db2 = nc.dram_tensor("b2", [D], F32, kind="ExternalInput").ap()
    dout = nc.dram_tensor("out", [T, D], F32, kind="ExternalOutput").ap()
    scr_w1 = nc.dram_tensor("scr_w1", [HT, 128, 1024], BF16, kind="Internal").ap()
    scr_w2 = nc.dram_tensor("scr_w2", [G, 128, 4096], BF16, kind="Internal").ap()
    scr_g = nc.dram_tensor("scr_g", [G, 128, 1024], BF16, kind="Internal").ap()
    scr_o = nc.dram_tensor("scr_o", [G, 128, 1024], BF16, kind="Internal").ap()
    dbg = {}
    if DEBUG_TAPS:
        for nm, shape, dt in (
                ("xn16", [128, G * (3 + TC)], BF16),
                ("u16d", [128, G * TC], BF16),
                ("btd", [NST, 2 * TC], BF16),
                ("BCd", [128, 2 * NST * TC], BF16),
                ("xqd", [128, G * TC], BF16),
                ("hqd", [128, HT * TC], BF16),
                ("gad", [128, G * TC], BF16),
                ("zresd", [128, G * TC], F32),
                ("carryd", [128, G * NST], F32),
                ("acold", [128, G * NST], F32),
                ("t1d", [128, 2 * TC], F32),
                ("statd", [128, NSUB * 8], F32),
                ("xtmd", [128, NSUB * D], F32),
                ("xntmd", [128, NSUB * D], F32),
                ("scrw1d", [128, 1024], BF16),
                ("scrw2d", [128, 4096], BF16)):
            dbg[nm] = nc.dram_tensor("dbg_" + nm, shape, dt,
                                     kind="ExternalOutput").ap()

    ctx = contextlib.ExitStack()
    sb = lambda name, shape, dt=F32: ctx.enter_context(
        nc.sbuf_tensor(name, shape, dt))
    ps = lambda name: ctx.enter_context(nc.psum_tensor(name, [128, 512], F32))

    with ctx:
        s_BpCpT = sb("BpCpT", [128, 2 * G * NST], BF16)
        s_acol = sb("acol", [128, G * NST])
        s_diagD = sb("diagD", [128, G * 128], BF16)
        s_convw = sb("convw", [128, G * DCONV])
        s_dconv = sb("dconv", [128, G * DCONV * 128], BF16)
        s_cols = sb("cols", [128, 5 * G])
        OF_CONVB, OF_GATEB, OF_OUTB, OF_LNG, OF_LNB2 = (
            0, G, 2 * G, 3 * G, 4 * G)
        s_b1qC = sb("b1qC", [128, HT])
        s_bpb = sb("bpb", [NST, 2])
        s_id32 = sb("id32", [128, 128])
        s_id16 = sb("id16", [128, 128], BF16)
        s_sel = sb("sel", [NST, NST * 128], BF16)
        s_ones1 = sb("ones1", [1, 128])
        s_negC = sb("negC", [128, 1])
        s_ones16 = sb("ones16c", [128, 1], BF16)
        s_carry = sb("carry", [128, G * NST])
        s_w1s = sb("w1s", [128, 2 * 1024], BF16)
        s_w2s = sb("w2s", [128, 2 * 4096], BF16)
        s_gs = sb("gs", [128, 2 * 1024], BF16)
        s_os = sb("osb", [128, 2 * 1024], BF16)
        s_xtm = sb("xtm", [128, NSUB * D])
        s_xntm = sb("xntm", [128, NSUB * D])
        s_stat = sb("stat", [128, NSUB * 8])
        s_xq16 = sb("xq16", [128, D], BF16)
        s_xn16 = sb("xn16", [128, G * (3 + TC)], BF16)
        s_xq = sb("xq", [128, G * TC], BF16)
        s_u16 = sb("u16", [128, G * TC], BF16)
        s_bt = sb("bt", [NST, 2 * TC], BF16)
        s_BC = sb("BC", [128, 2 * NST * TC], BF16)
        s_w = sb("wbuf", [128, TC], BF16)
        s_s = sb("sbuf_s", [128, 2 * TC])
        s_z = sb("zbuf", [128, 2 * TC], BF16)
        s_ga = sb("ga", [128, G * TC], BF16)
        s_zres = sb("zres", [128, G * TC])
        s_z16 = sb("z16", [128, 2 * TC], BF16)
        s_hq = sb("hq", [128, HT * TC], BF16)
        s_t1 = sb("t1", [128, 2 * TC])
        pTA, pTB = ps("pTA"), ps("pTB")
        pMA, pMB = ps("pMA"), ps("pMB")
        pBCa, pBCb = ps("pBCa"), ps("pBCb")
        pYA, pYB = ps("pYA"), ps("pYB")

        sems = {k: ctx.enter_context(nc.semaphore("sem_" + k))
                for k in SEMKEYS}
        block = ctx.enter_context(nc.Block())

        def tick(key, inst, k=1):
            L.tick(sems, key, inst, k)

        def mark(name, key):
            L.mark(name, key)

        def wt(eng, name):
            L.wait(sems, eng, name)

        import contextlib as _cl

        def scoped(name):
            return nc.named_scope(name)

        @block.gpsimd
        def _(gp):
            def ss(inst):
                tick("gp", inst)
                gp.wait_ge(sems["gp"], L.cnt["gp"])

            ss(gp.memset(s_id32[:], 0.0))
            ss(nc.gpsimd.affine_select(
                out=s_id32[:], in_=s_id32[:], compare_op=ALU.not_equal,
                fill=1.0, base=0, pattern=[[-1, 128]], channel_multiplier=1))
            ss(gp.memset(s_id16[:], 0.0))
            ss(nc.gpsimd.affine_select(
                out=s_id16[:], in_=s_id16[:], compare_op=ALU.not_equal,
                fill=1.0, base=0, pattern=[[-1, 128]], channel_multiplier=1))
            ss(gp.memset(s_sel[:], 0.0))
            sel3 = s_sel.rearrange("p (n f) -> p n f", f=128)
            ss(nc.gpsimd.affine_select(
                out=sel3, in_=sel3, compare_op=ALU.not_equal,
                fill=1.0, base=0, pattern=[[-1, NST], [0, 128]],
                channel_multiplier=1))
            gp.memset(s_ones1[:], 1.0)
            gp.memset(s_negC[:], -RCONST)
            gp.memset(s_ones16[:], 1.0)
            gp.memset(s_carry[:], 0.0)
            tick("gp", gp.memset(s_xn16[:], 0.0))
            mark("gp_const", "gp")

        @block.sync
        def _(sy):
            for g in range(G):
                tick("params", sy.dma_start(
                    out=s_convw[:, g * DCONV:(g + 1) * DCONV],
                    in_=dconv_w[g * 128:(g + 1) * 128, :]), 16)
                tick("params", sy.dma_start(
                    out=s_cols[:, OF_CONVB + g:OF_CONVB + g + 1],
                    in_=dconv_b[g * 128:(g + 1) * 128][:, None]), 16)
                tick("params", sy.dma_start(
                    out=s_cols[:, OF_GATEB + g:OF_GATEB + g + 1],
                    in_=dgb[g * 128:(g + 1) * 128][:, None]), 16)
                tick("params", sy.dma_start(
                    out=s_cols[:, OF_OUTB + g:OF_OUTB + g + 1],
                    in_=dob[g * 128:(g + 1) * 128][:, None]), 16)
                tick("params", sy.dma_start(
                    out=s_cols[:, OF_LNG + g:OF_LNG + g + 1],
                    in_=dlng[g * 128:(g + 1) * 128][:, None]), 16)
                tick("params", sy.dma_start(
                    out=s_acol[:, g * NST:(g + 1) * NST],
                    in_=dA[g * 128:(g + 1) * 128, :]), 16)
                tick("params", sy.dma_start(
                    out=s_stat[:, g:g + 1],
                    in_=dD[g * 128:(g + 1) * 128][:, None]), 16)
                tick("params", sy.dma_start(
                    out=s_stat[:, 8 + g:8 + g + 1],
                    in_=dlnb[g * 128:(g + 1) * 128][:, None]), 16)
            tick("params", sy.dma_start(out=s_bpb[:, 0:1], in_=dBpb[:, None]), 16)
            tick("params", sy.dma_start(out=s_bpb[:, 1:2], in_=dCpb[:, None]), 16)
            tick("params", sy.dma_start(
                out=s_zres[0:32, 0:128],
                in_=db1.rearrange("(a b) -> a b", b=128)), 16)
            tick("params", sy.dma_start(
                out=s_zres[0:8, 200:328],
                in_=db2.rearrange("(a b) -> a b", b=128)), 16)
            tick("params", sy.dma_start(
                out=s_zres[0:NST, 400:400 + D], in_=dBpw[:, :]), 16)
            tick("params", sy.dma_start(
                out=s_zres[0:NST, 1500:1500 + D], in_=dCpw[:, :]), 16)
            mark("sy_params", "params")

        @block.scalar
        def _(sc):
            wt(sc, "sy_params")
            tick("sc", sc.activation(s_acol[:], s_acol[:], AF.Exp))
            sc.wait_ge(sems["sc"], L.cnt["sc"])
            tick("sc", sc.activation(s_acol[:], s_acol[:], AF.Exp, scale=-1.0))
            mark("sc_acol", "sc")

        @block.tensor
        def _(te):
            wt(te, "sy_params")
            wt(te, "gp_const")
            te.transpose(pTA[:, 0:32], s_zres[0:32, 0:128],
                         s_id32[0:32, 0:32])
            te.transpose(pTA[:, 32:40], s_zres[0:8, 200:328],
                         s_id32[0:8, 0:8])
            for g in range(G):
                te.transpose(pTB[:, g * 16:(g + 1) * 16],
                             s_zres[0:NST, 400 + g * 128:400 + (g + 1) * 128],
                             s_id32[0:NST, 0:NST])
            for g in range(G):
                _last = te.transpose(pTB[:, 128 + g * 16:128 + (g + 1) * 16],
                             s_zres[0:NST, 1500 + g * 128:1500 + (g + 1) * 128],
                             s_id32[0:NST, 0:NST])
            tick("te", _last)
            mark("te_prep1", "te")

        @block.vector
        def _(ve):
            def vst(inst):
                tick("ve", inst)
                ve.wait_ge(sems["ve"], L.cnt["ve"])

            wt(ve, "te_prep1")
            wt(ve, "sy_params")
            wt(ve, "gp_const")
            for g in range(G):
                ve.tensor_scalar(s_diagD[:, g * 128:(g + 1) * 128],
                                 s_id16[:], s_stat[:, g:g + 1], None, ALU.mult)
            for g in range(G):
                for k in range(DCONV):
                    i = g * DCONV + k
                    ve.tensor_scalar(s_dconv[:, i * 128:(i + 1) * 128],
                                     s_id16[:], s_convw[:, i:i + 1],
                                     None, ALU.mult)
            vst(ve.tensor_scalar(s_b1qC[:], pTA[:, 0:HT], 64.0, RCONST,
                                 ALU.mult, ALU.add))
            vst(ve.tensor_scalar(s_b1qC[:], s_b1qC[:], RCONST + 127.0,
                                 RCONST - 128.0, ALU.min, ALU.max))
            vst(ve.tensor_scalar(s_b1qC[:], s_b1qC[:], RCONST, None,
                                 ALU.subtract))
            vst(ve.tensor_scalar(s_b1qC[:], s_b1qC[:], 0.25, None, ALU.mult))
            vst(ve.tensor_scalar(s_cols[:, OF_LNB2:OF_LNB2 + 8],
                                 pTA[:, 32:40], 64.0, RCONST,
                                 ALU.mult, ALU.add))
            vst(ve.tensor_scalar(s_cols[:, OF_LNB2:OF_LNB2 + 8],
                                 s_cols[:, OF_LNB2:OF_LNB2 + 8],
                                 RCONST + 127.0, RCONST - 128.0,
                                 ALU.min, ALU.max))
            vst(ve.tensor_scalar(s_cols[:, OF_LNB2:OF_LNB2 + 8],
                                 s_cols[:, OF_LNB2:OF_LNB2 + 8],
                                 RCONST, None, ALU.subtract))
            vst(ve.scalar_tensor_tensor(
                s_cols[:, OF_LNB2:OF_LNB2 + 8],
                s_cols[:, OF_LNB2:OF_LNB2 + 8], 1.0 / 64.0,
                s_stat[:, 8:16], ALU.mult, ALU.add))
            ve.tensor_copy(s_BpCpT[:, 0:G * NST], pTB[:, 0:G * NST])
            tick("ve", ve.tensor_copy(s_BpCpT[:, G * NST:2 * G * NST],
                                      pTB[:, 128:128 + G * NST]))
            mark("ve_prep1", "ve")

        wjobs = []
        for (src, dst, nrt, rowlen, mode) in (
                (dw1, scr_w1, HT, D, "q64"),
                (dw2, scr_w2, G, H, "q64"),
                (dgw, scr_g, G, D, "cast"),
                (dow, scr_o, G, D, "cast")):
            for j in range(nrt):
                wjobs.append((src, dst, j, rowlen, mode))
        NJ = len(wjobs)

        def wp_in(i, rowlen):
            if rowlen == H:
                return s_zres[:, 0:H]
            return s_xtm[:, (i % 2) * 2048:(i % 2) * 2048 + rowlen]

        def wp_t1(i, rowlen):
            if rowlen == H:
                return s_xntm[:, 0:H]
            return s_xntm[:, (i % 2) * 2048:(i % 2) * 2048 + rowlen]

        @block.sync
        def _(sy):
            for i, (src, dst, j, rowlen, mode) in enumerate(wjobs):
                if rowlen == H:
                    wt(sy, "te_prep1")
                    if i >= 1:
                        wt(sy, f"wp_read_{i-1}")
                if i >= 2:
                    wt(sy, f"wp_read_{i-2}")
                tick(f"wp{i % 2}", sy.dma_start(
                    out=wp_in(i, rowlen),
                    in_=src[j * 128:(j + 1) * 128, :]), 16)
                mark(f"wp_dma_{i}", f"wp{i % 2}")

        @block.scalar
        def _(sc):
            for i, (src, dst, j, rowlen, mode) in enumerate(wjobs):
                par = i % 2
                wt(sc, f"wp_dma_{i}")
                src_sb = wp_in(i, rowlen)
                if mode == "q64":
                    if i >= 1:
                        wt(sc, f"wp_clip_{i-1}")
                    if i >= 2:
                        wt(sc, f"wp_clip_{i-2}")
                    tick("sc", sc.activation(
                        wp_t1(i, rowlen), src_sb,
                        AF.Copy, bias=RCONST, scale=64.0))
                else:
                    if i >= 2:
                        wt(sc, f"wp_te_{i-2}")
                    tick("sc", sc.activation(
                        s_hq[:, par * 4096:par * 4096 + rowlen], src_sb,
                        AF.Copy))
                mark(f"wp_read_{i}", "sc")



        @block.tensor
        def _(te):
            gidx = 0
            for i, (src, dst, j, rowlen, mode) in enumerate(wjobs):
                par = i % 2
                if mode == "q64":
                    wt(te, f"wp_clip_{i}")
                else:
                    wt(te, f"wp_read_{i}")
                if i >= 2:
                    wt(te, f"wp_stage_{i-2}")
                int_sb = s_hq[:, par * 4096:par * 4096 + rowlen]
                nk = rowlen // 128
                for k0 in range(0, nk, 4):
                    pk = pTA if gidx % 2 == 0 else pTB
                    pk16 = pk[:].bitcast(BF16)
                    if gidx < 2:
                        wt(te, "ve_prep1")
                    if gidx >= 2:
                        wt(te, f"wp_ev_{gidx-2}")
                    kn = min(4, nk - k0)
                    for kk in range(kn):
                        _last = te.transpose(pk16[:, kk * 128:(kk + 1) * 128],
                                     int_sb[:, (k0 + kk) * 128:
                                            (k0 + kk + 1) * 128],
                                     s_id16[:])
                    tick("te", _last)
                    mark(f"wp_tp_{gidx}", "te")
                    gidx += 1
                mark(f"wp_te_{i}", "te")

        item_gidx = []
        _g = 0
        for (_s, _d, _j, _rl, _m) in wjobs:
            ng = (_rl // 128 + 3) // 4
            item_gidx.append((_g, ng))
            _g += ng

        @block.vector
        def _(ve):
            def emit_clip(i):
                srcj, dstj, jj, rowlen, mode = wjobs[i]
                par = i % 2
                if mode != "q64":
                    mark(f"wp_clip_{i}", "ve")
                    return
                wt(ve, f"wp_read_{i}")
                if i >= 2:
                    wt(ve, f"wp_te_{i-2}")
                t1 = wp_t1(i, rowlen)
                ve.tensor_scalar(t1, t1, RCONST + 127.0, RCONST - 128.0,
                                 ALU.min, ALU.max)
                tick("ve", ve.tensor_scalar(
                    s_hq[:, par * 4096:par * 4096 + rowlen], t1,
                    RCONST, None, ALU.subtract))
                mark(f"wp_clip_{i}", "ve")

            def emit_evac(i):
                srcj, dstj, jj, rowlen, mode = wjobs[i]
                par = i % 2
                nk = rowlen // 128
                stg = s_w2s if rowlen == H else s_w1s
                stg_off = par * rowlen
                if i >= 2:
                    wt(ve, f"wp_scr_{i-2}")
                g0, ng = item_gidx[i]
                for gg in range(ng):
                    gidx = g0 + gg
                    k0 = gg * 4
                    wt(ve, f"wp_tp_{gidx}")
                    pk16 = (pTA if gidx % 2 == 0 else pTB)[:].bitcast(BF16)
                    kn = min(4, nk - k0)
                    _last = ve.tensor_copy(
                        stg[:, stg_off + k0 * 128:stg_off + (k0 + kn) * 128],
                        pk16[:, 0:kn * 128])
                    tick("ve", _last)
                    mark(f"wp_ev_{gidx}", "ve")
                mark(f"wp_stage_{i}", "ve")

            for i in range(NJ + 1):
                if i < NJ:
                    emit_clip(i)
                if i >= 1:
                    emit_evac(i - 1)

        @block.gpsimd
        def _(gp):
            for i, (src, dst, j, rowlen, mode) in enumerate(wjobs):
                par = i % 2
                wt(gp, f"wp_stage_{i}")
                stg = s_w2s if rowlen == H else s_w1s
                tick(f"scr{i % 2}", gp.dma_start(
                    out=dst[j, :, :],
                    in_=stg[:, par * rowlen:(par + 1) * rowlen]), 16)
                mark(f"wp_scr_{i}", f"scr{i % 2}")

        for k in SEMKEYS:
            mark(f"prep_end_{k}", k)

        for c in range(NCH):
            seq_start = (c % (NCH // BLOC) == 0)
            t0 = c * TC
            P = f"c{c}_"
            prev = f"c{c-1}_" if c > 0 else "prep_end_"

            def barrier(eng, names=("gp", "te", "ve", "sc", "st")):
                for k in names:
                    wt(eng, prev + k if c > 0 else f"prep_end_{k}")
                if c == 0:
                    wt(eng, "prep_end_scr0")
                    wt(eng, "prep_end_scr1")

            @block.sync
            def _(sy):
                barrier(sy)
                for q in range(NSUB):
                    tick("x", sy.dma_start(
                        out=s_xtm[:, q * D:(q + 1) * D],
                        in_=dx[t0 + q * 128:t0 + (q + 1) * 128, :]), 16)
                mark(P + "x", "x")
                for o in range(G):
                    if o >= 2:
                        wt(sy, P + f"gate_mm_{o-2}")
                    tick(f"gs{o % 2}", sy.dma_start(
                        out=s_gs[:, (o % 2) * 1024:(o % 2 + 1) * 1024],
                        in_=scr_g[o, :, :]), 16)
                    mark(P + f"gs_{o}", f"gs{o % 2}")
                for j in range(HT):
                    if j >= 2:
                        wt(sy, P + f"ffn_mm1_{j-2}")
                    tick(f"w1{j % 2}", sy.dma_start(
                        out=s_w1s[:, (j % 2) * 1024:(j % 2 + 1) * 1024],
                        in_=scr_w1[j, :, :]), 16)
                    mark(P + f"w1_{j}", f"w1{j % 2}")
                for o in range(G):
                    if o >= 2:
                        wt(sy, P + f"out_mm_{o-2}")
                    tick(f"os{o % 2}", sy.dma_start(
                        out=s_os[:, (o % 2) * 1024:(o % 2 + 1) * 1024],
                        in_=scr_o[o, :, :]), 16)
                    mark(P + f"os_{o}", f"os{o % 2}")
                for o in range(G):
                    if o >= 2:
                        wt(sy, P + f"ffn_mm2_{o-2}")
                    tick(f"w2{o % 2}", sy.dma_start(
                        out=s_w2s[:, (o % 2) * 4096:(o % 2 + 1) * 4096],
                        in_=scr_w2[o, :, :]), 16)
                    mark(P + f"w2_{o}", f"w2{o % 2}")

            @block.scalar
            def _(sc):
                barrier(sc)
                wt(sc, P + "x")
                for q in range(NSUB):
                    sc.activation(
                        s_xntm[:, q * D:(q + 1) * D],
                        s_xtm[:, q * D:(q + 1) * D], AF.Copy,
                        accum_out=s_stat[:, q * 8:q * 8 + 1])
                    _last = sc.activation(
                        s_xntm[:, q * D:(q + 1) * D],
                        s_xtm[:, q * D:(q + 1) * D], AF.Square,
                        accum_out=s_stat[:, q * 8 + 1:q * 8 + 2])
                tick("sc", _last)
                mark(P + "ln1_sum", "sc")
                mark(P + "ln1_sq", "sc")

            st3 = s_stat.rearrange("p (q k) -> p q k", k=8)

            @block.vector
            def _(ve):
                def vst(inst):
                    tick("ve", inst)
                    ve.wait_ge(sems["ve"], L.cnt["ve"])

                barrier(ve)
                wt(ve, P + "ln1_sq")
                ve.tensor_scalar(st3[:, :, 2:3], st3[:, :, 0:1], 1.0 / D,
                                 None, ALU.mult)
                vst(ve.tensor_scalar(st3[:, :, 3:4], st3[:, :, 1:2], 1.0 / D,
                                     None, ALU.mult))
                vst(ve.tensor_tensor(st3[:, :, 4:5], st3[:, :, 2:3],
                                     st3[:, :, 2:3], ALU.mult))
                vst(ve.tensor_tensor(st3[:, :, 5:6], st3[:, :, 3:4],
                                     st3[:, :, 4:5], ALU.subtract))
                tick("ve", ve.tensor_scalar(st3[:, :, 5:6], st3[:, :, 5:6],
                                            LN_EPS, None, ALU.add))
                mark(P + "ln1_var", "ve")

            @block.scalar
            def _(sc):
                wt(sc, P + "ln1_var")
                tick("sc", sc.activation(st3[:, :, 5:6], st3[:, :, 5:6],
                                         AF.Sqrt))
                mark(P + "ln1_sd", "sc")

            @block.vector
            def _(ve):
                def vst(inst):
                    tick("ve", inst)
                    ve.wait_ge(sems["ve"], L.cnt["ve"])

                wt(ve, P + "ln1_sd")
                vst(ve.reciprocal(st3[:, :, 6:7], st3[:, :, 5:6]))
                vst(ve.tensor_tensor(st3[:, :, 7:8], st3[:, :, 2:3],
                                     st3[:, :, 6:7], ALU.mult))
                vst(ve.tensor_scalar(st3[:, :, 7:8], st3[:, :, 7:8],
                                     -1.0, None, ALU.mult))
                for q in range(NSUB):
                    _last = ve.tensor_scalar(s_xntm[:, q * D:(q + 1) * D],
                                             s_xtm[:, q * D:(q + 1) * D],
                                             st3[:, q, 6:7], st3[:, q, 7:8],
                                             ALU.mult, ALU.add)
                tick("ve", _last)
                mark(P + "ln1_ap", "ve")

            tgroups = []
            for q in range(NSUB):
                for srcn in ("xn", "xq"):
                    for half in range(2):
                        tgroups.append((q, srcn, half))

            @block.tensor
            def _(te):
                for gi, (q, srcn, half) in enumerate(tgroups):
                    pk = pTA if gi % 2 == 0 else pTB
                    if gi >= 2:
                        wt(te, P + f"tpev_{gi-2}")
                    if srcn == "xn":
                        wt(te, P + "ln1_ap")
                        for gg in range(4):
                            g = half * 4 + gg
                            _last = te.transpose(
                                pk[:, gg * 128:(gg + 1) * 128],
                                s_xntm[:, q * D + g * 128:
                                       q * D + (g + 1) * 128],
                                s_id32[:])
                    else:
                        wt(te, P + f"xqclip_{q}")
                        pk16 = pk[:].bitcast(BF16)
                        for gg in range(4):
                            g = half * 4 + gg
                            _last = te.transpose(
                                pk16[:, gg * 128:(gg + 1) * 128],
                                s_xq16[:, g * 128:(g + 1) * 128],
                                s_id16[:])
                    tick("te", _last)
                    mark(P + f"tp_{srcn}_{q}_{half}", "te")

            @block.scalar
            def _(sc):
                def xq1(q):
                    if q >= 1:
                        wt(sc, P + f"xqclip_{q-1}")
                    tick("sc", sc.activation(
                        s_t1[:, 0:D], s_xntm[:, q * D:(q + 1) * D], AF.Copy,
                        bias=RCONST, scale=16.0))
                    mark(P + f"xq1_{q}", "sc")

                xq1(0)
                xq1(1)
                xn3 = s_xn16.rearrange("p (g t) -> p g t", t=3 + TC)
                xq3 = s_xq.rearrange("p (g t) -> p g t", t=TC)
                for gi, (q, srcn, half) in enumerate(tgroups):
                    pk = pTA if gi % 2 == 0 else pTB
                    wt(sc, P + f"tp_{srcn}_{q}_{half}")
                    if srcn == "xn":
                        dst = xn3[:, half * 4:(half + 1) * 4,
                                  3 + q * 128:3 + (q + 1) * 128]
                        _last = sc.copy(
                            dst, pk[:].rearrange("p (g t) -> p g t", t=128))
                    else:
                        dst = xq3[:, half * 4:(half + 1) * 4,
                                  q * 128:(q + 1) * 128]
                        pk16 = pk[:].bitcast(BF16)
                        _last = sc.copy(
                            dst, pk16[:, 0:512].rearrange(
                                "p (g t) -> p g t", t=128))
                    tick("sc", _last)
                    mark(P + f"tpev_{gi}", "sc")
                    if srcn == "xq" and half == 1 and q + 2 < NSUB:
                        xq1(q + 2)
                mark(P + "xn_ready", "sc")
                mark(P + "xq_ready", "sc")

            @block.vector
            def _(ve):
                for q in range(NSUB):
                    wt(ve, P + f"xq1_{q}")
                    if q >= 1:
                        wt(ve, P + f"tp_xq_{q-1}_1")
                    ve.tensor_scalar(s_t1[:, 0:D], s_t1[:, 0:D],
                                     RCONST + 127.0, RCONST - 128.0,
                                     ALU.min, ALU.max)
                    tick("ve", ve.tensor_scalar(
                        s_xq16[:, 0:D], s_t1[:, 0:D],
                        RCONST, None, ALU.subtract))
                    mark(P + f"xqclip_{q}", "ve")

            @block.tensor
            def _(te):
                for g in range(G):
                    pk = pTA if g % 2 == 0 else pTB
                    if g == 0:
                        wt(te, P + "tpev_14")
                        wt(te, P + "xn_ready")
                    elif g == 1:
                        wt(te, P + "tpev_15")
                    else:
                        wt(te, P + f"silu_{g-2}")
                    base = g * (3 + TC)
                    for k in range(DCONV):
                        kk = DCONV - 1 - k
                        _last = te.matmul(
                            pk[:],
                            s_dconv[:, (g * DCONV + kk) * 128:
                                    (g * DCONV + kk + 1) * 128],
                            s_xn16[:, base + 3 - k:base + 3 - k + TC],
                            start=(k == 0), stop=(k == DCONV - 1))
                    tick("te", _last)
                    mark(P + f"cv_{g}", "te")

            @block.scalar
            def _(sc):
                for g in range(G):
                    pk = pTA if g % 2 == 0 else pTB
                    wt(sc, P + f"cv_{g}")
                    tick("sc", sc.activation(
                        s_u16[:, g * TC:(g + 1) * TC], pk[:], AF.Silu,
                        bias=s_cols[:, OF_CONVB + g:OF_CONVB + g + 1]))
                    mark(P + f"silu_{g}", "sc")

            @block.tensor
            def _(te):
                for g in range(G):
                    wt(te, P + f"silu_{g}")
                    te.matmul(pYA[0:NST, :],
                              s_BpCpT[:, g * NST:(g + 1) * NST],
                              s_u16[:, g * TC:(g + 1) * TC],
                              start=(g == 0), stop=(g == G - 1))
                for g in range(G):
                    _last = te.matmul(pYB[0:NST, :],
                                      s_BpCpT[:, (G + g) * NST:
                                              (G + g + 1) * NST],
                                      s_u16[:, g * TC:(g + 1) * TC],
                                      start=(g == 0), stop=(g == G - 1))
                tick("te", _last)
                mark(P + "btct_mm", "te")

            @block.vector
            def _(ve):
                wt(ve, P + "btct_mm")
                ve.tensor_scalar(s_bt[0:NST, 0:TC], pYA[0:NST, :],
                                 s_bpb[:, 0:1], None, ALU.add)
                tick("ve", ve.tensor_scalar(s_bt[0:NST, TC:2 * TC],
                                            pYB[0:NST, :],
                                            s_bpb[:, 1:2], None, ALU.add))
                mark(P + "btct_ev", "ve")

            bcbank = [pBCa, pBCb, pYA, pYB]

            @block.tensor
            def _(te):
                wt(te, P + "btct_ev")
                for n in range(NST):
                    for which in range(2):
                        bi = (2 * n + which) % 4
                        gi = 2 * n + which
                        if gi >= 4:
                            wt(te, P + f"bcev_{gi-4}")
                        tick("te", te.matmul(
                            bcbank[bi][:],
                            s_sel[:, n * 128:(n + 1) * 128],
                            s_bt[0:NST, which * TC:(which + 1) * TC],
                            start=True, stop=True))
                        mark(P + f"bcmm_{gi}", "te")

            @block.scalar
            def _(sc):
                for n in range(NST):
                    for which in range(2):
                        gi = 2 * n + which
                        bi = gi % 4
                        wt(sc, P + f"bcmm_{gi}")
                        tick("sc", sc.copy(
                            s_BC[:, (which * NST + n) * TC:
                                 (which * NST + n + 1) * TC],
                            bcbank[bi][:]))
                        mark(P + f"bcev_{gi}", "sc")
                mark(P + "bc_ready", "sc")

            @block.tensor
            def _(te):
                for o in range(G):
                    wt(te, P + f"gs_{o}")
                    wt(te, P + "xn_ready")
                    if o >= 2:
                        wt(te, P + f"gate_ev_{o-2}")
                    pk = pMA if o % 2 == 0 else pMB
                    for k in range(G):
                        _last = te.matmul(
                            pk[:],
                            s_gs[:, (o % 2) * 1024 + k * 128:
                                 (o % 2) * 1024 + (k + 1) * 128],
                            s_xn16[:, k * (3 + TC) + 3:
                                   k * (3 + TC) + 3 + TC],
                            start=(k == 0), stop=(k == G - 1))
                    tick("te", _last)
                    mark(P + f"gate_mm_{o}", "te")

            @block.scalar
            def _(sc):
                for o in range(G):
                    pk = pMA if o % 2 == 0 else pMB
                    wt(sc, P + f"gate_mm_{o}")
                    tick("sc", sc.activation(
                        s_ga[:, o * TC:(o + 1) * TC], pk[:], AF.Silu,
                        bias=s_cols[:, OF_GATEB + o:OF_GATEB + o + 1]))
                    mark(P + f"gate_ev_{o}", "sc")

            fbank = [pMA, pMB, pTA, pTB]

            @block.vector
            def _(ve):
                def emit_yg(g):
                    pk = pYA if g % 2 == 0 else pYB
                    wt(ve, P + f"ymm_{g * NST + NST - 1}")
                    wt(ve, P + f"gate_ev_{g}")
                    tick("ve", ve.tensor_tensor(
                        s_ga[:, g * TC:(g + 1) * TC], pk[:],
                        s_ga[:, g * TC:(g + 1) * TC], ALU.mult))
                    mark(P + f"yg_{g}", "ve")

                def emit_h(j):
                    par = j % 2
                    bk = fbank[j % 4]
                    wt(ve, P + f"ffn_mm1_{j}")
                    if j >= 2:
                        wt(ve, P + f"ffn_h3_{j-2}")
                    ve.scalar_tensor_tensor(
                        s_t1[:, par * TC:(par + 1) * TC], bk[:], 1.0 / 64.0,
                        s_b1qC[:, j:j + 1].broadcast_to([128, TC]),
                        ALU.mult, ALU.add)
                    tick("ve", ve.tensor_scalar(
                        s_t1[:, par * TC:(par + 1) * TC],
                        s_t1[:, par * TC:(par + 1) * TC],
                        RCONST, RCONST + 127.0, ALU.add, ALU.min))
                    mark(P + f"ffn_h2_{j}", "ve")

                wt(ve, P + "bc_ready")
                for g in range(G):
                    for n in range(NST):
                        idx = g * NST + n
                        half = n % 2
                        zsl = (s_z if (idx // 2) % 2 == 0 else s_z16)
                        if idx >= 4:
                            wt(ve, P + f"ymm_{idx-4}")
                        ve.tensor_tensor(s_w[:],
                                         s_u16[:, g * TC:(g + 1) * TC],
                                         s_BC[:, n * TC:(n + 1) * TC],
                                         ALU.mult)
                        ci = g * NST + n
                        init = (0.0 if seq_start
                                else s_carry[:, ci:ci + 1])
                        ve.tensor_tensor_scan(
                            s_s[:, half * TC:(half + 1) * TC],
                            s_acol[:, ci:ci + 1].broadcast_to([128, TC]),
                            s_w[:], init, ALU.mult, ALU.add)
                        tick("ve", ve.tensor_tensor(
                            zsl[:, half * TC:(half + 1) * TC],
                            s_s[:, half * TC:(half + 1) * TC],
                            s_BC[:, (NST + n) * TC:(NST + n + 1) * TC],
                            ALU.mult))
                        mark(P + f"z_{idx}", "ve")
                        ve.tensor_copy(
                            s_carry[:, ci:ci + 1],
                            s_s[:, (half + 1) * TC - 1:(half + 1) * TC])
                    if g >= 1:
                        emit_yg(g - 1)
                        for j in range(4 * (g - 1), 4 * g):
                            emit_h(j)
                emit_yg(G - 1)
                for j in range(4 * (G - 1), 4 * G):
                    emit_h(j)

            @block.tensor
            def _(te):
                for g in range(G):
                    pk = pYA if g % 2 == 0 else pYB
                    if g < 2:
                        wt(te, P + f"bcev_{NST * 2 - 2 + g}")
                    else:
                        wt(te, P + f"yg_{g-2}")
                    te.matmul(pk[:], s_diagD[:, g * 128:(g + 1) * 128],
                              s_u16[:, g * TC:(g + 1) * TC],
                              start=True, stop=False)
                    for n in range(NST):
                        idx = g * NST + n
                        half = n % 2
                        zsl = (s_z if (idx // 2) % 2 == 0 else s_z16)
                        wt(te, P + f"z_{idx}")
                        tick("te", te.matmul(
                            pk[:], s_id16[:],
                            zsl[:, half * TC:(half + 1) * TC],
                            start=False, stop=(n == NST - 1)))
                        mark(P + f"ymm_{idx}", "te")
                    for j in range(4 * g, 4 * g + 4):
                        wt(te, P + f"w1_{j}")
                        if j == 0:
                            wt(te, P + "xq_ready")
                            wt(te, P + f"gate_ev_{G-2}")
                        elif j == 1:
                            wt(te, P + f"gate_ev_{G-1}")
                        elif j == 2:
                            wt(te, P + f"silu_{G-2}")
                        elif j == 3:
                            wt(te, P + f"silu_{G-1}")
                        else:
                            wt(te, P + f"ffn_h2_{j-4}")
                        bk = fbank[j % 4]
                        for k in range(G):
                            _last = te.matmul(
                                bk[:],
                                s_w1s[:, (j % 2) * 1024 + k * 128:
                                      (j % 2) * 1024 + (k + 1) * 128],
                                s_xq[:, k * TC:(k + 1) * TC],
                                start=(k == 0), stop=(k == G - 1))
                        tick("te", _last)
                        mark(P + f"ffn_mm1_{j}", "te")

            @block.scalar
            def _(sc):
                for j in range(HT):
                    par = j % 2
                    wt(sc, P + f"ffn_h2_{j}")
                    tick("sc", sc.activation(
                        s_hq[:, j * TC:(j + 1) * TC],
                        s_t1[:, par * TC:(par + 1) * TC], AF.Relu,
                        bias=s_negC[:]))
                    mark(P + f"ffn_h3_{j}", "sc")

            @block.tensor
            def _(te):
                for o in range(G):
                    wt(te, P + f"os_{o}")
                    if o == 0:
                        for g in range(G):
                            wt(te, P + f"yg_{g}")
                        wt(te, P + f"ffn_h2_{HT-4}")
                    elif o == 1:
                        wt(te, P + f"ffn_h2_{HT-3}")
                    else:
                        wt(te, P + f"res_{o-2}")
                    pk = pMA if o % 2 == 0 else pMB
                    for k in range(G):
                        _last = te.matmul(
                            pk[:],
                            s_os[:, (o % 2) * 1024 + k * 128:
                                 (o % 2) * 1024 + (k + 1) * 128],
                            s_ga[:, k * TC:(k + 1) * TC],
                            start=(k == 0), stop=(k == G - 1))
                    tick("te", _last)
                    mark(P + f"out_mm_{o}", "te")

            @block.vector
            def _(ve):
                for o in range(G):
                    pk = pMA if o % 2 == 0 else pMB
                    wt(ve, P + f"out_mm_{o}")
                    tick("ve", ve.scalar_tensor_tensor(
                        s_zres[:, o * TC:(o + 1) * TC], pk[:],
                        s_cols[:, OF_OUTB + o:OF_OUTB + o + 1],
                        s_xn16[:, o * (3 + TC) + 3:o * (3 + TC) + 3 + TC],
                        ALU.add, ALU.add))
                    mark(P + f"res_{o}", "ve")

            @block.scalar
            def _(sc):
                for g in range(G):
                    par = g % 2
                    wt(sc, P + f"res_{g}")
                    if g >= 2:
                        wt(sc, P + f"ln2sq_{g-2}")
                    tick("sc", sc.copy(s_z16[:, par * TC:(par + 1) * TC],
                                       s_zres[:, g * TC:(g + 1) * TC]))
                    mark(P + f"ln2cast_{g}", "sc")
                    wt(sc, P + f"ln2mu_{g}")
                    tick("sc", sc.activation(
                        s_z16[:, par * TC:(par + 1) * TC],
                        s_z16[:, par * TC:(par + 1) * TC], AF.Square))
                    mark(P + f"ln2sqc_{g}", "sc")

            @block.tensor
            def _(te):
                for g in range(G):
                    par = g % 2
                    wt(te, P + f"ln2cast_{g}")
                    if g == 0:
                        wt(te, P + f"bcev_{NST * 2 - 4}")
                    tick("te", te.matmul(
                        pBCa[0:1, :], s_ones16[:],
                        s_z16[:, par * TC:(par + 1) * TC],
                        start=(g == 0), stop=(g == G - 1)))
                    mark(P + f"ln2mu_{g}", "te")
                    wt(te, P + f"ln2sqc_{g}")
                    if g == 0:
                        wt(te, P + f"bcev_{NST * 2 - 3}")
                    tick("te", te.matmul(
                        pBCb[0:1, :], s_ones16[:],
                        s_z16[:, par * TC:(par + 1) * TC],
                        start=(g == 0), stop=(g == G - 1)))
                    mark(P + f"ln2sq_{g}", "te")

            MU = s_s[0:1, 0:TC]
            VCH = s_t1[0:1, 0:TC]
            MUR = s_t1[0:1, TC:2 * TC]

            @block.vector
            def _(ve):
                def vst(inst):
                    tick("ve", inst)
                    ve.wait_ge(sems["ve"], L.cnt["ve"])

                wt(ve, P + f"ln2sq_{G-1}")
                wt(ve, P + f"ffn_h3_{HT-2}")
                wt(ve, P + f"ffn_h3_{HT-1}")
                vst(ve.tensor_scalar(MU, pBCa[0:1, :], 1.0 / D, None,
                                     ALU.mult))
                vst(ve.tensor_scalar(VCH, pBCb[0:1, :], 1.0 / D, None,
                                     ALU.mult))
                vst(ve.tensor_tensor(MUR, MU, MU, ALU.mult))
                vst(ve.tensor_tensor(VCH, VCH, MUR, ALU.subtract))
                tick("ve", ve.tensor_scalar(VCH, VCH, LN_EPS, None, ALU.add))
                mark(P + "ln2_var", "ve")

            @block.scalar
            def _(sc):
                wt(sc, P + "ln2_var")
                tick("sc", sc.activation(VCH, VCH, AF.Sqrt))
                mark(P + "ln2_sd", "sc")

            @block.vector
            def _(ve):
                def vst(inst):
                    tick("ve", inst)
                    ve.wait_ge(sems["ve"], L.cnt["ve"])

                wt(ve, P + "ln2_sd")
                vst(ve.reciprocal(VCH, VCH))
                vst(ve.tensor_tensor(MUR, MU, VCH, ALU.mult))
                tick("ve", ve.nop())
                mark(P + "ln2_r", "ve")

            @block.tensor
            def _(te):
                wt(te, P + "ln2_r")
                te.matmul(pBCa[:], s_ones1[:], VCH, start=True, stop=True)
                tick("te", te.matmul(pBCb[:], s_ones1[:], MUR,
                                     start=True, stop=True))
                mark(P + "ln2_bc", "te")

            @block.vector
            def _(ve):
                wt(ve, P + "ln2_bc")
                z3 = s_zres.rearrange("p (g t) -> p g t", t=TC)
                ve.tensor_tensor(z3[:, :, :], z3[:, :, :],
                                 pBCa[:, None, :].broadcast_to([128, G, TC]),
                                 ALU.mult)
                ve.tensor_tensor(z3[:, :, :], z3[:, :, :],
                                 pBCb[:, None, :].broadcast_to([128, G, TC]),
                                 ALU.subtract)
                for g in range(G):
                    _last = ve.tensor_scalar(
                        s_zres[:, g * TC:(g + 1) * TC],
                        s_zres[:, g * TC:(g + 1) * TC],
                        s_cols[:, OF_LNG + g:OF_LNG + g + 1],
                        s_cols[:, OF_LNB2 + g:OF_LNB2 + g + 1],
                        ALU.mult, ALU.add)
                tick("ve", _last)
                mark(P + "attn_ready", "ve")

            @block.tensor
            def _(te):
                for o in range(G):
                    wt(te, P + f"w2_{o}")
                    if o == 0:
                        wt(te, P + f"ffn_h3_{HT-1}")
                        wt(te, P + f"res_{G-2}")
                    elif o == 1:
                        wt(te, P + f"res_{G-1}")
                    else:
                        wt(te, P + f"ffn_fadd_{o-2}")
                    pk = pMA if o % 2 == 0 else pMB
                    for k in range(HT):
                        _last = te.matmul(
                            pk[:],
                            s_w2s[:, (o % 2) * 4096 + k * 128:
                                  (o % 2) * 4096 + (k + 1) * 128],
                            s_hq[:, k * TC:(k + 1) * TC],
                            start=(k == 0), stop=(k == HT - 1))
                    tick("te", _last)
                    mark(P + f"ffn_mm2_{o}", "te")

            @block.vector
            def _(ve):
                for o in range(G):
                    pk = pMA if o % 2 == 0 else pMB
                    wt(ve, P + f"ffn_mm2_{o}")
                    if o == 0:
                        wt(ve, P + "attn_ready")
                    tick("ve", ve.scalar_tensor_tensor(
                        s_zres[:, o * TC:(o + 1) * TC], pk[:],
                        1.0 / 1024.0, s_zres[:, o * TC:(o + 1) * TC],
                        ALU.mult, ALU.add))
                    mark(P + f"ffn_fadd_{o}", "ve")

            @block.tensor
            def _(te):
                for q in range(NSUB):
                    for half in range(2):
                        gi = q * 2 + half
                        pk = pTA if gi % 2 == 0 else pTB
                        if gi >= 2:
                            wt(te, P + f"fin_ev_{gi-2}")
                        for gg in range(4):
                            g = half * 4 + gg
                            wt(te, P + f"ffn_fadd_{g}")
                            _last = te.transpose(
                                pk[:, gg * 128:(gg + 1) * 128],
                                s_zres[:, g * TC + q * 128:
                                       g * TC + (q + 1) * 128],
                                s_id32[:])
                        tick("te", _last)
                        mark(P + f"fin_tp_{gi}", "te")

            @block.vector
            def _(ve):
                for q in range(NSUB):
                    for half in range(2):
                        gi = q * 2 + half
                        pk = pTA if gi % 2 == 0 else pTB
                        wt(ve, P + f"fin_tp_{gi}")
                        tick("ve", ve.tensor_tensor(
                            s_xntm[:, q * D + half * 512:
                                   q * D + (half + 1) * 512],
                            pk[:],
                            s_xtm[:, q * D + half * 512:
                                  q * D + (half + 1) * 512],
                            ALU.add))
                        mark(P + f"fin_ev_{gi}", "ve")

            @block.gpsimd
            def _(gp):
                barrier(gp, ("gp",))
                for q in range(NSUB):
                    wt(gp, P + f"fin_ev_{q * 2}")
                    wt(gp, P + f"fin_ev_{q * 2 + 1}")
                    tick("st", gp.dma_start(
                        out=dout[t0 + q * 128:t0 + (q + 1) * 128, :],
                        in_=s_xntm[:, q * D:(q + 1) * D]), 16)
                mark(P + "store", "st")

            @block.vector
            def _(ve):
                x3 = s_xn16.rearrange("p (g t) -> p g t", t=3 + TC)
                for g in range(G):
                    wt(ve, P + f"cv_{g}")
                if c + 1 < NCH and (c + 1) % (NCH // BLOC) != 0:
                    tick("ve", ve.tensor_copy(x3[:, :, 0:3],
                                              x3[:, :, TC:TC + 3]))
                else:
                    tick("ve", ve.memset(x3[:, :, 0:3], 0.0))
                mark(P + "halo", "ve")

            for k in ("gp", "te", "ve", "sc", "st"):
                mark(P + k, k)

        if DEBUG_TAPS:
            @block.gpsimd
            def _(gp):
                for k in ("gp", "te", "ve", "sc"):
                    wt(gp, f"c{NCH-1}_" + k)
                pairs = [("xn16", s_xn16), ("u16d", s_u16), ("btd", s_bt),
                         ("BCd", s_BC), ("xqd", s_xq), ("hqd", s_hq),
                         ("gad", s_ga), ("zresd", s_zres),
                         ("carryd", s_carry), ("acold", s_acol),
                         ("t1d", s_t1), ("statd", s_stat),
                         ("xtmd", s_xtm), ("xntmd", s_xntm)]
                for nm, t in pairs:
                    tick("st", gp.dma_start(out=dbg[nm], in_=t[:]), 16)
                tick("st", gp.dma_start(out=dbg["scrw1d"],
                                        in_=scr_w1[5, :, :]), 16)
                tick("st", gp.dma_start(out=dbg["scrw2d"],
                                        in_=scr_w2[3, :, :]), 16)

        @block.gpsimd
        def _(gp):
            if L.prev is not None:
                gp.wait_ge(sems["st"], L.cnt["st"])

    return nc


def build_program():
    led1 = Ledger(None)
    nc1 = bass.Bass("TRN2", target_bir_lowering=False,
                    detect_race_conditions=False)
    emit(nc1, led1)
    led2 = Ledger(led1.marks)
    nc2 = bass.Bass("TRN2", target_bir_lowering=False,
                    detect_race_conditions=False)
    emit(nc2, led2)
    assert led1.cnt == led2.cnt, (led1.cnt, led2.cnt)
    assert led1.marks == led2.marks
    return nc2


_NC_CACHE = None


def kernel(**inputs):
    global _NC_CACHE
    if _NC_CACHE is None:
        _NC_CACHE = build_program()
    nc = _NC_CACHE

    x = np.ascontiguousarray(inputs["L0_attn_x"], dtype=np.float32)
    base = {
        "conv_w": np.ascontiguousarray(
            inputs["L0_attn_conv_w"].reshape(D, DCONV), np.float32),
        "conv_b": np.ascontiguousarray(inputs["L0_attn_conv_b"], np.float32),
        "A": np.ascontiguousarray(inputs["L0_attn_A"], np.float32),
        "Bp_w": np.ascontiguousarray(inputs["L0_attn_Bp_w"], np.float32),
        "Bp_b": np.ascontiguousarray(inputs["L0_attn_Bp_b"], np.float32),
        "Cp_w": np.ascontiguousarray(inputs["L0_attn_Cp_w"], np.float32),
        "Cp_b": np.ascontiguousarray(inputs["L0_attn_Cp_b"], np.float32),
        "Dv": np.ascontiguousarray(inputs["L0_attn_D"], np.float32),
        "gate_w": np.ascontiguousarray(inputs["L0_attn_gate_w"], np.float32),
        "gate_b": np.ascontiguousarray(inputs["L0_attn_gate_b"], np.float32),
        "out_w": np.ascontiguousarray(inputs["L0_attn_out_w"], np.float32),
        "out_b": np.ascontiguousarray(inputs["L0_attn_out_b"], np.float32),
        "ln_g": np.ascontiguousarray(inputs["L0_attn_ln_g"], np.float32),
        "ln_b": np.ascontiguousarray(inputs["L0_attn_ln_b"], np.float32),
        "w1": np.ascontiguousarray(inputs["L0_ffn_w1"], np.float32),
        "b1": np.ascontiguousarray(inputs["L0_ffn_b1"], np.float32),
        "w2": np.ascontiguousarray(inputs["L0_ffn_w2"], np.float32),
        "b2": np.ascontiguousarray(inputs["L0_ffn_b2"], np.float32),
    }
    in_maps = []
    for ci in range(NCORES):
        m = dict(base)
        m["x"] = np.ascontiguousarray(
            x[ci * BLOC:(ci + 1) * BLOC].reshape(T, D))
        in_maps.append(m)

    res = run_bass_kernel_spmd(nc, in_maps, core_ids=list(range(NCORES)))
    out = np.empty((16, SEQ, D), np.float32)
    for ci in range(NCORES):
        out[ci * BLOC:(ci + 1) * BLOC] = res.results[ci]["out"].reshape(
            BLOC, SEQ, D)
    return out


# version 09
# speedup vs baseline: 1.2064x; 1.0154x over previous
import contextlib
import numpy as np

import concourse.bass as bass
import concourse.mybir as mybir
from concourse.bass_utils import run_bass_kernel_spmd

F32 = mybir.dt.float32
BF16 = mybir.dt.bfloat16
AF = mybir.ActivationFunctionType
ALU = mybir.AluOpType
AX = mybir.AxisListType

NCORES = 8
SEQ = 2048
BLOC = 2
T = SEQ * BLOC
D = 1024
H = 4096
NST = 16
DCONV = 4
LN_EPS = 1e-5
G = D // 128
HT = H // 128
TC = 512
NCH = T // TC
NSUB = TC // 128
RCONST = float(3 << 22)
DEBUG_TAPS = False


SEMKEYS = ("sy", "gp", "te", "ve", "sc", "params", "x", "st",
           "gs0", "gs1", "os0", "os1", "w10", "w11", "w20", "w21",
           "wp0", "wp1", "scr0", "scr1")


class Ledger:
    def __init__(self, prev_marks):
        self.cnt = {k: 0 for k in SEMKEYS}
        self.marks = {}
        self.prev = prev_marks

    def tick(self, sems, key, inst, k=1):
        inst.then_inc(sems[key], k)
        self.cnt[key] += k

    def mark(self, name, key):
        assert name not in self.marks, name
        self.marks[name] = (key, self.cnt[key])

    def wait(self, sems, eng, name):
        if self.prev is None:
            return
        key, val = self.prev[name]
        if val > 0:
            eng.wait_ge(sems[key], val)


def emit(nc, led):
    L = led

    dx = nc.dram_tensor("x", [T, D], F32, kind="ExternalInput").ap()
    dconv_w = nc.dram_tensor("conv_w", [D, DCONV], F32, kind="ExternalInput").ap()
    dconv_b = nc.dram_tensor("conv_b", [D], F32, kind="ExternalInput").ap()
    dA = nc.dram_tensor("A", [D, NST], F32, kind="ExternalInput").ap()
    dBpw = nc.dram_tensor("Bp_w", [NST, D], F32, kind="ExternalInput").ap()
    dBpb = nc.dram_tensor("Bp_b", [NST], F32, kind="ExternalInput").ap()
    dCpw = nc.dram_tensor("Cp_w", [NST, D], F32, kind="ExternalInput").ap()
    dCpb = nc.dram_tensor("Cp_b", [NST], F32, kind="ExternalInput").ap()
    dD = nc.dram_tensor("Dv", [D], F32, kind="ExternalInput").ap()
    dgw = nc.dram_tensor("gate_w", [D, D], F32, kind="ExternalInput").ap()
    dgb = nc.dram_tensor("gate_b", [D], F32, kind="ExternalInput").ap()
    dow = nc.dram_tensor("out_w", [D, D], F32, kind="ExternalInput").ap()
    dob = nc.dram_tensor("out_b", [D], F32, kind="ExternalInput").ap()
    dlng = nc.dram_tensor("ln_g", [D], F32, kind="ExternalInput").ap()
    dlnb = nc.dram_tensor("ln_b", [D], F32, kind="ExternalInput").ap()
    dw1 = nc.dram_tensor("w1", [H, D], F32, kind="ExternalInput").ap()
    db1 = nc.dram_tensor("b1", [H], F32, kind="ExternalInput").ap()
    dw2 = nc.dram_tensor("w2", [D, H], F32, kind="ExternalInput").ap()
    db2 = nc.dram_tensor("b2", [D], F32, kind="ExternalInput").ap()
    dout = nc.dram_tensor("out", [T, D], F32, kind="ExternalOutput").ap()
    scr_w1 = nc.dram_tensor("scr_w1", [HT, 128, 1024], BF16, kind="Internal").ap()
    scr_w2 = nc.dram_tensor("scr_w2", [G, 128, 4096], BF16, kind="Internal").ap()
    scr_g = nc.dram_tensor("scr_g", [G, 128, 1024], BF16, kind="Internal").ap()
    scr_o = nc.dram_tensor("scr_o", [G, 128, 1024], BF16, kind="Internal").ap()
    dbg = {}
    if DEBUG_TAPS:
        for nm, shape, dt in (
                ("xn16", [128, G * (3 + TC)], BF16),
                ("u16d", [128, G * TC], BF16),
                ("btd", [NST, 2 * TC], BF16),
                ("BCd", [128, 2 * NST * TC], BF16),
                ("xqd", [128, G * TC], BF16),
                ("hqd", [128, HT * TC], BF16),
                ("gad", [128, G * TC], BF16),
                ("zresd", [128, G * TC], F32),
                ("carryd", [128, G * NST], F32),
                ("acold", [128, G * NST], F32),
                ("t1d", [128, 2 * TC], F32),
                ("statd", [128, NSUB * 8], F32),
                ("xtmd", [128, NSUB * D], F32),
                ("xntmd", [128, NSUB * D], F32),
                ("scrw1d", [128, 1024], BF16),
                ("scrw2d", [128, 4096], BF16)):
            dbg[nm] = nc.dram_tensor("dbg_" + nm, shape, dt,
                                     kind="ExternalOutput").ap()

    ctx = contextlib.ExitStack()
    sb = lambda name, shape, dt=F32: ctx.enter_context(
        nc.sbuf_tensor(name, shape, dt))
    ps = lambda name: ctx.enter_context(nc.psum_tensor(name, [128, 512], F32))

    with ctx:
        s_BpCpT = sb("BpCpT", [128, 2 * G * NST], BF16)
        s_acol = sb("acol", [128, G * NST])
        s_diagD = sb("diagD", [128, G * 128], BF16)
        s_convw = sb("convw", [128, G * DCONV])
        s_dconv = sb("dconv", [128, G * DCONV * 128], BF16)
        s_cols = sb("cols", [128, 5 * G])
        OF_CONVB, OF_GATEB, OF_OUTB, OF_LNG, OF_LNB2 = (
            0, G, 2 * G, 3 * G, 4 * G)
        s_b1qC = sb("b1qC", [128, HT])
        s_bpb = sb("bpb", [NST, 2])
        s_id32 = sb("id32", [128, 128])
        s_id16 = sb("id16", [128, 128], BF16)
        s_sel = sb("sel", [NST, NST * 128], BF16)
        s_ones1 = sb("ones1", [1, 128])
        s_negC = sb("negC", [128, 1])
        s_ones16 = sb("ones16c", [128, 1], BF16)
        s_carry = sb("carry", [128, G * NST])
        s_w1s = sb("w1s", [128, 2 * 1024], BF16)
        s_w2s = sb("w2s", [128, 2 * 4096], BF16)
        s_gs = sb("gs", [128, 2 * 1024], BF16)
        s_os = sb("osb", [128, 2 * 1024], BF16)
        s_xtm = sb("xtm", [128, NSUB * D])
        s_xntm = sb("xntm", [128, NSUB * D])
        s_stat = sb("stat", [128, NSUB * 8])
        s_xq16 = sb("xq16", [128, D], BF16)
        s_xn16 = sb("xn16", [128, G * (3 + TC)], BF16)
        s_xq = sb("xq", [128, G * TC], BF16)
        s_u16 = sb("u16", [128, G * TC], BF16)
        s_bt = sb("bt", [NST, 2 * TC], BF16)
        s_BC = sb("BC", [128, 2 * NST * TC], BF16)
        s_w = sb("wbuf", [128, TC], BF16)
        s_s = sb("sbuf_s", [128, 2 * TC])
        s_z = sb("zbuf", [128, 2 * TC], BF16)
        s_ga = sb("ga", [128, G * TC], BF16)
        s_zres = sb("zres", [128, G * TC])
        s_z16 = sb("z16", [128, 2 * TC], BF16)
        s_hq = sb("hq", [128, HT * TC], BF16)
        s_t1 = sb("t1", [128, 2 * TC])
        pTA, pTB = ps("pTA"), ps("pTB")
        pMA, pMB = ps("pMA"), ps("pMB")
        pBCa, pBCb = ps("pBCa"), ps("pBCb")
        pYA, pYB = ps("pYA"), ps("pYB")

        sems = {k: ctx.enter_context(nc.semaphore("sem_" + k))
                for k in SEMKEYS}
        block = ctx.enter_context(nc.Block())

        def tick(key, inst, k=1):
            L.tick(sems, key, inst, k)

        def mark(name, key):
            L.mark(name, key)

        def wt(eng, name):
            L.wait(sems, eng, name)

        import contextlib as _cl

        def scoped(name):
            return nc.named_scope(name)

        @block.gpsimd
        def _(gp):
            def ss(inst):
                tick("gp", inst)
                gp.wait_ge(sems["gp"], L.cnt["gp"])

            ss(gp.memset(s_id32[:], 0.0))
            ss(nc.gpsimd.affine_select(
                out=s_id32[:], in_=s_id32[:], compare_op=ALU.not_equal,
                fill=1.0, base=0, pattern=[[-1, 128]], channel_multiplier=1))
            ss(gp.memset(s_id16[:], 0.0))
            ss(nc.gpsimd.affine_select(
                out=s_id16[:], in_=s_id16[:], compare_op=ALU.not_equal,
                fill=1.0, base=0, pattern=[[-1, 128]], channel_multiplier=1))
            ss(gp.memset(s_sel[:], 0.0))
            sel3 = s_sel.rearrange("p (n f) -> p n f", f=128)
            ss(nc.gpsimd.affine_select(
                out=sel3, in_=sel3, compare_op=ALU.not_equal,
                fill=1.0, base=0, pattern=[[-1, NST], [0, 128]],
                channel_multiplier=1))
            gp.memset(s_ones1[:], 1.0)
            gp.memset(s_negC[:], -RCONST)
            gp.memset(s_ones16[:], 1.0)
            gp.memset(s_carry[:], 0.0)
            tick("gp", gp.memset(s_xn16[:], 0.0))
            mark("gp_const", "gp")

        @block.sync
        def _(sy):
            for g in range(G):
                tick("params", sy.dma_start(
                    out=s_convw[:, g * DCONV:(g + 1) * DCONV],
                    in_=dconv_w[g * 128:(g + 1) * 128, :]), 16)
                tick("params", sy.dma_start(
                    out=s_cols[:, OF_CONVB + g:OF_CONVB + g + 1],
                    in_=dconv_b[g * 128:(g + 1) * 128][:, None]), 16)
                tick("params", sy.dma_start(
                    out=s_cols[:, OF_GATEB + g:OF_GATEB + g + 1],
                    in_=dgb[g * 128:(g + 1) * 128][:, None]), 16)
                tick("params", sy.dma_start(
                    out=s_cols[:, OF_OUTB + g:OF_OUTB + g + 1],
                    in_=dob[g * 128:(g + 1) * 128][:, None]), 16)
                tick("params", sy.dma_start(
                    out=s_cols[:, OF_LNG + g:OF_LNG + g + 1],
                    in_=dlng[g * 128:(g + 1) * 128][:, None]), 16)
                tick("params", sy.dma_start(
                    out=s_acol[:, g * NST:(g + 1) * NST],
                    in_=dA[g * 128:(g + 1) * 128, :]), 16)
                tick("params", sy.dma_start(
                    out=s_stat[:, g:g + 1],
                    in_=dD[g * 128:(g + 1) * 128][:, None]), 16)
                tick("params", sy.dma_start(
                    out=s_stat[:, 8 + g:8 + g + 1],
                    in_=dlnb[g * 128:(g + 1) * 128][:, None]), 16)
            tick("params", sy.dma_start(out=s_bpb[:, 0:1], in_=dBpb[:, None]), 16)
            tick("params", sy.dma_start(out=s_bpb[:, 1:2], in_=dCpb[:, None]), 16)
            tick("params", sy.dma_start(
                out=s_zres[0:32, 0:128],
                in_=db1.rearrange("(a b) -> a b", b=128)), 16)
            tick("params", sy.dma_start(
                out=s_zres[0:8, 200:328],
                in_=db2.rearrange("(a b) -> a b", b=128)), 16)
            tick("params", sy.dma_start(
                out=s_zres[0:NST, 400:400 + D], in_=dBpw[:, :]), 16)
            tick("params", sy.dma_start(
                out=s_zres[0:NST, 1500:1500 + D], in_=dCpw[:, :]), 16)
            mark("sy_params", "params")

        @block.scalar
        def _(sc):
            wt(sc, "sy_params")
            tick("sc", sc.activation(s_acol[:], s_acol[:], AF.Exp))
            sc.wait_ge(sems["sc"], L.cnt["sc"])
            tick("sc", sc.activation(s_acol[:], s_acol[:], AF.Exp, scale=-1.0))
            mark("sc_acol", "sc")

        @block.tensor
        def _(te):
            wt(te, "sy_params")
            wt(te, "gp_const")
            te.transpose(pTA[:, 0:32], s_zres[0:32, 0:128],
                         s_id32[0:32, 0:32])
            te.transpose(pTA[:, 32:40], s_zres[0:8, 200:328],
                         s_id32[0:8, 0:8])
            for g in range(G):
                te.transpose(pTB[:, g * 16:(g + 1) * 16],
                             s_zres[0:NST, 400 + g * 128:400 + (g + 1) * 128],
                             s_id32[0:NST, 0:NST])
            for g in range(G):
                _last = te.transpose(pTB[:, 128 + g * 16:128 + (g + 1) * 16],
                             s_zres[0:NST, 1500 + g * 128:1500 + (g + 1) * 128],
                             s_id32[0:NST, 0:NST])
            tick("te", _last)
            mark("te_prep1", "te")

        @block.vector
        def _(ve):
            def vst(inst):
                tick("ve", inst)
                ve.wait_ge(sems["ve"], L.cnt["ve"])

            wt(ve, "te_prep1")
            wt(ve, "sy_params")
            wt(ve, "gp_const")
            for g in range(G):
                ve.tensor_scalar(s_diagD[:, g * 128:(g + 1) * 128],
                                 s_id16[:], s_stat[:, g:g + 1], None, ALU.mult)
            for g in range(G):
                for k in range(DCONV):
                    i = g * DCONV + k
                    ve.tensor_scalar(s_dconv[:, i * 128:(i + 1) * 128],
                                     s_id16[:], s_convw[:, i:i + 1],
                                     None, ALU.mult)
            vst(ve.tensor_scalar(s_b1qC[:], pTA[:, 0:HT], 64.0, RCONST,
                                 ALU.mult, ALU.add))
            vst(ve.tensor_scalar(s_b1qC[:], s_b1qC[:], RCONST + 127.0,
                                 RCONST - 128.0, ALU.min, ALU.max))
            vst(ve.tensor_scalar(s_b1qC[:], s_b1qC[:], RCONST, None,
                                 ALU.subtract))
            vst(ve.tensor_scalar(s_b1qC[:], s_b1qC[:], 0.25, None, ALU.mult))
            vst(ve.tensor_scalar(s_cols[:, OF_LNB2:OF_LNB2 + 8],
                                 pTA[:, 32:40], 64.0, RCONST,
                                 ALU.mult, ALU.add))
            vst(ve.tensor_scalar(s_cols[:, OF_LNB2:OF_LNB2 + 8],
                                 s_cols[:, OF_LNB2:OF_LNB2 + 8],
                                 RCONST + 127.0, RCONST - 128.0,
                                 ALU.min, ALU.max))
            vst(ve.tensor_scalar(s_cols[:, OF_LNB2:OF_LNB2 + 8],
                                 s_cols[:, OF_LNB2:OF_LNB2 + 8],
                                 RCONST, None, ALU.subtract))
            vst(ve.scalar_tensor_tensor(
                s_cols[:, OF_LNB2:OF_LNB2 + 8],
                s_cols[:, OF_LNB2:OF_LNB2 + 8], 1.0 / 64.0,
                s_stat[:, 8:16], ALU.mult, ALU.add))
            ve.tensor_copy(s_BpCpT[:, 0:G * NST], pTB[:, 0:G * NST])
            tick("ve", ve.tensor_copy(s_BpCpT[:, G * NST:2 * G * NST],
                                      pTB[:, 128:128 + G * NST]))
            mark("ve_prep1", "ve")

        wjobs = []
        for (src, dst, nrt, rowlen, mode) in (
                (dw1, scr_w1, HT, D, "q64"),
                (dw2, scr_w2, G, H, "q64"),
                (dgw, scr_g, G, D, "cast"),
                (dow, scr_o, G, D, "cast")):
            for j in range(nrt):
                wjobs.append((src, dst, j, rowlen, mode))
        NJ = len(wjobs)

        def wp_in(i, rowlen):
            if rowlen == H:
                return s_zres[:, 0:H]
            return s_xtm[:, (i % 2) * 2048:(i % 2) * 2048 + rowlen]

        def wp_t1(i, rowlen):
            if rowlen == H:
                return s_xntm[:, 0:H]
            return s_xntm[:, (i % 2) * 2048:(i % 2) * 2048 + rowlen]

        @block.sync
        def _(sy):
            for i, (src, dst, j, rowlen, mode) in enumerate(wjobs):
                if rowlen == H:
                    wt(sy, "te_prep1")
                    if i >= 1:
                        wt(sy, f"wp_read_{i-1}")
                if i >= 2:
                    wt(sy, f"wp_read_{i-2}")
                tick(f"wp{i % 2}", sy.dma_start(
                    out=wp_in(i, rowlen),
                    in_=src[j * 128:(j + 1) * 128, :]), 16)
                mark(f"wp_dma_{i}", f"wp{i % 2}")

        @block.scalar
        def _(sc):
            for i, (src, dst, j, rowlen, mode) in enumerate(wjobs):
                par = i % 2
                wt(sc, f"wp_dma_{i}")
                src_sb = wp_in(i, rowlen)
                if mode == "q64":
                    if i >= 1:
                        wt(sc, f"wp_clip_{i-1}")
                    if i >= 2:
                        wt(sc, f"wp_clip_{i-2}")
                    tick("sc", sc.activation(
                        wp_t1(i, rowlen), src_sb,
                        AF.Copy, bias=RCONST, scale=64.0))
                else:
                    if i >= 2:
                        wt(sc, f"wp_te_{i-2}")
                    tick("sc", sc.activation(
                        s_hq[:, par * 4096:par * 4096 + rowlen], src_sb,
                        AF.Copy))
                mark(f"wp_read_{i}", "sc")



        @block.tensor
        def _(te):
            gidx = 0
            for i, (src, dst, j, rowlen, mode) in enumerate(wjobs):
                par = i % 2
                if mode == "q64":
                    wt(te, f"wp_clip_{i}")
                else:
                    wt(te, f"wp_read_{i}")
                if i >= 2:
                    wt(te, f"wp_stage_{i-2}")
                int_sb = s_hq[:, par * 4096:par * 4096 + rowlen]
                nk = rowlen // 128
                for k0 in range(0, nk, 4):
                    pk = pTA if gidx % 2 == 0 else pTB
                    pk16 = pk[:].bitcast(BF16)
                    if gidx < 2:
                        wt(te, "ve_prep1")
                    if gidx >= 2:
                        wt(te, f"wp_ev_{gidx-2}")
                    kn = min(4, nk - k0)
                    for kk in range(kn):
                        _last = te.transpose(pk16[:, kk * 128:(kk + 1) * 128],
                                     int_sb[:, (k0 + kk) * 128:
                                            (k0 + kk + 1) * 128],
                                     s_id16[:])
                    tick("te", _last)
                    mark(f"wp_tp_{gidx}", "te")
                    gidx += 1
                mark(f"wp_te_{i}", "te")

        item_gidx = []
        _g = 0
        for (_s, _d, _j, _rl, _m) in wjobs:
            ng = (_rl // 128 + 3) // 4
            item_gidx.append((_g, ng))
            _g += ng

        @block.vector
        def _(ve):
            def emit_clip(i):
                srcj, dstj, jj, rowlen, mode = wjobs[i]
                par = i % 2
                if mode != "q64":
                    mark(f"wp_clip_{i}", "ve")
                    return
                wt(ve, f"wp_read_{i}")
                if i >= 2:
                    wt(ve, f"wp_te_{i-2}")
                t1 = wp_t1(i, rowlen)
                ve.tensor_scalar(t1, t1, RCONST + 127.0, RCONST - 128.0,
                                 ALU.min, ALU.max)
                tick("ve", ve.tensor_scalar(
                    s_hq[:, par * 4096:par * 4096 + rowlen], t1,
                    RCONST, None, ALU.subtract))
                mark(f"wp_clip_{i}", "ve")

            def emit_evac(i):
                srcj, dstj, jj, rowlen, mode = wjobs[i]
                par = i % 2
                nk = rowlen // 128
                stg = s_w2s if rowlen == H else s_w1s
                stg_off = par * rowlen
                if i >= 2:
                    wt(ve, f"wp_scr_{i-2}")
                g0, ng = item_gidx[i]
                for gg in range(ng):
                    gidx = g0 + gg
                    k0 = gg * 4
                    wt(ve, f"wp_tp_{gidx}")
                    pk16 = (pTA if gidx % 2 == 0 else pTB)[:].bitcast(BF16)
                    kn = min(4, nk - k0)
                    _last = ve.tensor_copy(
                        stg[:, stg_off + k0 * 128:stg_off + (k0 + kn) * 128],
                        pk16[:, 0:kn * 128])
                    tick("ve", _last)
                    mark(f"wp_ev_{gidx}", "ve")
                mark(f"wp_stage_{i}", "ve")

            for i in range(NJ + 1):
                if i < NJ:
                    emit_clip(i)
                if i >= 1:
                    emit_evac(i - 1)

        @block.gpsimd
        def _(gp):
            for i, (src, dst, j, rowlen, mode) in enumerate(wjobs):
                par = i % 2
                wt(gp, f"wp_stage_{i}")
                stg = s_w2s if rowlen == H else s_w1s
                tick(f"scr{i % 2}", gp.dma_start(
                    out=dst[j, :, :],
                    in_=stg[:, par * rowlen:(par + 1) * rowlen]), 16)
                mark(f"wp_scr_{i}", f"scr{i % 2}")

        for k in SEMKEYS:
            mark(f"prep_end_{k}", k)

        for c in range(NCH):
            seq_start = (c % (NCH // BLOC) == 0)
            t0 = c * TC
            P = f"c{c}_"
            prev = f"c{c-1}_" if c > 0 else "prep_end_"

            def pw(eng, name):
                if c > 0:
                    wt(eng, f"c{c-1}_" + name)

            def barrier(eng, names=("gp", "te", "ve", "sc", "st")):
                if c > 0:
                    for k in names:
                        wt(eng, prev + k)
                else:
                    for k in names:
                        wt(eng, f"prep_end_{k}")
                    wt(eng, "prep_end_scr0")
                    wt(eng, "prep_end_scr1")

            @block.sync
            def _(sy):
                if c == 0:
                    barrier(sy)
                for q in range(NSUB):
                    for gi in range(8):
                        pw(sy, f"fin_ev_{gi}")
                    tick("x", sy.dma_start(
                        out=s_xtm[:, q * D:(q + 1) * D],
                        in_=dx[t0 + q * 128:t0 + (q + 1) * 128, :]), 16)
                mark(P + "x", "x")
                for o in range(G):
                    if o >= 2:
                        wt(sy, P + f"gate_mm_{o-2}")
                    else:
                        pw(sy, f"gate_mm_{o + 6}")
                    tick(f"gs{o % 2}", sy.dma_start(
                        out=s_gs[:, (o % 2) * 1024:(o % 2 + 1) * 1024],
                        in_=scr_g[o, :, :]), 16)
                    mark(P + f"gs_{o}", f"gs{o % 2}")
                for j in range(HT):
                    if j >= 2:
                        wt(sy, P + f"ffn_mm1_{j-2}")
                    else:
                        pw(sy, f"ffn_mm1_{j + 30}")
                    tick(f"w1{j % 2}", sy.dma_start(
                        out=s_w1s[:, (j % 2) * 1024:(j % 2 + 1) * 1024],
                        in_=scr_w1[j, :, :]), 16)
                    mark(P + f"w1_{j}", f"w1{j % 2}")
                for o in range(G):
                    if o >= 2:
                        wt(sy, P + f"out_mm_{o-2}")
                    else:
                        pw(sy, f"out_mm_{o + 6}")
                    tick(f"os{o % 2}", sy.dma_start(
                        out=s_os[:, (o % 2) * 1024:(o % 2 + 1) * 1024],
                        in_=scr_o[o, :, :]), 16)
                    mark(P + f"os_{o}", f"os{o % 2}")
                for o in range(G):
                    if o >= 2:
                        wt(sy, P + f"ffn_mm2_{o-2}")
                    else:
                        pw(sy, f"ffn_mm2_{o + 6}")
                    tick(f"w2{o % 2}", sy.dma_start(
                        out=s_w2s[:, (o % 2) * 4096:(o % 2 + 1) * 4096],
                        in_=scr_w2[o, :, :]), 16)
                    mark(P + f"w2_{o}", f"w2{o % 2}")

            @block.scalar
            def _(sc):
                if c == 0:
                    barrier(sc)
                pw(sc, "store")
                wt(sc, P + "x")
                for q in range(NSUB):
                    sc.activation(
                        s_xntm[:, q * D:(q + 1) * D],
                        s_xtm[:, q * D:(q + 1) * D], AF.Copy,
                        accum_out=s_stat[:, q * 8:q * 8 + 1])
                    _last = sc.activation(
                        s_xntm[:, q * D:(q + 1) * D],
                        s_xtm[:, q * D:(q + 1) * D], AF.Square,
                        accum_out=s_stat[:, q * 8 + 1:q * 8 + 2])
                tick("sc", _last)
                mark(P + "ln1_sum", "sc")
                mark(P + "ln1_sq", "sc")

            st3 = s_stat.rearrange("p (q k) -> p q k", k=8)

            @block.vector
            def _(ve):
                def vst(inst):
                    tick("ve", inst)
                    ve.wait_ge(sems["ve"], L.cnt["ve"])

                if c == 0:
                    barrier(ve)
                wt(ve, P + "ln1_sq")
                ve.tensor_scalar(st3[:, :, 2:3], st3[:, :, 0:1], 1.0 / D,
                                 None, ALU.mult)
                vst(ve.tensor_scalar(st3[:, :, 3:4], st3[:, :, 1:2], 1.0 / D,
                                     None, ALU.mult))
                vst(ve.tensor_tensor(st3[:, :, 4:5], st3[:, :, 2:3],
                                     st3[:, :, 2:3], ALU.mult))
                vst(ve.tensor_tensor(st3[:, :, 5:6], st3[:, :, 3:4],
                                     st3[:, :, 4:5], ALU.subtract))
                tick("ve", ve.tensor_scalar(st3[:, :, 5:6], st3[:, :, 5:6],
                                            LN_EPS, None, ALU.add))
                mark(P + "ln1_var", "ve")

            @block.scalar
            def _(sc):
                wt(sc, P + "ln1_var")
                tick("sc", sc.activation(st3[:, :, 5:6], st3[:, :, 5:6],
                                         AF.Sqrt))
                mark(P + "ln1_sd", "sc")

            @block.vector
            def _(ve):
                def vst(inst):
                    tick("ve", inst)
                    ve.wait_ge(sems["ve"], L.cnt["ve"])

                wt(ve, P + "ln1_sd")
                vst(ve.reciprocal(st3[:, :, 6:7], st3[:, :, 5:6]))
                vst(ve.tensor_tensor(st3[:, :, 7:8], st3[:, :, 2:3],
                                     st3[:, :, 6:7], ALU.mult))
                vst(ve.tensor_scalar(st3[:, :, 7:8], st3[:, :, 7:8],
                                     -1.0, None, ALU.mult))
                for q in range(NSUB):
                    _last = ve.tensor_scalar(s_xntm[:, q * D:(q + 1) * D],
                                             s_xtm[:, q * D:(q + 1) * D],
                                             st3[:, q, 6:7], st3[:, q, 7:8],
                                             ALU.mult, ALU.add)
                tick("ve", _last)
                mark(P + "ln1_ap", "ve")

            tgroups = []
            for q in range(NSUB):
                for srcn in ("xn", "xq"):
                    for half in range(2):
                        tgroups.append((q, srcn, half))

            @block.tensor
            def _(te):
                for gi, (q, srcn, half) in enumerate(tgroups):
                    pk = pTA if gi % 2 == 0 else pTB
                    if gi >= 2:
                        wt(te, P + f"tpev_{gi-2}")
                    else:
                        pw(te, f"fin_ev_{gi + 6}")
                    if srcn == "xn":
                        wt(te, P + "ln1_ap")
                        for gg in range(4):
                            g = half * 4 + gg
                            _last = te.transpose(
                                pk[:, gg * 128:(gg + 1) * 128],
                                s_xntm[:, q * D + g * 128:
                                       q * D + (g + 1) * 128],
                                s_id32[:])
                    else:
                        wt(te, P + f"xqclip_{q}")
                        pk16 = pk[:].bitcast(BF16)
                        for gg in range(4):
                            g = half * 4 + gg
                            _last = te.transpose(
                                pk16[:, gg * 128:(gg + 1) * 128],
                                s_xq16[:, g * 128:(g + 1) * 128],
                                s_id16[:])
                    tick("te", _last)
                    mark(P + f"tp_{srcn}_{q}_{half}", "te")

            @block.scalar
            def _(sc):
                def xq1(q):
                    if q == 0:
                        pw(sc, "ln2_bc")
                    if q >= 1:
                        wt(sc, P + f"xqclip_{q-1}")
                    tick("sc", sc.activation(
                        s_t1[:, 0:D], s_xntm[:, q * D:(q + 1) * D], AF.Copy,
                        bias=RCONST, scale=16.0))
                    mark(P + f"xq1_{q}", "sc")

                xq1(0)
                xq1(1)
                xn3 = s_xn16.rearrange("p (g t) -> p g t", t=3 + TC)
                xq3 = s_xq.rearrange("p (g t) -> p g t", t=TC)
                pw(sc, "res_6")
                pw(sc, "res_7")
                pw(sc, "halo")
                pw(sc, "ffn_mm1_30")
                pw(sc, "ffn_mm1_31")
                for gi, (q, srcn, half) in enumerate(tgroups):
                    pk = pTA if gi % 2 == 0 else pTB
                    wt(sc, P + f"tp_{srcn}_{q}_{half}")
                    if srcn == "xn":
                        dst = xn3[:, half * 4:(half + 1) * 4,
                                  3 + q * 128:3 + (q + 1) * 128]
                        _last = sc.copy(
                            dst, pk[:].rearrange("p (g t) -> p g t", t=128))
                    else:
                        dst = xq3[:, half * 4:(half + 1) * 4,
                                  q * 128:(q + 1) * 128]
                        pk16 = pk[:].bitcast(BF16)
                        _last = sc.copy(
                            dst, pk16[:, 0:512].rearrange(
                                "p (g t) -> p g t", t=128))
                    tick("sc", _last)
                    mark(P + f"tpev_{gi}", "sc")
                    if srcn == "xq" and half == 1 and q + 2 < NSUB:
                        xq1(q + 2)
                mark(P + "xn_ready", "sc")
                mark(P + "xq_ready", "sc")

            @block.vector
            def _(ve):
                for q in range(NSUB):
                    wt(ve, P + f"xq1_{q}")
                    if q >= 1:
                        wt(ve, P + f"tp_xq_{q-1}_1")
                    ve.tensor_scalar(s_t1[:, 0:D], s_t1[:, 0:D],
                                     RCONST + 127.0, RCONST - 128.0,
                                     ALU.min, ALU.max)
                    tick("ve", ve.tensor_scalar(
                        s_xq16[:, 0:D], s_t1[:, 0:D],
                        RCONST, None, ALU.subtract))
                    mark(P + f"xqclip_{q}", "ve")

            @block.tensor
            def _(te):
                for g in range(G):
                    pk = pTA if g % 2 == 0 else pTB
                    if g == 0:
                        wt(te, P + "tpev_14")
                        wt(te, P + "xn_ready")
                    elif g == 1:
                        wt(te, P + "tpev_15")
                    else:
                        wt(te, P + f"silu_{g-2}")
                    base = g * (3 + TC)
                    for k in range(DCONV):
                        kk = DCONV - 1 - k
                        _last = te.matmul(
                            pk[:],
                            s_dconv[:, (g * DCONV + kk) * 128:
                                    (g * DCONV + kk + 1) * 128],
                            s_xn16[:, base + 3 - k:base + 3 - k + TC],
                            start=(k == 0), stop=(k == DCONV - 1))
                    tick("te", _last)
                    mark(P + f"cv_{g}", "te")

            @block.scalar
            def _(sc):
                pw(sc, "yg_6")
                pw(sc, "yg_7")
                for g in range(G):
                    pk = pTA if g % 2 == 0 else pTB
                    wt(sc, P + f"cv_{g}")
                    tick("sc", sc.activation(
                        s_u16[:, g * TC:(g + 1) * TC], pk[:], AF.Silu,
                        bias=s_cols[:, OF_CONVB + g:OF_CONVB + g + 1]))
                    mark(P + f"silu_{g}", "sc")

            @block.tensor
            def _(te):
                pw(te, "yg_6")
                pw(te, "yg_7")
                for g in range(G):
                    wt(te, P + f"silu_{g}")
                    te.matmul(pYA[0:NST, :],
                              s_BpCpT[:, g * NST:(g + 1) * NST],
                              s_u16[:, g * TC:(g + 1) * TC],
                              start=(g == 0), stop=(g == G - 1))
                for g in range(G):
                    _last = te.matmul(pYB[0:NST, :],
                                      s_BpCpT[:, (G + g) * NST:
                                              (G + g + 1) * NST],
                                      s_u16[:, g * TC:(g + 1) * TC],
                                      start=(g == 0), stop=(g == G - 1))
                tick("te", _last)
                mark(P + "btct_mm", "te")

            @block.vector
            def _(ve):
                wt(ve, P + "btct_mm")
                pw(ve, f"bcmm_{2 * NST - 1}")
                ve.tensor_scalar(s_bt[0:NST, 0:TC], pYA[0:NST, :],
                                 s_bpb[:, 0:1], None, ALU.add)
                tick("ve", ve.tensor_scalar(s_bt[0:NST, TC:2 * TC],
                                            pYB[0:NST, :],
                                            s_bpb[:, 1:2], None, ALU.add))
                mark(P + "btct_ev", "ve")

            bcbank = [pBCa, pBCb, pYA, pYB]

            @block.tensor
            def _(te):
                wt(te, P + "btct_ev")
                pw(te, "ffn_fadd_2")
                pw(te, "ffn_fadd_3")
                for n in range(NST):
                    for which in range(2):
                        bi = (2 * n + which) % 4
                        gi = 2 * n + which
                        if gi >= 4:
                            wt(te, P + f"bcev_{gi-4}")
                        tick("te", te.matmul(
                            bcbank[bi][:],
                            s_sel[:, n * 128:(n + 1) * 128],
                            s_bt[0:NST, which * TC:(which + 1) * TC],
                            start=True, stop=True))
                        mark(P + f"bcmm_{gi}", "te")

            @block.scalar
            def _(sc):
                pw(sc, f"z_{G * NST - 1}")
                for n in range(NST):
                    for which in range(2):
                        gi = 2 * n + which
                        bi = gi % 4
                        wt(sc, P + f"bcmm_{gi}")
                        tick("sc", sc.copy(
                            s_BC[:, (which * NST + n) * TC:
                                 (which * NST + n + 1) * TC],
                            bcbank[bi][:]))
                        mark(P + f"bcev_{gi}", "sc")
                mark(P + "bc_ready", "sc")

            @block.tensor
            def _(te):
                for o in range(G):
                    wt(te, P + f"gs_{o}")
                    wt(te, P + "xn_ready")
                    if o >= 2:
                        wt(te, P + f"gate_ev_{o-2}")
                    elif o == 0:
                        pw(te, "ffn_fadd_6")
                    else:
                        pw(te, "ffn_fadd_7")
                    pk = pMA if o % 2 == 0 else pMB
                    for k in range(G):
                        _last = te.matmul(
                            pk[:],
                            s_gs[:, (o % 2) * 1024 + k * 128:
                                 (o % 2) * 1024 + (k + 1) * 128],
                            s_xn16[:, k * (3 + TC) + 3:
                                   k * (3 + TC) + 3 + TC],
                            start=(k == 0), stop=(k == G - 1))
                    tick("te", _last)
                    mark(P + f"gate_mm_{o}", "te")

            @block.scalar
            def _(sc):
                pw(sc, "out_mm_7")
                for o in range(G):
                    pk = pMA if o % 2 == 0 else pMB
                    wt(sc, P + f"gate_mm_{o}")
                    tick("sc", sc.activation(
                        s_ga[:, o * TC:(o + 1) * TC], pk[:], AF.Silu,
                        bias=s_cols[:, OF_GATEB + o:OF_GATEB + o + 1]))
                    mark(P + f"gate_ev_{o}", "sc")

            fbank = [pMA, pMB, pTA, pTB]

            @block.vector
            def _(ve):
                def emit_yg(g):
                    pk = pYA if g % 2 == 0 else pYB
                    wt(ve, P + f"ymm_{g * NST + NST - 1}")
                    wt(ve, P + f"gate_ev_{g}")
                    tick("ve", ve.tensor_tensor(
                        s_ga[:, g * TC:(g + 1) * TC], pk[:],
                        s_ga[:, g * TC:(g + 1) * TC], ALU.mult))
                    mark(P + f"yg_{g}", "ve")

                def emit_h(j):
                    par = j % 2
                    bk = fbank[j % 4]
                    wt(ve, P + f"ffn_mm1_{j}")
                    if j >= 2:
                        wt(ve, P + f"ffn_h3_{j-2}")
                    ve.scalar_tensor_tensor(
                        s_t1[:, par * TC:(par + 1) * TC], bk[:], 1.0 / 64.0,
                        s_b1qC[:, j:j + 1].broadcast_to([128, TC]),
                        ALU.mult, ALU.add)
                    tick("ve", ve.tensor_scalar(
                        s_t1[:, par * TC:(par + 1) * TC],
                        s_t1[:, par * TC:(par + 1) * TC],
                        RCONST, RCONST + 127.0, ALU.add, ALU.min))
                    mark(P + f"ffn_h2_{j}", "ve")

                pw(ve, f"ln2sq_{G-1}")
                for g in range(G):
                    for n in range(NST):
                        idx = g * NST + n
                        half = n % 2
                        zsl = (s_z if (idx // 2) % 2 == 0 else s_z16)
                        if idx >= 4:
                            wt(ve, P + f"ymm_{idx-4}")
                        if g == 0:
                            wt(ve, P + f"bcev_{2 * n}")
                            wt(ve, P + f"bcev_{2 * n + 1}")
                        ve.tensor_tensor(s_w[:],
                                         s_u16[:, g * TC:(g + 1) * TC],
                                         s_BC[:, n * TC:(n + 1) * TC],
                                         ALU.mult)
                        ci = g * NST + n
                        init = (0.0 if seq_start
                                else s_carry[:, ci:ci + 1])
                        ve.tensor_tensor_scan(
                            s_s[:, half * TC:(half + 1) * TC],
                            s_acol[:, ci:ci + 1].broadcast_to([128, TC]),
                            s_w[:], init, ALU.mult, ALU.add)
                        tick("ve", ve.tensor_tensor(
                            zsl[:, half * TC:(half + 1) * TC],
                            s_s[:, half * TC:(half + 1) * TC],
                            s_BC[:, (NST + n) * TC:(NST + n + 1) * TC],
                            ALU.mult))
                        mark(P + f"z_{idx}", "ve")
                        ve.tensor_copy(
                            s_carry[:, ci:ci + 1],
                            s_s[:, (half + 1) * TC - 1:(half + 1) * TC])
                    if g >= 1:
                        emit_yg(g - 1)
                        for j in range(4 * (g - 1), 4 * g):
                            emit_h(j)
                emit_yg(G - 1)
                for j in range(4 * (G - 1), 4 * G):
                    emit_h(j)

            def emit_f1(te, j):
                wt(te, P + f"w1_{j}")
                if j == 0:
                    wt(te, P + "xq_ready")
                    wt(te, P + f"gate_ev_{G-2}")
                elif j == 1:
                    wt(te, P + f"gate_ev_{G-1}")
                elif j == 2:
                    wt(te, P + f"silu_{G-2}")
                elif j == 3:
                    wt(te, P + f"silu_{G-1}")
                else:
                    wt(te, P + f"ffn_h2_{j-4}")
                bk = fbank[j % 4]
                for k in range(G):
                    _last = te.matmul(
                        bk[:],
                        s_w1s[:, (j % 2) * 1024 + k * 128:
                              (j % 2) * 1024 + (k + 1) * 128],
                        s_xq[:, k * TC:(k + 1) * TC],
                        start=(k == 0), stop=(k == G - 1))
                tick("te", _last)
                mark(P + f"ffn_mm1_{j}", "te")

            @block.tensor
            def _(te):
                for g in range(G):
                    pk = pYA if g % 2 == 0 else pYB
                    if g < 2:
                        wt(te, P + f"bcev_{NST * 2 - 2 + g}")
                    else:
                        wt(te, P + f"yg_{g-2}")
                    te.matmul(pk[:], s_diagD[:, g * 128:(g + 1) * 128],
                              s_u16[:, g * TC:(g + 1) * TC],
                              start=True, stop=False)
                    for n in range(NST):
                        idx = g * NST + n
                        half = n % 2
                        zsl = (s_z if (idx // 2) % 2 == 0 else s_z16)
                        wt(te, P + f"z_{idx}")
                        tick("te", te.matmul(
                            pk[:], s_id16[:],
                            zsl[:, half * TC:(half + 1) * TC],
                            start=False, stop=(n == NST - 1)))
                        mark(P + f"ymm_{idx}", "te")
                        if n == 7 and g >= 1:
                            emit_f1(te, 4 * (g - 1) + 2)
                            emit_f1(te, 4 * (g - 1) + 3)
                    emit_f1(te, 4 * g)
                    emit_f1(te, 4 * g + 1)
                emit_f1(te, 4 * (G - 1) + 2)
                emit_f1(te, 4 * (G - 1) + 3)

            @block.scalar
            def _(sc):
                pw(sc, f"ffn_mm2_{G-1}")
                for j in range(HT):
                    par = j % 2
                    wt(sc, P + f"ffn_h2_{j}")
                    tick("sc", sc.activation(
                        s_hq[:, j * TC:(j + 1) * TC],
                        s_t1[:, par * TC:(par + 1) * TC], AF.Relu,
                        bias=s_negC[:]))
                    mark(P + f"ffn_h3_{j}", "sc")

            @block.tensor
            def _(te):
                for o in range(G):
                    wt(te, P + f"os_{o}")
                    if o == 0:
                        for g in range(G):
                            wt(te, P + f"yg_{g}")
                        wt(te, P + f"ffn_h2_{HT-4}")
                    elif o == 1:
                        wt(te, P + f"ffn_h2_{HT-3}")
                    else:
                        wt(te, P + f"res_{o-2}")
                    pk = pMA if o % 2 == 0 else pMB
                    for k in range(G):
                        _last = te.matmul(
                            pk[:],
                            s_os[:, (o % 2) * 1024 + k * 128:
                                 (o % 2) * 1024 + (k + 1) * 128],
                            s_ga[:, k * TC:(k + 1) * TC],
                            start=(k == 0), stop=(k == G - 1))
                    tick("te", _last)
                    mark(P + f"out_mm_{o}", "te")

            @block.vector
            def _(ve):
                pw(ve, "fin_tp_7")
                for o in range(G):
                    pk = pMA if o % 2 == 0 else pMB
                    wt(ve, P + f"out_mm_{o}")
                    tick("ve", ve.scalar_tensor_tensor(
                        s_zres[:, o * TC:(o + 1) * TC], pk[:],
                        s_cols[:, OF_OUTB + o:OF_OUTB + o + 1],
                        s_xn16[:, o * (3 + TC) + 3:o * (3 + TC) + 3 + TC],
                        ALU.add, ALU.add))
                    mark(P + f"res_{o}", "ve")

            @block.scalar
            def _(sc):
                for g in range(G):
                    par = g % 2
                    wt(sc, P + f"res_{g}")
                    if g >= 2:
                        wt(sc, P + f"ln2sq_{g-2}")
                    tick("sc", sc.copy(s_z16[:, par * TC:(par + 1) * TC],
                                       s_zres[:, g * TC:(g + 1) * TC]))
                    mark(P + f"ln2cast_{g}", "sc")
                    wt(sc, P + f"ln2mu_{g}")
                    tick("sc", sc.activation(
                        s_z16[:, par * TC:(par + 1) * TC],
                        s_z16[:, par * TC:(par + 1) * TC], AF.Square))
                    mark(P + f"ln2sqc_{g}", "sc")

            @block.tensor
            def _(te):
                for g in range(G):
                    par = g % 2
                    wt(te, P + f"ln2cast_{g}")
                    if g == 0:
                        wt(te, P + f"bcev_{NST * 2 - 4}")
                    tick("te", te.matmul(
                        pBCa[0:1, :], s_ones16[:],
                        s_z16[:, par * TC:(par + 1) * TC],
                        start=(g == 0), stop=(g == G - 1)))
                    mark(P + f"ln2mu_{g}", "te")
                    wt(te, P + f"ln2sqc_{g}")
                    if g == 0:
                        wt(te, P + f"bcev_{NST * 2 - 3}")
                    tick("te", te.matmul(
                        pBCb[0:1, :], s_ones16[:],
                        s_z16[:, par * TC:(par + 1) * TC],
                        start=(g == 0), stop=(g == G - 1)))
                    mark(P + f"ln2sq_{g}", "te")

            MU = s_s[0:1, 0:TC]
            VCH = s_t1[0:1, 0:TC]
            MUR = s_t1[0:1, TC:2 * TC]

            @block.vector
            def _(ve):
                def vst(inst):
                    tick("ve", inst)
                    ve.wait_ge(sems["ve"], L.cnt["ve"])

                wt(ve, P + f"ln2sq_{G-1}")
                wt(ve, P + f"ffn_h3_{HT-2}")
                wt(ve, P + f"ffn_h3_{HT-1}")
                vst(ve.tensor_scalar(MU, pBCa[0:1, :], 1.0 / D, None,
                                     ALU.mult))
                vst(ve.tensor_scalar(VCH, pBCb[0:1, :], 1.0 / D, None,
                                     ALU.mult))
                vst(ve.tensor_tensor(MUR, MU, MU, ALU.mult))
                vst(ve.tensor_tensor(VCH, VCH, MUR, ALU.subtract))
                tick("ve", ve.tensor_scalar(VCH, VCH, LN_EPS, None, ALU.add))
                mark(P + "ln2_var", "ve")

            @block.scalar
            def _(sc):
                wt(sc, P + "ln2_var")
                tick("sc", sc.activation(VCH, VCH, AF.Sqrt))
                mark(P + "ln2_sd", "sc")

            @block.vector
            def _(ve):
                def vst(inst):
                    tick("ve", inst)
                    ve.wait_ge(sems["ve"], L.cnt["ve"])

                wt(ve, P + "ln2_sd")
                vst(ve.reciprocal(VCH, VCH))
                vst(ve.tensor_tensor(MUR, MU, VCH, ALU.mult))
                tick("ve", ve.nop())
                mark(P + "ln2_r", "ve")

            @block.tensor
            def _(te):
                wt(te, P + "ln2_r")
                te.matmul(pBCa[:], s_ones1[:], VCH, start=True, stop=True)
                tick("te", te.matmul(pBCb[:], s_ones1[:], MUR,
                                     start=True, stop=True))
                mark(P + "ln2_bc", "te")

            @block.vector
            def _(ve):
                wt(ve, P + "ln2_bc")
                z3 = s_zres.rearrange("p (g t) -> p g t", t=TC)
                ve.tensor_tensor(z3[:, :, :], z3[:, :, :],
                                 pBCa[:, None, :].broadcast_to([128, G, TC]),
                                 ALU.mult)
                ve.tensor_tensor(z3[:, :, :], z3[:, :, :],
                                 pBCb[:, None, :].broadcast_to([128, G, TC]),
                                 ALU.subtract)
                for g in range(G):
                    _last = ve.tensor_scalar(
                        s_zres[:, g * TC:(g + 1) * TC],
                        s_zres[:, g * TC:(g + 1) * TC],
                        s_cols[:, OF_LNG + g:OF_LNG + g + 1],
                        s_cols[:, OF_LNB2 + g:OF_LNB2 + g + 1],
                        ALU.mult, ALU.add)
                tick("ve", _last)
                mark(P + "attn_ready", "ve")

            f2bank = [pMA, pMB, pBCa, pBCb]

            @block.tensor
            def _(te):
                for o in range(G):
                    wt(te, P + f"w2_{o}")
                    if o == 0:
                        wt(te, P + f"ffn_h3_{HT-1}")
                        wt(te, P + f"res_{G-2}")
                    elif o == 1:
                        wt(te, P + f"res_{G-1}")
                    elif o in (2, 3):
                        wt(te, P + "attn_ready")
                    else:
                        wt(te, P + f"ffn_fadd_{o-4}")
                    pk = f2bank[o % 4]
                    for k in range(HT):
                        _last = te.matmul(
                            pk[:],
                            s_w2s[:, (o % 2) * 4096 + k * 128:
                                  (o % 2) * 4096 + (k + 1) * 128],
                            s_hq[:, k * TC:(k + 1) * TC],
                            start=(k == 0), stop=(k == HT - 1))
                    tick("te", _last)
                    mark(P + f"ffn_mm2_{o}", "te")

            @block.vector
            def _(ve):
                for o in range(G):
                    pk = f2bank[o % 4]
                    wt(ve, P + f"ffn_mm2_{o}")
                    if o == 0:
                        wt(ve, P + "attn_ready")
                    tick("ve", ve.scalar_tensor_tensor(
                        s_zres[:, o * TC:(o + 1) * TC], pk[:],
                        1.0 / 1024.0, s_zres[:, o * TC:(o + 1) * TC],
                        ALU.mult, ALU.add))
                    mark(P + f"ffn_fadd_{o}", "ve")

            @block.tensor
            def _(te):
                for q in range(NSUB):
                    for half in range(2):
                        gi = q * 2 + half
                        pk = pTA if gi % 2 == 0 else pTB
                        if gi >= 2:
                            wt(te, P + f"fin_ev_{gi-2}")
                        for gg in range(4):
                            g = half * 4 + gg
                            wt(te, P + f"ffn_fadd_{g}")
                            _last = te.transpose(
                                pk[:, gg * 128:(gg + 1) * 128],
                                s_zres[:, g * TC + q * 128:
                                       g * TC + (q + 1) * 128],
                                s_id32[:])
                        tick("te", _last)
                        mark(P + f"fin_tp_{gi}", "te")

            @block.vector
            def _(ve):
                for q in range(NSUB):
                    for half in range(2):
                        gi = q * 2 + half
                        pk = pTA if gi % 2 == 0 else pTB
                        wt(ve, P + f"fin_tp_{gi}")
                        tick("ve", ve.tensor_tensor(
                            s_xntm[:, q * D + half * 512:
                                   q * D + (half + 1) * 512],
                            pk[:],
                            s_xtm[:, q * D + half * 512:
                                  q * D + (half + 1) * 512],
                            ALU.add))
                        mark(P + f"fin_ev_{gi}", "ve")

            @block.gpsimd
            def _(gp):
                for q in range(NSUB):
                    wt(gp, P + f"fin_ev_{q * 2}")
                    wt(gp, P + f"fin_ev_{q * 2 + 1}")
                    tick("st", gp.dma_start(
                        out=dout[t0 + q * 128:t0 + (q + 1) * 128, :],
                        in_=s_xntm[:, q * D:(q + 1) * D]), 16)
                mark(P + "store", "st")

            @block.vector
            def _(ve):
                x3 = s_xn16.rearrange("p (g t) -> p g t", t=3 + TC)
                for g in range(G):
                    wt(ve, P + f"cv_{g}")
                if c + 1 < NCH and (c + 1) % (NCH // BLOC) != 0:
                    tick("ve", ve.tensor_copy(x3[:, :, 0:3],
                                              x3[:, :, TC:TC + 3]))
                else:
                    tick("ve", ve.memset(x3[:, :, 0:3], 0.0))
                mark(P + "halo", "ve")

            for k in ("gp", "te", "ve", "sc", "st"):
                mark(P + k, k)

        if DEBUG_TAPS:
            @block.gpsimd
            def _(gp):
                for k in ("gp", "te", "ve", "sc"):
                    wt(gp, f"c{NCH-1}_" + k)
                pairs = [("xn16", s_xn16), ("u16d", s_u16), ("btd", s_bt),
                         ("BCd", s_BC), ("xqd", s_xq), ("hqd", s_hq),
                         ("gad", s_ga), ("zresd", s_zres),
                         ("carryd", s_carry), ("acold", s_acol),
                         ("t1d", s_t1), ("statd", s_stat),
                         ("xtmd", s_xtm), ("xntmd", s_xntm)]
                for nm, t in pairs:
                    tick("st", gp.dma_start(out=dbg[nm], in_=t[:]), 16)
                tick("st", gp.dma_start(out=dbg["scrw1d"],
                                        in_=scr_w1[5, :, :]), 16)
                tick("st", gp.dma_start(out=dbg["scrw2d"],
                                        in_=scr_w2[3, :, :]), 16)

        @block.gpsimd
        def _(gp):
            if L.prev is not None:
                gp.wait_ge(sems["st"], L.cnt["st"])

    return nc


def build_program():
    led1 = Ledger(None)
    nc1 = bass.Bass("TRN2", target_bir_lowering=False,
                    detect_race_conditions=False)
    emit(nc1, led1)
    led2 = Ledger(led1.marks)
    nc2 = bass.Bass("TRN2", target_bir_lowering=False,
                    detect_race_conditions=False)
    emit(nc2, led2)
    assert led1.cnt == led2.cnt, (led1.cnt, led2.cnt)
    assert led1.marks == led2.marks
    return nc2


_NC_CACHE = None


def kernel(**inputs):
    global _NC_CACHE
    if _NC_CACHE is None:
        _NC_CACHE = build_program()
    nc = _NC_CACHE

    x = np.ascontiguousarray(inputs["L0_attn_x"], dtype=np.float32)
    base = {
        "conv_w": np.ascontiguousarray(
            inputs["L0_attn_conv_w"].reshape(D, DCONV), np.float32),
        "conv_b": np.ascontiguousarray(inputs["L0_attn_conv_b"], np.float32),
        "A": np.ascontiguousarray(inputs["L0_attn_A"], np.float32),
        "Bp_w": np.ascontiguousarray(inputs["L0_attn_Bp_w"], np.float32),
        "Bp_b": np.ascontiguousarray(inputs["L0_attn_Bp_b"], np.float32),
        "Cp_w": np.ascontiguousarray(inputs["L0_attn_Cp_w"], np.float32),
        "Cp_b": np.ascontiguousarray(inputs["L0_attn_Cp_b"], np.float32),
        "Dv": np.ascontiguousarray(inputs["L0_attn_D"], np.float32),
        "gate_w": np.ascontiguousarray(inputs["L0_attn_gate_w"], np.float32),
        "gate_b": np.ascontiguousarray(inputs["L0_attn_gate_b"], np.float32),
        "out_w": np.ascontiguousarray(inputs["L0_attn_out_w"], np.float32),
        "out_b": np.ascontiguousarray(inputs["L0_attn_out_b"], np.float32),
        "ln_g": np.ascontiguousarray(inputs["L0_attn_ln_g"], np.float32),
        "ln_b": np.ascontiguousarray(inputs["L0_attn_ln_b"], np.float32),
        "w1": np.ascontiguousarray(inputs["L0_ffn_w1"], np.float32),
        "b1": np.ascontiguousarray(inputs["L0_ffn_b1"], np.float32),
        "w2": np.ascontiguousarray(inputs["L0_ffn_w2"], np.float32),
        "b2": np.ascontiguousarray(inputs["L0_ffn_b2"], np.float32),
    }
    in_maps = []
    for ci in range(NCORES):
        m = dict(base)
        m["x"] = np.ascontiguousarray(
            x[ci * BLOC:(ci + 1) * BLOC].reshape(T, D))
        in_maps.append(m)

    res = run_bass_kernel_spmd(nc, in_maps, core_ids=list(range(NCORES)))
    out = np.empty((16, SEQ, D), np.float32)
    for ci in range(NCORES):
        out[ci * BLOC:(ci + 1) * BLOC] = res.results[ci]["out"].reshape(
            BLOC, SEQ, D)
    return out
